# version 05
# baseline (speedup 1.0000x reference)
import math
import contextlib
import numpy as np
import ml_dtypes
import concourse.bass as bass
import concourse.mybir as mybir
from concourse.bass_utils import run_bass_kernel_spmd

F32 = mybir.dt.float32
BF16 = mybir.dt.bfloat16
I32 = mybir.dt.int32
AF = mybir.ActivationFunctionType
ALU = mybir.AluOpType
AX = mybir.AxisListType.X

NCORES = 8
N = 8192
DM = 4096
TLOC = N // NCORES
NT = TLOC // 128
NDT = DM // 128
NCH = 16

BF = ml_dtypes.bfloat16

NF4 = [-1.0, -0.6961928009986877, -0.5250730514526367, -0.39491748809814453,
       -0.28444138169288635, -0.18477343022823334, -0.09105003625154495, 0.0,
       0.07958029955625534, 0.16093020141124725, 0.24611230194568634,
       0.33791524171829224, 0.44070982933044434, 0.5626170039176941,
       0.7229568362236023, 1.0]

H1, HKV, HD = 32, 8, 128
_s0 = 2 ** (-2 ** (-(math.log2(H1) - 3)))
SLOPES = [_s0 * _s0 ** i for i in range(H1)]
WIN = [max(640, int(140.0 / s) + 1) for s in SLOPES]
WC = [int(math.ceil(w / 512.0)) + 1 for w in WIN]
B_SHIFT = 40.0

SKV_A = [min(max(WC[4 * kv + g] for g in range(4)), 8) for kv in range(HKV)]
SKV_B = [min(max(WC[4 * kv + g] for g in range(4)), 16) for kv in range(HKV)]
SMAX = 16
SH_TAB = [[min(WC[h], SKV_A[h // 4]), min(WC[h], SKV_B[h // 4])] for h in range(H1)]

H0, E0 = 8, 64
NQKV = 12
NKT = N // 128

NQKV1 = 48

HF = 11008
NHT = HF // 128
HTILES = [11] * 7 + [9]
HTOFF = [sum(HTILES[:r]) for r in range(NCORES + 1)]
HSH_PAD = 1408


def cr_of(g):
    return 2 * min(g, 15 - g) + (1 if g >= 8 else 0)


def rank_lt(ht):
    for r in range(NCORES):
        if ht < HTOFF[r + 1]:
            return r, ht - HTOFF[r]
    raise ValueError



class _Ct:
    __slots__ = ("v",)

    def __init__(self):
        self.v = 0


class Sch:
    ENGINES = ("sync", "gpsimd", "tensor", "vector", "scalar")

    def __init__(self, nc, ctx):
        self.nc = nc
        self.ctx = ctx
        self.q = {e: [] for e in self.ENGINES}
        self.sems = {}

    def sem(self, name):
        if name not in self.sems:
            h = self.ctx.enter_context(self.nc.semaphore(name))
            self.sems[name] = (h, _Ct())
        return self.sems[name]

    def val(self, name):
        return self.sem(name)[1].v

    def wait(self, eng, name, v=None):
        h, ct = self.sem(name)
        if v is None:
            v = ct.v
        if v <= 0:
            return
        self.q[eng].append(lambda e, h=h, v=v: e.wait_ge(h, v))

    def op(self, eng, build, inc=None, n=1):
        if inc is not None:
            h, ct = self.sem(inc)
            ct.v += n

            def f(e, build=build, h=h, n=n):
                build(e).then_inc(h, n)
            self.q[eng].append(f)
            return ct.v
        self.q[eng].append(build)
        return None

    def dma(self, eng, out, in_, inc=None):
        return self.op(eng, lambda e, o=out, i=in_: e.dma_start(out=o, in_=i),
                       inc=inc, n=16)

    def collective(self, in_ap, out_ap, name):
        h, ct = self.sem(name)
        ct.v = 1

        def f(e, h=h, in_ap=in_ap, out_ap=out_ap):
            e.collective_compute(
                "AllGather", mybir.AluOpType.bypass,
                replica_groups=[list(range(NCORES))],
                ins=[in_ap.opt()], outs=[out_ap.opt()]).then_inc(h)
        self.q['gpsimd'].append(f)

    def cwait(self, eng, name):
        h, _ = self.sem(name)
        self.q[eng].append(lambda e, h=h: e.wait_ge(h, 1))

    def indirect(self, out, in_, idx_ap, inc=None):
        return self.op(
            'gpsimd',
            lambda e, o=out, i=in_, x=idx_ap: e.indirect_dma_start(
                o, None, i, bass.IndirectOffsetOnAxis(ap=x, axis=0)),
            inc=inc, n=16)

    def mm_group(self, pairs, psum_ap, inc):
        nmm = len(pairs)
        v = None
        for i, (lh, rh) in enumerate(pairs):
            v = self.op('tensor',
                        lambda e, lh=lh, rh=rh, s=(i == 0), st=(i == nmm - 1):
                        e.matmul(psum_ap, lh, rh, start=s, stop=st,
                                 skip_group_check=True),
                        inc=(inc if i == nmm - 1 else None))
        return v

    def mm(self, psum_ap, lh, rh, start, stop, inc=None, always=False):
        return self.op('tensor',
                       lambda e, lh=lh, rh=rh, s=start, st=stop:
                       e.matmul(psum_ap, lh, rh, start=s, stop=st,
                                skip_group_check=True),
                       inc=(inc if (stop or always) else None))

    def barrier(self):
        snap = [(h, ct.v) for (n, (h, ct)) in self.sems.items()
                if ct.v > 0 and not n.startswith('cc')]
        for eng in self.ENGINES:
            for h, v in snap:
                self.q[eng].append(lambda e, h=h, v=v: e.wait_ge(h, v))

    def run(self, block):
        for eng in self.ENGINES:
            ops = self.q[eng]

            def body(handle, ops=ops):
                for fn in ops:
                    fn(handle)
            getattr(block, eng)(body)



def tile_w(w_t):
    K, O = w_t.shape
    nd, no = K // 128, O // 128
    return np.ascontiguousarray(
        w_t.reshape(nd, 128, no, 128).transpose(2, 1, 0, 3).reshape(no, 128, nd * 128))


def cols128(b):
    no = b.shape[0] // 128
    return np.ascontiguousarray(b.reshape(no, 128).T.astype(np.float32))


def _rel_consts():
    i = np.arange(128)[:, None]
    j = np.arange(512)[None, :]
    rel = (j - i).astype(np.float32)
    out = [rel]
    for m in range(4):
        r = rel.copy()
        r[(-m * 128 + j - i) > 0] = -1e33
        out.append(r)
    return np.ascontiguousarray(np.stack(out))


def l1b_slot_gc(core):
    out = np.full((HKV, 2, SMAX), -1, dtype=np.int64)
    for kv in range(HKV):
        for gi, ch in enumerate((core, 15 - core)):
            S_g = (SKV_A if gi == 0 else SKV_B)[kv]
            for s in range(S_g):
                gc = ch - (S_g - 1) + s
                out[kv, gi, s] = gc if gc >= 0 else -1
    return out


def l1b_bias_table(core):
    tab = np.full((H1 * 2 * SMAX * 4,), -1e30, dtype=np.float32)
    gc = l1b_slot_gc(core)
    for h in range(H1):
        kv = h // 4
        for gi, ch in enumerate((core, 15 - core)):
            S_g = (SKV_A if gi == 0 else SKV_B)[kv]
            S_h = SH_TAB[h][gi]
            for s in range(S_g - S_h, S_g):
                g = gc[kv, gi, s]
                if g < 0:
                    continue
                for j in range(4):
                    q0 = ch * 512 + j * 128
                    k0 = g * 512
                    tab[bias_col(h, gi, s, j)] = SLOPES[h] * (k0 - q0) - B_SHIFT
    return np.ascontiguousarray(
        np.broadcast_to(tab[None, :], (128, tab.shape[0])).astype(np.float32))


def bias_col(h, gi, s, j):
    return ((h * 2 + gi) * SMAX + s) * 4 + j


def l1b_kidx(core):
    gc = l1b_slot_gc(core)
    out = np.zeros((128, HKV * 2 * SMAX), dtype=np.int32)
    p = np.arange(128, dtype=np.int32)
    for kv in range(HKV):
        for gi in range(2):
            for s in range(SMAX):
                g = gc[kv, gi, s]
                base = 0 if g < 0 else cr_of(int(g)) * 1024 + kv * 128
                out[:, (kv * 2 + gi) * SMAX + s] = base + p
    return np.ascontiguousarray(out)



class Dequant:

    def __init__(self, sch, S, cod_u, amx_u, cod_d, amx_d, wupc, wdnc):
        self.sch = sch
        self.cod_i = S("dq_cod", [128, 2048], I32)
        self.codb = S("dq_codb", [128, 2048], BF16)
        self.amx = S("dq_amx", [128, 2048], F32)
        self.dacc = [S(f"dq_dacc{i}", [128, 2048], BF16) for i in range(2)]
        self.dtmp = S("dq_dtmp", [128, 2048], BF16)
        self.units = []
        for dt in range(NDT):
            self.units.append((cod_u[dt * 128:(dt + 1) * 128, :],
                               amx_u[2 * dt:2 * dt + 1, :],
                               amx_u[2 * dt + 1:2 * dt + 2, :],
                               wupc[dt * 128:(dt + 1) * 128, :], HSH_PAD))
        for ht in range(11):
            for hf in range(2):
                c0 = hf * 2048
                self.units.append((cod_d[ht * 128:(ht + 1) * 128, c0:c0 + 2048],
                                   amx_d[2 * ht:2 * ht + 1, c0:c0 + 2048],
                                   amx_d[2 * ht + 1:2 * ht + 2, c0:c0 + 2048],
                                   wdnc[ht * 128:(ht + 1) * 128, c0:c0 + 2048],
                                   2048))
        self.i = 0
        self.cv_prev = 0
        self.st = {0: 0, 1: 0}
        self.n_emitted_st = 0

    def emit(self, n):
        sch = self.sch
        for _ in range(n):
            if self.i >= len(self.units):
                return
            i = self.i
            self.i += 1
            src, am0, am1, dst, w = self.units[i]
            b = i % 2
            if self.cv_prev:
                sch.wait('gpsimd', 've', self.cv_prev)
            ld = sch.dma('gpsimd', self.cod_i[:, 0:w], src, inc='dqld')
            ld = sch.dma('gpsimd', self.amx[0:64, 0:w],
                         am0.broadcast_to([64, w]), inc='dqld')
            ld = sch.dma('gpsimd', self.amx[64:128, 0:w],
                         am1.broadcast_to([64, w]), inc='dqld')
            sch.wait('vector', 'dqld', ld)
            v = sch.op('vector', lambda e, w=w: e.tensor_copy(
                self.codb[:, 0:w], self.cod_i[:, 0:w]), inc='ve')
            if self.st[b]:
                sch.wait('vector', 'dqst%d' % b, self.st[b])
            sch.wait('vector', 've', v)
            v = sch.op('vector', lambda e, b=b, w=w: e.tensor_scalar(
                self.dacc[b][:, 0:w], self.codb[:, 0:w], 0.0, NF4[0],
                ALU.is_equal, ALU.mult), inc='ve')
            for vi in range(1, 16):
                v2 = sch.op('vector', lambda e, vi=vi, w=w: e.tensor_scalar(
                    self.dtmp[:, 0:w], self.codb[:, 0:w], float(vi), NF4[vi],
                    ALU.is_equal, ALU.mult), inc='ve')
                sch.wait('vector', 've', v2)
                v = sch.op('vector', lambda e, b=b, w=w: e.tensor_add(
                    self.dacc[b][:, 0:w], self.dacc[b][:, 0:w],
                    self.dtmp[:, 0:w]), inc='ve')
                sch.wait('vector', 've', v)
            v = sch.op('vector', lambda e, b=b, w=w: e.tensor_mul(
                self.dacc[b][:, 0:w], self.dacc[b][:, 0:w],
                self.amx[:, 0:w]), inc='ve')
            self.cv_prev = v
            sch.wait('sync', 've', v)
            self.st[b] = sch.dma('sync', dst, self.dacc[b][:, 0:w],
                                 inc='dqst%d' % b)



def emit_ln_tile(sch, x_sb, xn_sb, scr, stt, eps=1e-5):
    v = sch.op('vector', lambda e: e.reduce_sum(stt[:, 0:1], x_sb[:], axis=AX), inc='ve')
    a = sch.op('scalar', lambda e: e.activation(scr[:], x_sb[:], AF.Square,
                                                accum_out=stt[:, 1:2]), inc='ac')
    sch.wait('vector', 'ac', a)
    sch.wait('vector', 've', v)
    v = sch.op('vector', lambda e: e.tensor_scalar_mul(stt[:, 2:3], stt[:, 0:1], 1.0 / DM), inc='ve')
    sch.wait('vector', 've', v)
    v = sch.op('vector', lambda e: e.tensor_mul(stt[:, 4:5], stt[:, 2:3], stt[:, 2:3]), inc='ve')
    v = sch.op('vector', lambda e: e.tensor_scalar(stt[:, 3:4], stt[:, 1:2], 1.0 / DM,
                                                   None, ALU.mult), inc='ve')
    sch.wait('vector', 've', v)
    v = sch.op('vector', lambda e: e.tensor_sub(stt[:, 5:6], stt[:, 3:4], stt[:, 4:5]), inc='ve')
    sch.wait('vector', 've', v)
    v = sch.op('vector', lambda e: e.tensor_scalar_add(stt[:, 5:6], stt[:, 5:6], 1e-5), inc='ve')
    sch.wait('vector', 've', v)
    v = sch.op('vector', lambda e: e.reciprocal(stt[:, 6:7], stt[:, 5:6]), inc='ve')
    sch.wait('scalar', 've', v)
    a = sch.op('scalar', lambda e: e.activation(stt[:, 7:8], stt[:, 6:7], AF.Sqrt), inc='ac')
    sch.wait('vector', 'ac', a)
    v = sch.op('vector', lambda e: e.tensor_scalar(xn_sb[:], x_sb[:], stt[:, 2:3],
                                                   stt[:, 7:8], ALU.subtract, ALU.mult), inc='ve')
    return v



def build_fused(debug=False):
    nc = bass.Bass("TRN2", target_bir_lowering=False)
    T = lambda n, s, d, k: nc.dram_tensor(n, s, d, kind=k).ap()
    EI, EO = "ExternalInput", "ExternalOutput"
    x_tm = T("x_tm", [TLOC, DM], F32, EI)
    xT = T("xT", [DM, TLOC], F32, EI)
    wqkv0 = T("wqkv0", [NQKV, 128, DM], F32, EI)
    bqkv = T("bqkv", [128, NQKV], F32, EI)
    wrout = T("wrout", [128, NDT * 8], F32, EI)
    wa = T("wa", [1, 128, DM], F32, EI)
    wbase = T("wbase", [NDT, 128, DM], F32, EI)
    bbase = T("bbase", [128, NDT], F32, EI)
    bstk = T("bstk", [NDT, 128, 128], F32, EI)
    sel8 = T("sel8", [8, 128], F32, EI)
    idf = T("idf", [128, 128], F32, EI)
    idb = T("idb", [128, 128], BF16, EI)
    ones1 = T("ones1", [1, 128], F32, EI)
    ones128 = T("ones128", [128, 1], F32, EI)
    ones128b = T("ones128b", [128, 1], BF16, EI)
    wo0 = T("wo0", [NDT, 128, 512], F32, EI)
    bo0 = T("bo0", [128, NDT], F32, EI)
    wqkv1 = T("wqkv1", [NQKV1, 128, DM], F32, EI)
    wnq = T("wnq", [128, 1], F32, EI)
    wnk = T("wnk", [128, 1], F32, EI)
    la_up = T("la_up", [128, NDT * 16], F32, EI)
    rel5 = T("rel5", [5, 128, 512], F32, EI)
    biast = T("biast", [128, H1 * 2 * SMAX * 4], F32, EI)
    kidx = T("kidx", [128, HKV * 2 * SMAX], I32, EI)
    cod_u = T("cod_u", [DM, HSH_PAD], I32, EI)
    amx_u = T("amx_u", [64, HSH_PAD], F32, EI)
    cod_d = T("cod_d", [HSH_PAD, DM], I32, EI)
    amx_d = T("amx_d", [22, DM], F32, EI)
    wo1 = T("wo1", [NDT, 128, DM], F32, EI)
    bup = T("bup", [128, NHT], F32, EI)
    bdn = T("bdn", [128, NDT], F32, EI)
    lbup = T("lbup", [16, HF], F32, EI)
    ladn = T("ladn", [128, NHT * 16], F32, EI)
    lbdn = T("lbdn", [16, DM], F32, EI)
    IN = "Internal"
    DK = EO if debug else IN
    k0c = T("k0c", [1024, 512], BF16, IN)
    v0c = T("v0c", [1024, 512], BF16, IN)
    KG0 = T("KG0", [8192, 512], BF16, IN)
    VG0 = T("VG0", [8192, 512], BF16, IN)
    g0_scr = T("g0_scr", [DM, TLOC], F32, DK)
    x1_scr = T("x1_scr", [DM, TLOC], F32, DK)
    xn1_scr = T("xn1_scr", [DM, TLOC], BF16, DK)
    q1_scr = T("q1_scr", [DM, TLOC], BF16, DK)
    k1c = T("k1c", [2048, 512], BF16, IN)
    v1c = T("v1c", [2048, 512], BF16, IN)
    KG1 = T("KG1", [16384, 512], BF16, IN)
    VG1 = T("VG1", [16384, 512], BF16, IN)
    wupc = T("wupc", [DM, HSH_PAD], BF16, IN)
    wdnc = T("wdnc", [HSH_PAD, DM], BF16, IN)
    if debug:
        k1d = T("k1d", [2048, 512], BF16, EO)
        v1d = T("v1d", [2048, 512], BF16, EO)
        wud = T("wud", [DM, HSH_PAD], BF16, EO)
        wdd = T("wdd", [HSH_PAD, DM], BF16, EO)
        dbg0 = T("dbg0", [128, 8192], F32, EO)
        x2d = T("x2d", [NHT * 128, TLOC], BF16, EO)
        wugd = T("wugd", [NCORES * DM, HSH_PAD], BF16, EO)
        wdgd = T("wdgd", [NCORES * HSH_PAD, DM], BF16, EO)
        u1d = T("u1d", [16, TLOC], BF16, EO)
        u2d = T("u2d", [16, TLOC], BF16, EO)
    WUPG = T("WUPG", [NCORES * DM, HSH_PAD], BF16, IN)
    WDNG = T("WDNG", [NCORES * HSH_PAD, DM], BF16, IN)
    o_scr = T("o_scr", [DM, TLOC], BF16, DK)
    g1_scr = T("g1_scr", [DM, TLOC], BF16, DK)
    outF = T("outF", [DM, TLOC], F32, EO)

    with contextlib.ExitStack() as ctx:
        sch = Sch(nc, ctx)
        S0 = lambda name, shape, dt=F32: ctx.enter_context(nc.sbuf_tensor(name, shape, dt))
        idb_s = S0("idb_s", [128, 128], BF16)
        ones1_s = S0("ones1_s", [1, 128], F32)
        ones128_s = S0("ones128_s", [128, 1], F32)
        ones128b_s = S0("ones128b_s", [128, 1], BF16)
        q0_sb = S0("q0_sb", [128, 4, TLOC], BF16)
        u1s = S0("u1s", [16, TLOC], BF16)
        s01 = contextlib.ExitStack()
        xn_res = s01.enter_context(nc.sbuf_tensor("xn0_res", [128, NDT, TLOC], BF16))

        ld0 = sch.dma('gpsimd', idb_s[:], idb[:], inc='ld')
        ld0 = sch.dma('gpsimd', ones1_s[:], ones1[:], inc='ld')
        ld0 = sch.dma('gpsimd', ones128_s[:], ones128[:], inc='ld')
        ld0 = sch.dma('gpsimd', ones128b_s[:], ones128b[:], inc='ld')
        LD_CONST = ld0

        block = ctx.enter_context(nc.Block())

        with contextlib.ExitStack() as pctx:
            S = lambda name, shape, dt=F32: pctx.enter_context(nc.sbuf_tensor(name, shape, dt))
            xsb = [S(f"ln_x{i}", [128, DM], F32) for i in range(2)]
            xnb = [S(f"ln_xn{i}", [128, DM], BF16) for i in range(2)]
            scr = S("ln_scr", [128, DM], F32)
            stats = S("ln_st", [128, 8 * 2], F32)
            pstb = [pctx.enter_context(nc.psum_tensor(f"ln_pstb{i}", [128, 1024], BF16))
                    for i in range(2)]
            sch.wait('tensor', 'ld', LD_CONST)
            fin = [None] * NT
            for t in range(NT):
                b = t % 2
                if t >= 2:
                    vv, av = fin[t - 2]
                    sch.wait('gpsimd', 've', vv)
                    sch.wait('gpsimd', 'ac', av)
                ld = sch.dma('gpsimd', xsb[b][:], x_tm[t * 128:(t + 1) * 128, :], inc='lnl%d' % b)
                sch.wait('vector', 'lnl%d' % b, ld)
                sch.wait('scalar', 'lnl%d' % b, ld)
                lnv = emit_ln_tile(sch, xsb[b], xnb[b], scr, stats[:, 8 * b:8 * (b + 1)])
                sch.wait('tensor', 've', lnv)
                for d in range(NDT):
                    par = (d // 8) % 2
                    if d >= 16:
                        sch.wait('tensor', 've', cp8[d - 16])
                    tv = sch.op('tensor', lambda e, d=d, pb=pstb[par], b=b: e.transpose(
                        pb[:, (d % 8) * 128:(d % 8) * 128 + 128],
                        xnb[b][:, d * 128:(d + 1) * 128], idb_s[:]), inc='pe')
                    if d % 8 == 7:
                        sch.wait('vector', 'pe', tv)
                        cv = sch.op('vector', lambda e, d=d, pb=pstb[par], t=t: e.tensor_copy(
                            xn_res[:, d - 7:d + 1, t * 128:(t + 1) * 128],
                            pb[:, 0:1024].rearrange("p (a b) -> p a b", a=8)),
                            inc='ve')
                        if d == 7:
                            cp8 = [0] * NDT
                        for z in range(d - 7, d + 1):
                            cp8[z] = cv
                fin[t] = (cp8[NDT - 1], sch.val('ac'))
            LN0_VE = sch.val('ve')
        sch.barrier()

        with contextlib.ExitStack() as pctx:
            S = lambda name, shape, dt=F32: pctx.enter_context(nc.sbuf_tensor(name, shape, dt))
            wsl = [S(f"a_wsl{i}", [128, DM], BF16) for i in range(2)]
            wrs = S("a_wrs", [128, NDT * 8], BF16)
            idf_s = S("a_idf", [128, 128], F32)
            sel_s = S("a_sel", [8, 128], BF16)
            bq_s = S("a_bq", [128, NQKV], F32)
            bb_s = S("a_bb", [128, NDT], F32)
            bst_s = S("a_bst", [128, NDT, 128], BF16)
            lg = S("a_lg", [8, TLOC], F32)
            lgt = S("a_lgt", [128, NT, 8], F32)
            cstat = S("a_cst", [128, NT, 8], F32)
            cmsk = S("a_cmsk", [128, 8], F32)
            cl2 = S("a_cl2", [128, 8], F32)
            cel = S("a_cel", [128, 8], F32)
            ctm = S("a_ctm", [128, NT, 8], BF16)
            cfm = S("a_cfm", [8, TLOC], BF16)
            crep = S("a_crep", [128, TLOC], F32)
            hw = S("a_hw", [128, TLOC], BF16)
            ev = [S(f"a_ev{i}", [128, TLOC], BF16) for i in range(2)]
            gv = [S(f"a_gv{i}", [128, TLOC], F32) for i in range(2)]
            ps = [pctx.enter_context(nc.psum_tensor(f"a_ps{i}", [128, 512], F32))
                  for i in range(2)]
            psr = pctx.enter_context(nc.psum_tensor("a_psr", [8, 512], F32))
            pstf = pctx.enter_context(nc.psum_tensor("a_pstf", [128, 512], F32))
            pstb2 = pctx.enter_context(nc.psum_tensor("a_pstb", [128, 1024], BF16))

            ld = sch.dma('gpsimd', wrs[:], wrout[:], inc='ld')
            ld = sch.dma('gpsimd', idf_s[:], idf[:], inc='ld')
            ld = sch.dma('gpsimd', sel_s[:], sel8[:], inc='ld')
            ld = sch.dma('gpsimd', bq_s[:], bqkv[:], inc='ld')
            ld = sch.dma('gpsimd', bb_s[:], bbase[:], inc='ld')
            ld = sch.dma('gpsimd', bst_s[:], bstk.transpose([1, 0, 2]), inc='ld')
            LD0 = ld

            sch.wait('tensor', 'ld', LD0)
            sch.wait('tensor', 've', LN0_VE)
            for c in range(2):
                if c == 1:
                    sch.wait('tensor', 've', vv)
                pairs = [(wrs[:, d * 8:(d + 1) * 8], xn_res[:, d, c * 512:(c + 1) * 512])
                         for d in range(NDT)]
                pv = sch.mm_group(pairs, psr[:], inc='pe')
                sch.wait('vector', 'pe', pv)
                vv = sch.op('vector', lambda e, c=c, lg=lg: e.tensor_copy(
                    lg[:, c * 512:(c + 1) * 512], psr[:]), inc='ve')
            sch.wait('tensor', 've', vv)
            for t in range(NT):
                tv = sch.op('tensor', lambda e, t=t, lg=lg: e.transpose(
                    pstf[:, 0:8], lg[0:8, t * 128:(t + 1) * 128], idf_s[0:8, 0:8]), inc='pe')
                sch.wait('vector', 'pe', tv)
                vv = sch.op('vector', lambda e, t=t: e.tensor_copy(lgt[:, t, :], pstf[:, 0:8]), inc='ve')
                sch.wait('tensor', 've', vv)
            for t in range(NT):
                st = cstat[:, t, :]
                l_ = lgt[:, t, :]
                v = sch.op('vector', lambda e, l_=l_, st=st: e.reduce_max(st[:, 0:1], l_, axis=AX), inc='ve')
                sch.wait('vector', 've', v)
                v = sch.op('vector', lambda e, l_=l_, st=st: e.tensor_scalar(
                    cmsk[:], l_, st[:, 0:1], -1e30, ALU.is_equal, ALU.mult), inc='ve')
                sch.wait('vector', 've', v)
                v = sch.op('vector', lambda e, l_=l_: e.tensor_add(cl2[:], l_, cmsk[:]), inc='ve')
                sch.wait('vector', 've', v)
                v = sch.op('vector', lambda e, st=st: e.reduce_max(st[:, 1:2], cl2[:], axis=AX), inc='ve')
                sch.wait('vector', 've', v)
                v = sch.op('vector', lambda e, st=st: e.tensor_scalar_mul(st[:, 2:3], st[:, 0:1], -1.0), inc='ve')
                sch.wait('scalar', 've', v)
                a = sch.op('scalar', lambda e, l_=l_, st=st: e.activation(
                    cel[:], l_, AF.Exp, bias=st[:, 2:3]), inc='ac')
                sch.wait('vector', 'ac', a)
                v = sch.op('vector', lambda e, st=st: e.tensor_sub(st[:, 3:4], st[:, 1:2], st[:, 0:1]), inc='ve')
                sch.wait('scalar', 've', v)
                a = sch.op('scalar', lambda e, st=st: e.activation(st[:, 4:5], st[:, 3:4], AF.Exp), inc='ac')
                sch.wait('vector', 'ac', a)
                v = sch.op('vector', lambda e, st=st: e.tensor_scalar_add(st[:, 5:6], st[:, 4:5], 1.0), inc='ve')
                sch.wait('vector', 've', v)
                v = sch.op('vector', lambda e, st=st: e.reciprocal(st[:, 6:7], st[:, 5:6]), inc='ve')
                sch.wait('vector', 've', v)
                v = sch.op('vector', lambda e, st=st: e.tensor_scalar_mul(st[:, 7:8], st[:, 6:7], 2.0), inc='ve')
                v = sch.op('vector', lambda e, l_=l_, st=st: e.tensor_scalar(
                    cmsk[:], l_, st[:, 1:2], None, ALU.is_ge), inc='ve')
                sch.wait('vector', 've', v)
                v = sch.op('vector', lambda e: e.tensor_mul(cl2[:], cel[:], cmsk[:]), inc='ve')
                sch.wait('vector', 've', v)
                v = sch.op('vector', lambda e, t=t, st=st: e.tensor_scalar(
                    ctm[:, t, :], cl2[:], st[:, 7:8], None, ALU.mult), inc='ve')
                sch.wait('tensor', 've', v)
                tv = sch.op('tensor', lambda e, t=t: e.transpose(
                    pstb2[0:8, 0:128], ctm[:, t, :], idb_s[:]), inc='pe')
                sch.wait('vector', 'pe', tv)
                v = sch.op('vector', lambda e, t=t: e.tensor_copy(
                    cfm[0:8, t * 128:(t + 1) * 128], pstb2[0:8, 0:128]), inc='ve')
                sch.wait('tensor', 've', v)
            for c in range(2):
                pv = sch.mm_group([(sel_s[:], cfm[0:8, c * 512:(c + 1) * 512])], ps[c][:], inc='pe')
                sch.wait('vector', 'pe', pv)
                vv = sch.op('vector', lambda e, c=c: e.tensor_copy(
                    crep[:, c * 512:(c + 1) * 512], ps[c][:]), inc='ve')
            nld = {0: 0, 1: 0}
            nld[0] += 1
            wa_v = sch.dma('gpsimd', wsl[0][:], wa[0, :, :], inc='alw0')
            sch.wait('tensor', 'alw0', wa_v)
            sch.wait('tensor', 've', vv)
            for c in range(2):
                pairs = [(wsl[0][:, d * 128:(d + 1) * 128], xn_res[:, d, c * 512:(c + 1) * 512])
                         for d in range(NDT)]
                pv = sch.mm_group(pairs, ps[c][:], inc='pe')
                sch.wait('vector', 'pe', pv)
                vv = sch.op('vector', lambda e, c=c: e.tensor_mul(
                    hw[:, c * 512:(c + 1) * 512], ps[c][:], crep[:, c * 512:(c + 1) * 512]), inc='ve')
            HW_DONE = vv

            stk = {0: 0, 1: 0}
            for o in range(NQKV):
                b = o % 2
                if o >= 2:
                    sch.wait('gpsimd', 'aev', 2 * (o - 1))
                else:
                    sch.wait('gpsimd', 'pe', sch.val('pe'))
                nld[b] += 1
                sch.dma('gpsimd', wsl[b][:], wqkv0[o, :, :], inc='alw%d' % b)
                sch.wait('tensor', 'alw%d' % b, 16 * nld[b])
                if o == 0:
                    sch.wait('tensor', 've', HW_DONE)
                for c in range(2):
                    u = 2 * o + c
                    pairs = [(wsl[b][:, d * 128:(d + 1) * 128],
                              xn_res[:, d, c * 512:(c + 1) * 512]) for d in range(NDT)]
                    if u >= 2:
                        sch.wait('tensor', 'aev', u - 1)
                    pv = sch.mm_group(pairs, ps[c][:], inc='pe')
                    sch.wait('vector', 'pe', pv)
                    if o < 4:
                        v = sch.op('vector', lambda e, o=o, c=c: e.tensor_scalar(
                            q0_sb[:, o, c * 512:(c + 1) * 512], ps[c][:],
                            bq_s[:, o:o + 1], None, ALU.add), inc='aev')
                    else:
                        b2 = o % 2
                        if c == 0:
                            sch.wait('vector', 'astk%d' % b2, stk[b2])
                        v = sch.op('vector', lambda e, o=o, c=c, b2=b2: e.tensor_scalar(
                            ev[b2][:, c * 512:(c + 1) * 512], ps[c][:],
                            bq_s[:, o:o + 1], None, ALU.add), inc='aev')
                        if c == 1:
                            dstT = k0c if o < 8 else v0c
                            fo = (o - 4) % 4
                            sch.wait('sync', 'aev', v)
                            sv = sch.dma('sync', dstT[fo * 128:fo * 128 + 128, :],
                                         ev[b2][:, 0:512], inc='astk%d' % b2)
                            sv = sch.dma('sync', dstT[512 + fo * 128:512 + fo * 128 + 128, :],
                                         ev[b2][:, 512:1024], inc='astk%d' % b2)
                            stk[b2] = sv
                if o == 7:
                    sch.wait('gpsimd', 'astk0', sch.val('astk0'))
                    sch.wait('gpsimd', 'astk1', sch.val('astk1'))
                    sch.collective(k0c, KG0, 'ccK0')
                if o == 11:
                    sch.wait('gpsimd', 'astk0', sch.val('astk0'))
                    sch.wait('gpsimd', 'astk1', sch.val('astk1'))
                    sch.collective(v0c, VG0, 'ccV0')

            stg = {0: 0, 1: 0}
            for o in range(NDT):
                b = o % 2
                sch.wait('gpsimd', 'aev', 24 + 2 * max(o - 1, 0))
                nld[b] += 1
                sch.dma('gpsimd', wsl[b][:], wbase[o, :, :], inc='alw%d' % b)
                sch.wait('tensor', 'alw%d' % b, 16 * nld[b])
                for c in range(2):
                    u = 24 + 2 * o + c
                    pairs = [(wsl[b][:, d * 128:(d + 1) * 128],
                              xn_res[:, d, c * 512:(c + 1) * 512]) for d in range(NDT)]
                    pairs.append((bst_s[:, o, :], hw[:, c * 512:(c + 1) * 512]))
                    sch.wait('tensor', 'aev', u - 1)
                    pv = sch.mm_group(pairs, ps[c][:], inc='pe')
                    sch.wait('scalar', 'pe', pv)
                    b2 = o % 2
                    if c == 0:
                        sch.wait('scalar', 'astg%d' % b2, stg[b2])
                    v = sch.op('scalar', lambda e, o=o, c=c, b2=b2: e.activation(
                        gv[b2][:, c * 512:(c + 1) * 512], ps[c][:], AF.Silu,
                        bias=bb_s[:, o:o + 1]), inc='aev')
                    if c == 1:
                        sch.wait('sync', 'aev', v)
                        stg[b2] = sch.dma('sync', g0_scr[o * 128:(o + 1) * 128, :],
                                          gv[b2][:], inc='astg%d' % b2)
            sch.wait('sync', 'astg0', sch.val('astg0'))
            sch.wait('sync', 'astg1', sch.val('astg1'))
            if debug:
                sch.wait('gpsimd', 'aev', sch.val('aev'))
                sch.wait('gpsimd', 've', sch.val('ve'))
                dv = sch.dma('gpsimd', dbg0[0:128, 0:1024], crep[:], inc='dbg0')
                dv = sch.dma('gpsimd', dbg0[0:8, 1024:2048], lg[:], inc='dbg0')
                dv = sch.dma('gpsimd', dbg0[0:8, 2048:3072], cfm[:], inc='dbg0')
                dv = sch.dma('gpsimd',
                             dbg0[0:128, 3072:7168].rearrange("p (d t) -> p d t", d=32),
                             xn_res[:, :, 896:1024], inc='dbg0')
                dv = sch.dma('gpsimd', dbg0[0:128, 7168:8192], hw[:], inc='dbg0')
                sch.wait('gpsimd', 'dbg0', dv)
            L0A_PE = sch.val('pe')

        s01.close()
        sch.barrier()

        o_sb0 = ctx.enter_context(nc.sbuf_tensor("o_sb0", [128, 4, TLOC], BF16))
        with contextlib.ExitStack() as pctx:
            S = lambda name, shape, dt: pctx.enter_context(nc.sbuf_tensor(name, shape, dt))
            kp = S("b_kp", [128, N], BF16)
            vf = S("b_vf", [128, N], BF16)
            vp = S("b_vp", [128, NKT, 128], BF16)
            att = S("b_att", [128, N], BF16)
            attn = S("b_attn", [128, N], BF16)
            panels = S("b_pan", [128, NKT, 256], BF16)
            mcols = S("b_mc", [128, NCH + 8], F32)
            zcols = S("b_zc", [128, NCH + 8], F32)
            dq = Dequant(sch, lambda n, s, d: pctx.enter_context(nc.sbuf_tensor(n, s, d)),
                         cod_u, amx_u, cod_d, amx_d, wupc, wdnc)
            pss = [pctx.enter_context(nc.psum_tensor(f"b_pss{i}", [128, 512], F32))
                   for i in range(2)]
            pstb = [pctx.enter_context(nc.psum_tensor(f"b_pstb{i}", [128, 1024], BF16))
                    for i in range(2)]
            pso = pctx.enter_context(nc.psum_tensor("b_pso", [64, 256], F32))

            kv_users_ac = 0
            prev_exp = 0
            sch.cwait('gpsimd', 'ccK0')
            sch.cwait('gpsimd', 'ccV0')
            for p in range(4):
                if p > 0:
                    sch.wait('gpsimd', 'ac', kv_users_ac)
                    sch.wait('gpsimd', 'pe', sch.val('pe'))
                for cr in range(NCH):
                    lk = sch.dma('gpsimd', kp[:, cr * 512:(cr + 1) * 512],
                                 KG0[cr * 512 + p * 128:cr * 512 + (p + 1) * 128, :],
                                 inc='bldk')
                for cr in range(NCH):
                    lk = sch.dma('gpsimd', vf[:, cr * 512:(cr + 1) * 512],
                                 VG0[cr * 512 + p * 128:cr * 512 + (p + 1) * 128, :],
                                 inc='bldk')
                sch.wait('tensor', 'bldk', lk)
                for kt in range(NKT):
                    par = (kt // 8) % 2
                    if kt >= 16:
                        sch.wait('tensor', 've', vcp8[kt - 16])
                    elif p > 0 and kt < 16:
                        sch.wait('tensor', 've', prev_pan_last)
                    tv = sch.op('tensor', lambda e, kt=kt, pb=pstb[par]: e.transpose(
                        pb[:, (kt % 8) * 128:(kt % 8) * 128 + 128],
                        vf[:, kt * 128:(kt + 1) * 128], idb_s[:]), inc='pe')
                    if kt % 8 == 7:
                        sch.wait('vector', 'pe', tv)
                        cv = sch.op('vector', lambda e, kt=kt, pb=pstb[par]: e.tensor_copy(
                            vp[:, kt - 7:kt + 1, :],
                            pb[:, 0:1024].rearrange("p (a b) -> p a b", a=8)),
                            inc='ve')
                        if kt == 7:
                            vcp8 = [0] * NKT
                        for z in range(kt - 7, kt + 1):
                            vcp8[z] = cv
                VP_VE = vcp8[NKT - 1]
                for hh in range(2):
                    qh_ = q0_sb[hh * 64:(hh + 1) * 64, :, :]
                    kh = kp[hh * 64:(hh + 1) * 64, :]
                    for qtg in range(4):
                        pan_v = None
                        for qt in range(2):
                            tqi = qtg * 2 + qt
                            lhs = qh_[:, p, tqi * 128:(tqi + 1) * 128]
                            rmax_v = [0] * NCH
                            for c in range(NCH):
                                b = c % 2
                                if c >= 2:
                                    sch.wait('tensor', 've', rmax_v[c - 2])
                                else:
                                    sch.wait('tensor', 'ac', prev_exp)
                                pv = sch.mm_group([(lhs, kh[:, c * 512:(c + 1) * 512])],
                                                  pss[b][:], inc='pe')
                                sch.wait('vector', 'pe', pv)
                                rmax_v[c] = sch.op('vector', lambda e, c=c, pb=pss[b]: e.reduce_max(
                                    mcols[:, c:c + 1], pb[:], axis=AX), inc='ve')
                            sch.wait('vector', 've', rmax_v[NCH - 1])
                            v = sch.op('vector', lambda e: e.reduce_max(
                                mcols[:, NCH:NCH + 1], mcols[:, 0:NCH], axis=AX), inc='ve')
                            sch.wait('vector', 've', v)
                            v = sch.op('vector', lambda e: e.tensor_scalar_mul(
                                mcols[:, NCH + 1:NCH + 2], mcols[:, NCH:NCH + 1], -1.0), inc='ve')
                            sch.wait('scalar', 've', v)
                            exp_v = [0] * NCH
                            for c in range(NCH):
                                b = c % 2
                                sch.wait('tensor', 've', v)
                                if c >= 2:
                                    sch.wait('tensor', 'ac', exp_v[c - 2])
                                pv = sch.mm_group([(lhs, kh[:, c * 512:(c + 1) * 512])],
                                                  pss[b][:], inc='pe')
                                sch.wait('scalar', 'pe', pv)
                                exp_v[c] = sch.op('scalar', lambda e, c=c, pb=pss[b], att=att: e.activation(
                                    att[:, c * 512:(c + 1) * 512], pb[:], AF.Exp,
                                    bias=mcols[:, NCH + 1:NCH + 2],
                                    accum_out=zcols[:, c:c + 1]), inc='ac')
                            prev_exp = exp_v[NCH - 1]
                            sch.wait('vector', 'ac', exp_v[NCH - 1])
                            v = sch.op('vector', lambda e: e.reduce_sum(
                                zcols[:, NCH:NCH + 1], zcols[:, 0:NCH], axis=AX), inc='ve')
                            sch.wait('vector', 've', v)
                            v = sch.op('vector', lambda e: e.reciprocal(
                                zcols[:, NCH + 1:NCH + 2], zcols[:, NCH:NCH + 1]), inc='ve')
                            sch.wait('vector', 've', v)
                            v = sch.op('vector', lambda e: e.tensor_scalar_mul(
                                zcols[:, NCH + 2:NCH + 3], zcols[:, NCH + 1:NCH + 2],
                                1.0 / math.sqrt(512.0)), inc='ve')
                            sch.wait('vector', 've', v)
                            normv = sch.op('vector', lambda e, att=att, attn=attn: e.tensor_scalar(
                                attn[:], att[:], zcols[:, NCH + 2:NCH + 3], None, ALU.mult), inc='ve')
                            sch.wait('tensor', 've', normv)
                            if hh == 0 and qtg == 0 and qt == 0:
                                sch.wait('tensor', 've', VP_VE)
                            for kt in range(NKT):
                                half = (kt // 4) % 2
                                if kt >= 8:
                                    sch.wait('tensor', 've', pan_cp[kt - 8])
                                tv = sch.op('tensor', lambda e, kt=kt, pb=pstb[half], attn=attn: e.transpose(
                                    pb[:, (kt % 4) * 128:(kt % 4) * 128 + 128],
                                    attn[:, kt * 128:(kt + 1) * 128], idb_s[:]), inc='pe')
                                if kt % 4 == 3:
                                    sch.wait('vector', 'pe', tv)
                                    pv4 = sch.op('vector', lambda e, kt=kt, pb=pstb[half], qt=qt,
                                                 panels=panels: e.tensor_copy(
                                        panels[:, kt - 3:kt + 1, qt * 128:(qt + 1) * 128],
                                        pb[:, 0:512]
                                        .rearrange("p (a b) -> p a b", a=4)), inc='ve')
                                    if kt == 3:
                                        pan_cp = [0] * NKT
                                    for z_ in range(kt - 3, kt + 1):
                                        pan_cp[z_] = pv4
                            pan_v = pan_cp[NKT - 1]
                        prev_pan_last = pan_v
                        sch.wait('tensor', 've', pan_v)
                        pairs = [(vp[:, kt, hh * 64:(hh + 1) * 64], panels[:, kt, :])
                                 for kt in range(NKT)]
                        pvv = sch.mm_group(pairs, pso[:], inc='pe')
                        sch.wait('vector', 'pe', pvv)
                        ov = sch.op('vector', lambda e, hh=hh, p=p, qtg=qtg, pso=pso: e.tensor_copy(
                            o_sb0[hh * 64:(hh + 1) * 64, p, qtg * 256:(qtg + 1) * 256],
                            pso[:]), inc='ve')
                        sch.wait('tensor', 've', ov)
                        kv_users_ac = sch.val('ac')
                    dq.emit(4)

        sch.barrier()

        s23 = contextlib.ExitStack()
        xn1_used_as = s23.enter_context(
            nc.sbuf_tensor("xn1_res", [128, NDT, TLOC], BF16))
        with contextlib.ExitStack() as pctx:
            S = lambda name, shape, dt=F32: pctx.enter_context(nc.sbuf_tensor(name, shape, dt))
            wo_sb = S("c_wo", [128, NDT, 512], BF16)
            bo_s = S("c_bo", [128, NDT], F32)
            x1hold = S("c_x1", [128, NDT, 512], F32)
            xTt = [S(f"c_xt{i}", [128, 512], F32) for i in range(2)]
            g0t = [S(f"c_g{i}", [128, 512], F32) for i in range(2)]
            fb1 = [S(f"c_f1{i}", [128, 512], F32) for i in range(2)]
            sqb = [S(f"c_sq{i}", [128, 512], F32) for i in range(2)]
            pscb = [pctx.enter_context(nc.psum_tensor(f"c_ps{i}", [128, 512], F32))
                    for i in range(2)]
            psm = [pctx.enter_context(nc.psum_tensor(f"c_psm{i}", [1, 512], F32))
                   for i in range(2)]
            psq = [pctx.enter_context(nc.psum_tensor(f"c_psq{i}", [1, 512], F32))
                   for i in range(2)]
            psmb = pctx.enter_context(nc.psum_tensor("c_psmb", [128, 512], F32))
            psrb = pctx.enter_context(nc.psum_tensor("c_psrb", [128, 512], F32))
            rowv = S("c_rowv", [1, 512 * 3], F32)

            lw = sch.dma('gpsimd', wo_sb[:], wo0.transpose([1, 0, 2]), inc='clw')
            lw = sch.dma('gpsimd', bo_s[:], bo0[:], inc='clw')
            sch.wait('tensor', 'clw', lw)
            cmb_v = {0: 0, 1: 0}
            sq_a = {0: 0, 1: 0}
            qs_pe = {0: 0, 1: 0}
            MQ = 0
            for ch in range(2):
                if ch == 1:
                    sch.wait('vector', 'pe', MQ)
                for o in range(NDT):
                    b = o % 2
                    u = ch * NDT + o
                    if u >= 2:
                        sch.wait('gpsimd', 've', cmb_v[b])
                    lx = sch.dma('gpsimd', xTt[b][:],
                                 xT[o * 128:(o + 1) * 128, ch * 512:(ch + 1) * 512],
                                 inc='cldx%d' % b)
                    lx = sch.dma('gpsimd', g0t[b][:],
                                 g0_scr[o * 128:(o + 1) * 128, ch * 512:(ch + 1) * 512],
                                 inc='cldx%d' % b)
                    if u >= 2:
                        sch.wait('tensor', 've', cmb_v[b])
                        sch.wait('tensor', 'ac', sq_a[b])
                    pairs = [(wo_sb[:, o, j * 128:(j + 1) * 128],
                              o_sb0[:, j, ch * 512:(ch + 1) * 512]) for j in range(4)]
                    pv = sch.mm_group(pairs, pscb[b][:], inc='pe')
                    sch.wait('vector', 'pe', pv)
                    sch.wait('vector', 'cldx%d' % b, lx)
                    v = sch.op('vector', lambda e, b=b, o=o: e.tensor_scalar(
                        fb1[b][:], pscb[b][:], bo_s[:, o:o + 1], None, ALU.add), inc='ve')
                    sch.wait('vector', 've', v)
                    v = sch.op('vector', lambda e, b=b: e.tensor_mul(
                        fb1[b][:], fb1[b][:], g0t[b][:]), inc='ve')
                    sch.wait('vector', 've', v)
                    v = sch.op('vector', lambda e, b=b, o=o: e.tensor_add(
                        x1hold[:, o, :], fb1[b][:], xTt[b][:]), inc='ve')
                    cmb_v[b] = v
                    sch.wait('sync', 've', v)
                    sch.dma('sync', x1_scr[o * 128:(o + 1) * 128, ch * 512:(ch + 1) * 512],
                            x1hold[:, o, :], inc='cstx')
                    sch.wait('scalar', 've', v)
                    if qs_pe[b]:
                        sch.wait('scalar', 'pe', qs_pe[b])
                    a = sch.op('scalar', lambda e, b=b, o=o: e.activation(
                        sqb[b][:], x1hold[:, o, :], AF.Square), inc='ac')
                    sq_a[b] = a
                    sch.wait('tensor', 've', v)
                    sch.mm(psm[ch][:], ones128_s[:], x1hold[:, o, :],
                           start=(o == 0), stop=(o == NDT - 1), inc='pe')
                    sch.wait('tensor', 'ac', a)
                    qs_pe[b] = sch.mm(psq[ch][:], ones128_s[:], sqb[b][:],
                                      start=(o == 0), stop=(o == NDT - 1),
                                      inc='pe', always=True)
                MQ = sch.val('pe')
                sch.wait('vector', 'pe', MQ)
                rr = rowv[0:1, 0:512]
                rs = rowv[0:1, 512:1024]
                v = sch.op('vector', lambda e, ch=ch, rr=rr: e.tensor_scalar_mul(
                    rr, psm[ch][:], 1.0 / DM), inc='ve')
                sch.wait('vector', 've', v)
                v = sch.op('vector', lambda e, ch=ch, rr=rr, rs=rs: e.tensor_scalar(
                    rs, psq[ch][:], 1.0 / DM, None, ALU.mult), inc='ve')
                sch.wait('vector', 've', v)
                v = sch.op('vector', lambda e, rr=rr, rowv=rowv: e.tensor_mul(
                    rowv[0:1, 1024:1536], rr, rr), inc='ve')
                sch.wait('vector', 've', v)
                v = sch.op('vector', lambda e, rs=rs, rowv=rowv: e.tensor_sub(
                    rs, rs, rowv[0:1, 1024:1536]), inc='ve')
                sch.wait('vector', 've', v)
                v = sch.op('vector', lambda e, rs=rs: e.tensor_scalar_add(rs, rs, 1e-5), inc='ve')
                sch.wait('vector', 've', v)
                v = sch.op('vector', lambda e, rs=rs: e.reciprocal(rs, rs), inc='ve')
                sch.wait('scalar', 've', v)
                a = sch.op('scalar', lambda e, rs=rs: e.activation(rs, rs, AF.Sqrt), inc='ac')
                sch.wait('tensor', 'ac', a)
                if ch == 1:
                    sch.wait('tensor', 've', NRM_VE)
                sch.mm(psmb[:], ones1_s[:], rr, start=True, stop=True, inc='pe')
                bv = sch.mm(psrb[:], ones1_s[:], rs, start=True, stop=True, inc='pe')
                sch.wait('vector', 'pe', bv)
                for o in range(NDT):
                    b = o % 2
                    v = sch.op('vector', lambda e, b=b, o=o: e.tensor_sub(
                        fb1[b][:], x1hold[:, o, :], psmb[:]), inc='ve')
                    sch.wait('vector', 've', v)
                    v = sch.op('vector', lambda e, b=b, o=o, ch=ch: e.tensor_mul(
                        xn1_used_as[:, o, ch * 512:(ch + 1) * 512], fb1[b][:], psrb[:]), inc='ve')
                    sch.wait('sync', 've', v)
                    sch.dma('sync', xn1_scr[o * 128:(o + 1) * 128, ch * 512:(ch + 1) * 512],
                            xn1_used_as[:, o, ch * 512:(ch + 1) * 512], inc='cstx')
                NRM_VE = sch.val('ve')
            sch.wait('sync', 'cstx', sch.val('cstx'))

        sch.barrier()

        with contextlib.ExitStack() as pctx:
            S = lambda name, shape, dt=F32: pctx.enter_context(nc.sbuf_tensor(name, shape, dt))
            wsl = [S(f"e_wsl{i}", [128, DM], BF16) for i in range(2)]
            la_s = S("e_la", [128, NDT * 16], BF16)
            wn_s = S("e_wn", [128, 2], F32)
            sqa = [S(f"e_sq{i}", [128, 512], BF16) for i in range(2)]
            tmpA = [S(f"e_tmp{i}", [128, 512], F32) for i in range(2)]
            qev = [S(f"e_qe{i}", [128, 512], BF16) for i in range(2)]
            rwa = S("e_rw", [1, 1024], F32)
            dq2 = Dequant.__new__(Dequant)
            dq2.sch = sch
            dq2.cod_i = S("e_dqc", [128, 2048], I32)
            dq2.codb = S("e_dqcb", [128, 2048], BF16)
            dq2.amx = S("e_dqa", [128, 2048], F32)
            dq2.dacc = [S(f"e_dqd{i}", [128, 2048], BF16) for i in range(2)]
            dq2.dtmp = S("e_dqt", [128, 2048], BF16)
            dq2.units = dq.units
            dq2.i = dq.i
            dq2.cv_prev = 0
            dq2.st = {0: 0, 1: 0}
            psA = [pctx.enter_context(nc.psum_tensor(f"e_ps{i}", [128, 512], F32))
                   for i in range(2)]
            psn = [pctx.enter_context(nc.psum_tensor(f"e_psn{i}", [1, 512], F32))
                   for i in range(2)]
            psrbA = [pctx.enter_context(nc.psum_tensor(f"e_psrb{i}", [128, 512], F32))
                     for i in range(2)]
            psu = pctx.enter_context(nc.psum_tensor("e_psu", [16, 512], F32))

            ld = sch.dma('gpsimd', la_s[:], la_up[:], inc='eld')
            ld = sch.dma('gpsimd', wn_s[:, 0:1], wnq[:], inc='eld')
            ld = sch.dma('gpsimd', wn_s[:, 1:2], wnk[:], inc='eld')
            sch.wait('tensor', 'eld', ld)
            for c in range(2):
                if c == 1:
                    sch.wait('tensor', 've', uv)
                pairs = [(la_s[:, d * 16:(d + 1) * 16], xn1_used_as[:, d, c * 512:(c + 1) * 512])
                         for d in range(NDT)]
                pv = sch.mm_group(pairs, psu[:], inc='pe')
                sch.wait('vector', 'pe', pv)
                uv = sch.op('vector', lambda e, c=c: e.tensor_copy(
                    u1s[0:16, c * 512:(c + 1) * 512], psu[:]), inc='ve')

            nld1 = {0: 0, 1: 0}
            eev = {0: 0, 1: 0}
            est = {0: 0, 1: 0}
            eac = {0: 0, 1: 0}
            for o in range(NQKV1):
                b = o % 2
                if o >= 2:
                    sch.wait('gpsimd', 've', eev[0])
                    sch.wait('gpsimd', 've', eev[1])
                nld1[b] += 1
                sch.dma('gpsimd', wsl[b][:], wqkv1[o, :, :], inc='elw%d' % b)
                sch.wait('tensor', 'elw%d' % b, 16 * nld1[b])
                for c in range(2):
                    b2 = (2 * o + c) % 2
                    pairs = [(wsl[b][:, d * 128:(d + 1) * 128],
                              xn1_used_as[:, d, c * 512:(c + 1) * 512]) for d in range(NDT)]
                    if 2 * o + c >= 2:
                        sch.wait('tensor', 've', eev[b2])
                    pv = sch.mm_group(pairs, psA[b2][:], inc='pe')
                    if o < 40:
                        sch.wait('scalar', 'pe', pv)
                        if eac[b2]:
                            sch.wait('scalar', 've', eev[b2])
                        a = sch.op('scalar', lambda e, b2=b2: e.activation(
                            sqa[b2][:], psA[b2][:], AF.Square), inc='ac')
                        eac[b2] = a
                        sch.wait('tensor', 'ac', a)
                        nv = sch.mm(psn[b2][:], ones128b_s[:], sqa[b2][:],
                                    start=True, stop=True, inc='pe')
                        sch.wait('vector', 'pe', nv)
                        rr = rwa[0:1, b2 * 512:b2 * 512 + 512]
                        v = sch.op('vector', lambda e, b2=b2, rr=rr: e.tensor_scalar(
                            rr, psn[b2][:], 1.0 / 128.0, 1e-6, ALU.mult, ALU.add), inc='ve')
                        sch.wait('vector', 've', v)
                        v = sch.op('vector', lambda e, rr=rr: e.reciprocal(rr, rr), inc='ve')
                        sch.wait('scalar', 've', v)
                        a = sch.op('scalar', lambda e, rr=rr: e.activation(rr, rr, AF.Sqrt), inc='ac')
                        if o < 32:
                            sch.wait('vector', 'ac', a)
                            v = sch.op('vector', lambda e, rr=rr: e.tensor_scalar_mul(
                                rr, rr, 1.0 / math.sqrt(HD)), inc='ve')
                            sch.wait('tensor', 've', v)
                        else:
                            sch.wait('tensor', 'ac', a)
                        bv = sch.mm(psrbA[b2][:], ones1_s[:], rr, start=True, stop=True, inc='pe')
                        sch.wait('vector', 'pe', bv)
                        wcol = wn_s[:, 0:1] if o < 32 else wn_s[:, 1:2]
                        v = sch.op('vector', lambda e, b2=b2, wcol=wcol: e.tensor_scalar(
                            tmpA[b2][:], psA[b2][:], wcol, None, ALU.mult), inc='ve')
                        sch.wait('vector', 've', v)
                        if est[b2]:
                            sch.wait('vector', 'est%d' % b2, est[b2])
                        v = sch.op('vector', lambda e, b2=b2: e.tensor_mul(
                            qev[b2][:], tmpA[b2][:], psrbA[b2][:]), inc='ve')
                    else:
                        sch.wait('vector', 'pe', pv)
                        if est[b2]:
                            sch.wait('vector', 'est%d' % b2, est[b2])
                        v = sch.op('vector', lambda e, b2=b2: e.tensor_copy(
                            qev[b2][:], psA[b2][:]), inc='ve')
                    eev[b2] = v
                    sch.wait('sync', 've', v)
                    if o < 32:
                        dst = q1_scr[o * 128:(o + 1) * 128, c * 512:(c + 1) * 512]
                    elif o < 40:
                        fo = o - 32
                        dst = k1c[c * 1024 + fo * 128:c * 1024 + (fo + 1) * 128, :]
                    else:
                        fo = o - 40
                        dst = v1c[c * 1024 + fo * 128:c * 1024 + (fo + 1) * 128, :]
                    est[b2] = sch.dma('sync', dst, qev[b2][:], inc='est%d' % b2)
                if o == 1:
                    sch.wait('gpsimd', 'dqst0', dq.st[0])
                    sch.wait('gpsimd', 'dqst1', dq.st[1])
                    sch.collective(wupc, WUPG, 'ccWU')
                if o == 39:
                    sch.wait('gpsimd', 'est0', sch.val('est0'))
                    sch.wait('gpsimd', 'est1', sch.val('est1'))
                    sch.collective(k1c, KG1, 'ccK1')
                if o == 47:
                    sch.wait('gpsimd', 'est0', sch.val('est0'))
                    sch.wait('gpsimd', 'est1', sch.val('est1'))
                    sch.collective(v1c, VG1, 'ccV1')
                if o >= 4 and o % 2 == 0:
                    dq2.emit(1)
            dq2.emit(54)
            sch.wait('gpsimd', 'dqst0', dq2.st[0])
            sch.wait('gpsimd', 'dqst1', dq2.st[1])
            sch.collective(wdnc, WDNG, 'ccWD')

        s23.close()
        sch.barrier()

        with contextlib.ExitStack() as pctx:
            S = lambda name, shape, dt: pctx.enter_context(nc.sbuf_tensor(name, shape, dt))
            ksb = [S(f"f_k{i}", [128, SMAX * 512], BF16) for i in range(2)]
            vsb = [S(f"f_v{i}", [128, SMAX * 4, 128], BF16) for i in range(2)]
            vstg = [S(f"f_vs{i}", [128, 512], BF16) for i in range(2)]
            panels = S("f_pan", [128, SMAX * 4, 256], BF16)
            att = S("f_att", [128, SMAX * 512], BF16)
            qh = [S(f"f_qh{i}", [128, TLOC], BF16) for i in range(2)]
            scrb = [S(f"f_sc{i}", [128, 512], F32) for i in range(2)]
            zc = S("f_zc", [128, 24], F32)
            ob = [S(f"f_ob{i}", [128, 512], BF16) for i in range(2)]
            bia_s = S("f_bia", [128, H1 * 2 * SMAX * 4], F32)
            rel_s = S("f_rel", [128, 5, 512], F32)
            rels = S("f_rels", [128, 5, 512], F32)
            kidx_s = S("f_kidx", [128, HKV * 2 * SMAX], I32)
            pss = [pctx.enter_context(nc.psum_tensor(f"f_pss{i}", [128, 512], F32))
                   for i in range(2)]
            pstb = [pctx.enter_context(nc.psum_tensor(f"f_pstb{i}", [128, 1024], BF16))
                    for i in range(2)]
            pso = pctx.enter_context(nc.psum_tensor("f_pso", [128, 512], F32))

            ld0 = sch.dma('gpsimd', bia_s[:], biast[:], inc='fld')
            ld0 = sch.dma('gpsimd', rel_s[:], rel5.transpose([1, 0, 2]), inc='fld')
            ld0 = sch.dma('gpsimd', kidx_s[:], kidx[:], inc='fld')
            sch.wait('tensor', 'fld', ld0)
            sch.wait('scalar', 'fld', ld0)
            sch.wait('vector', 'fld', ld0)
            sch.cwait('gpsimd', 'ccK1')
            sch.cwait('gpsimd', 'ccV1')
            kv_done = 0
            kv_done_prev = 0
            qh_done = {0: 0, 1: 0}
            st_o = {0: 0, 1: 0}
            exp_prev = {0: 0, 1: 0}
            vstg_pe = {0: 0, 1: 0}
            hq_idx = 0
            for kv in range(HKV):
                for gi in range(2):
                    it = kv * 2 + gi
                    bkv = it % 2
                    S_g = (SKV_A if gi == 0 else SKV_B)[kv]
                    if it >= 2:
                        sch.wait('gpsimd', 'pe', kv_done_prev)
                    for s in range(S_g):
                        col = (kv * 2 + gi) * SMAX + s
                        sch.indirect(ksb[bkv][:, s * 512:(s + 1) * 512], KG1[0:128, :],
                                     kidx_s[:, col:col + 1], inc='fldk%d' % bkv)
                    lk = None
                    for s in range(S_g):
                        vb = s % 2
                        col = (kv * 2 + gi) * SMAX + s
                        if vstg_pe[vb]:
                            sch.wait('gpsimd', 'pe', vstg_pe[vb])
                        lv = sch.indirect(vstg[vb][:], VG1[0:128, :],
                                          kidx_s[:, col:col + 1], inc='fldv%d' % vb)
                        sch.wait('tensor', 'fldv%d' % vb, lv)
                        par = s % 2
                        if it >= 1 or s >= 2:
                            sch.wait('tensor', 've', vq_cp.get((bkv, s - 2), prev_pan8))
                        tvv = None
                        for kb in range(4):
                            tvv = sch.op('tensor', lambda e, kb=kb, par=par, vb=vb: e.transpose(
                                pstb[par][:, kb * 128:(kb + 1) * 128],
                                vstg[vb][:, kb * 128:(kb + 1) * 128], idb_s[:]), inc='pe')
                        vstg_pe[vb] = tvv
                        sch.wait('vector', 'pe', tvv)
                        cvv = sch.op('vector', lambda e, s=s, par=par, bkv=bkv: e.tensor_copy(
                            vsb[bkv][:, s * 4:(s + 1) * 4, :],
                            pstb[par][:, 0:512].rearrange("p (a b) -> p a b", a=4)), inc='ve')
                        if it == 0 and s == 0:
                            vq_cp = {}
                            prev_pan8 = 0
                        vq_cp[(bkv, s)] = cvv
                    sch.wait('tensor', 'fldk%d' % bkv, sch.val('fldk%d' % bkv))
                    sch.wait('tensor', 've', cvv)
                    kv_done_prev = kv_done
                    for hq in range(4):
                        h = kv * 4 + hq
                        bq = hq_idx % 2
                        if hq_idx >= 2:
                            sch.wait('gpsimd', 'pe', qh_done[bq])
                        lq = sch.dma('gpsimd', qh[bq][:], q1_scr[h * 128:(h + 1) * 128, :],
                                     inc='fldq%d' % bq)
                        S_h = SH_TAB[h][gi]
                        s0 = S_g - S_h
                        sch.wait('vector', 'fldq%d' % bq, lq)
                        sch.wait('vector', 'pe', kv_done)
                        rv = None
                        for vv5 in range(5):
                            rv = sch.op('vector', lambda e, vv5=vv5, h=h: e.tensor_scalar(
                                rels[:, vv5, :], rel_s[:, vv5, :], float(SLOPES[h]),
                                None, ALU.mult), inc='ve')
                        sch.wait('tensor', 've', rv)
                        hq_idx += 1
                        pan_last = None
                        pv_half = 0
                        for j in range(4):
                            jj = j % 2
                            lhs = qh[bq][:, gi * 512 + j * 128: gi * 512 + (j + 1) * 128]
                            exp_v = {}
                            for si in range(S_h):
                                s = s0 + si
                                b = si % 2
                                sch.wait('tensor', 'ac', exp_prev[b])
                                pv = sch.mm_group(
                                    [(lhs, ksb[bkv][:, s * 512:(s + 1) * 512])],
                                    pss[b][:], inc='pe')
                                variant = (1 + j) if s == S_g - 1 else 0
                                sch.wait('vector', 'pe', pv)
                                vv = sch.op('vector', lambda e, b=b, variant=variant:
                                            e.scalar_tensor_tensor(
                                                scrb[b][:], pss[b][:], 0.0,
                                                rels[:, variant, :],
                                                ALU.add, ALU.add), inc='ve')
                                sch.wait('scalar', 've', vv)
                                col = bias_col(h, gi, s, j)
                                av = sch.op('scalar', lambda e, b=b, si=si, col=col:
                                            e.activation(
                                                att[:, si * 512:(si + 1) * 512],
                                                scrb[b][:], AF.Exp,
                                                bias=bia_s[:, col:col + 1],
                                                accum_out=zc[:, si:si + 1]), inc='ac')
                                exp_v[si] = av
                                exp_prev[b] = av
                            sch.wait('vector', 'ac', exp_v[S_h - 1])
                            v = sch.op('vector', lambda e, S_h=S_h: e.reduce_sum(
                                zc[:, 16:17], zc[:, 0:S_h], axis=AX), inc='ve')
                            sch.wait('vector', 've', v)
                            v = sch.op('vector', lambda e: e.reciprocal(
                                zc[:, 17:18], zc[:, 16:17]), inc='ve')
                            sch.wait('vector', 've', v)
                            normv = sch.op('vector', lambda e, S_h=S_h: e.tensor_scalar(
                                att[:, 0:S_h * 512], att[:, 0:S_h * 512],
                                zc[:, 17:18], None, ALU.mult), inc='ve')
                            sch.wait('tensor', 've', normv)
                            pan_cp = {}
                            for kt in range(S_h * 4):
                                half = (kt // 4) % 2
                                if kt >= 8:
                                    sch.wait('tensor', 've', pan_cp[kt - 8])
                                tv = sch.op('tensor', lambda e, kt=kt, half=half: e.transpose(
                                    pstb[half][:, (kt % 4) * 128:(kt % 4) * 128 + 128],
                                    att[:, kt * 128:(kt + 1) * 128], idb_s[:]), inc='pe')
                                if kt % 4 == 3 or kt == S_h * 4 - 1:
                                    k0g = (kt // 4) * 4
                                    nk4 = kt - k0g + 1
                                    sch.wait('vector', 'pe', tv)
                                    if j >= 2:
                                        sch.wait('vector', 'pe', pv_half)
                                    pv4 = sch.op('vector', lambda e, k0g=k0g, nk4=nk4,
                                                 half=half, jj=jj, s0=s0: e.tensor_copy(
                                        panels[:, s0 * 4 + k0g:s0 * 4 + k0g + nk4,
                                               jj * 128:(jj + 1) * 128],
                                        pstb[half][:, 0:nk4 * 128]
                                        .rearrange("p (a b) -> p a b", a=nk4)), inc='ve')
                                    for z_ in range(k0g, kt + 1):
                                        pan_cp[z_] = pv4
                            pan_last = pan_cp[S_h * 4 - 1]
                            if j % 2 == 1:
                                sch.wait('tensor', 've', pan_last)
                                jh = j // 2
                                pairs = [(vsb[bkv][:, kt, :], panels[:, kt, :])
                                         for kt in range(s0 * 4, S_g * 4)]
                                pv_half = sch.mm_group(
                                    pairs, pso[:, jh * 256:(jh + 1) * 256], inc='pe')
                        prev_pan8 = pan_last
                        pvv = pv_half
                        kv_done = pvv
                        qh_done[bq] = pvv
                        bo_ = hq_idx % 2
                        sch.wait('vector', 'pe', pvv)
                        if st_o[bo_]:
                            sch.wait('vector', 'fsto%d' % bo_, st_o[bo_])
                        ov = sch.op('vector', lambda e, bo_=bo_: e.tensor_copy(
                            ob[bo_][:], pso[:]), inc='ve')
                        sch.wait('sync', 've', ov)
                        st_o[bo_] = sch.dma(
                            'sync', o_scr[h * 128:(h + 1) * 128, gi * 512:(gi + 1) * 512],
                            ob[bo_][:], inc='fsto%d' % bo_)
                        sch.wait('tensor', 've', ov)
            sch.wait('sync', 'fsto0', sch.val('fsto0'))
            sch.wait('sync', 'fsto1', sch.val('fsto1'))

        sch.barrier()

        with contextlib.ExitStack() as pctx:
            S = lambda name, shape, dt: pctx.enter_context(nc.sbuf_tensor(name, shape, dt))
            x2 = S("g_x2", [128, NHT, 512], BF16)
            u2s = S("g_u2", [16, 512], BF16)
            lbup_s = S("g_lbu", [16, HF], BF16)
            ladn_s = S("g_lad", [128, NHT * 16], BF16)
            lbdn_s = S("g_lbd", [16, DM], BF16)
            bup_s = S("g_bup", [128, NHT], F32)
            bdn_s = S("g_bdn", [128, NDT], F32)
            xnc = S("g_xnc", [128, NDT, 512], BF16)
            wu_sl = [S(f"g_wu{i}", [128, 896], BF16) for i in range(2)]
            wd_sl = [S(f"g_wd{i}", [128, 1024], BF16) for i in range(2)]
            gvb = [S(f"g_gv{i}", [128, 512], BF16) for i in range(2)]

            lc = sch.dma('gpsimd', lbup_s[:], lbup[:], inc='gld')
            lc = sch.dma('gpsimd', ladn_s[:], ladn[:], inc='gld')
            lc = sch.dma('gpsimd', lbdn_s[:], lbdn[:], inc='gld')
            lc = sch.dma('gpsimd', bup_s[:], bup[:], inc='gld')
            lc = sch.dma('gpsimd', bdn_s[:], bdn[:], inc='gld')
            sch.wait('tensor', 'gld', lc)
            sch.wait('scalar', 'gld', lc)
            sch.cwait('sync', 'ccWU')
            sch.cwait('sync', 'ccWD')
            sch.cwait('tensor', 'ccWU')
            sch.cwait('tensor', 'ccWD')

            nxl = 0
            wu_pe = {0: 0, 1: 0}
            wd_pe = {0: 0, 1: 0}
            x2_ac = 0
            dn_ac = 0
            stg1 = {0: 0, 1: 0}
            XNC_PREV = 0
            nslw = {0: 0, 1: 0}
            nsld = {0: 0, 1: 0}
            for tc in range(2):
                with contextlib.ExitStack() as tctx:
                    psH = [tctx.enter_context(nc.psum_tensor(f"g_psH{tc}{i}", [128, 512], F32))
                           for i in range(7)]
                    psu2 = tctx.enter_context(nc.psum_tensor(f"g_psu2{tc}", [16, 512], F32))
                    if tc > 0:
                        sch.wait('gpsimd', 'pe', XNC_PREV)
                    nxl += 1
                    lx = sch.dma('gpsimd', xnc[:],
                                 xn1_scr[:, tc * 512:(tc + 1) * 512]
                                 .rearrange("(d p) t -> p d t", p=128), inc='gxl')
                    sch.wait('tensor', 'gxl', 16 * nxl)
                    for r in range(NCORES):
                        ntr = HTILES[r]
                        for (g0, g1) in ((0, 7), (7, ntr)):
                            gs = g1 - g0
                            if x2_ac:
                                sch.wait('tensor', 'ac', x2_ac)
                            if dn_ac:
                                sch.wait('tensor', 'ac', dn_ac)
                            for j in range(gs):
                                ht = HTOFF[r] + g0 + j
                                sch.mm(psH[j][:], lbup_s[0:16, ht * 128:(ht + 1) * 128],
                                       u1s[0:16, tc * 512:(tc + 1) * 512],
                                       start=True, stop=False)
                            for dt in range(NDT):
                                b = dt % 2
                                if wu_pe[b]:
                                    sch.wait('sync', 'pe', wu_pe[b])
                                nslw[b] += 1
                                sch.dma('sync', wu_sl[b][:, 0:gs * 128],
                                        WUPG[r * DM + dt * 128:r * DM + (dt + 1) * 128,
                                             g0 * 128:g1 * 128], inc='gwu%d' % b)
                                sch.wait('tensor', 'gwu%d' % b, 16 * nslw[b])
                                for j in range(gs):
                                    pv = sch.mm(psH[j][:], wu_sl[b][:, j * 128:(j + 1) * 128],
                                                xnc[:, dt, :], start=False, stop=(dt == NDT - 1),
                                                inc='pe', always=(j == gs - 1))
                                wu_pe[b] = pv
                            PV_UP = sch.val('pe')
                            sch.wait('scalar', 'pe', PV_UP)
                            for j in range(gs):
                                ht = HTOFF[r] + g0 + j
                                a = sch.op('scalar', lambda e, pb=psH[j], ht=ht: e.activation(
                                    x2[:, ht, :], pb[:], AF.Relu,
                                    bias=bup_s[:, ht:ht + 1]), inc='ac')
                                x2_ac = a
                                sch.wait('tensor', 'ac', a)
                                sch.mm(psu2[:], ladn_s[:, ht * 16:(ht + 1) * 16], x2[:, ht, :],
                                       start=(ht == 0), stop=(ht == NHT - 1),
                                       inc=('pe' if ht == NHT - 1 else None))
                    U2PE = sch.val('pe')
                    sch.wait('vector', 'pe', U2PE)
                    u2v = sch.op('vector', lambda e, pb=psu2: e.tensor_copy(u2s[:], pb[:]), inc='ve')
                    if debug:
                        sch.wait('sync', 'ac', x2_ac)
                        dv = sch.dma('sync',
                                     x2d[:, tc * 512:(tc + 1) * 512]
                                     .rearrange("(ht p) t -> p ht t", p=128),
                                     x2[:], inc='dbgf')
                        sch.wait('sync', 've', u2v)
                        dv = sch.dma('sync', u2d[0:16, tc * 512:(tc + 1) * 512],
                                     u2s[:], inc='dbgf')
                        dv = sch.dma('sync', u1d, u1s[:], inc='dbgf')
                        sch.wait('sync', 'dbgf', dv)
                with contextlib.ExitStack() as tctx:
                    psD = [tctx.enter_context(nc.psum_tensor(f"g_psD{tc}{i}", [128, 512], F32))
                           for i in range(8)]
                    sch.wait('tensor', 've', u2v)
                    for dg in range(4):
                        if x2_ac:
                            sch.wait('tensor', 'ac', x2_ac)
                        if dn_ac:
                            sch.wait('tensor', 'ac', dn_ac)
                        for j in range(8):
                            dt = dg * 8 + j
                            sch.mm(psD[j][:], lbdn_s[0:16, dt * 128:(dt + 1) * 128],
                                   u2s[:], start=True, stop=False)
                        for ht in range(NHT):
                            b = ht % 2
                            r, lt = rank_lt(ht)
                            if wd_pe[b]:
                                sch.wait('sync', 'pe', wd_pe[b])
                            nsld[b] += 1
                            sch.dma('sync', wd_sl[b][:],
                                    WDNG[r * HSH_PAD + lt * 128:r * HSH_PAD + (lt + 1) * 128,
                                         dg * 1024:(dg + 1) * 1024], inc='gwd%d' % b)
                            sch.wait('tensor', 'gwd%d' % b, 16 * nsld[b])
                            for j in range(8):
                                pv = sch.mm(psD[j][:], wd_sl[b][:, j * 128:(j + 1) * 128],
                                            x2[:, ht, :], start=False, stop=(ht == NHT - 1),
                                            inc='pe', always=(j == 7))
                            wd_pe[b] = pv
                        PV_DN = sch.val('pe')
                        sch.wait('scalar', 'pe', PV_DN)
                        for j in range(8):
                            dt = dg * 8 + j
                            b2 = j % 2
                            if stg1[b2]:
                                sch.wait('scalar', 'gst%d' % b2, stg1[b2])
                            a = sch.op('scalar', lambda e, pb=psD[j], dt=dt, b2=b2: e.activation(
                                gvb[b2][:], pb[:], AF.Silu,
                                bias=bdn_s[:, dt:dt + 1]), inc='ac')
                            dn_ac = a
                            sch.wait('sync', 'ac', a)
                            stg1[b2] = sch.dma(
                                'sync', g1_scr[dt * 128:(dt + 1) * 128,
                                               tc * 512:(tc + 1) * 512],
                                gvb[b2][:], inc='gst%d' % b2)
                    XNC_PREV = sch.val('pe')
            sch.wait('sync', 'gst0', sch.val('gst0'))
            sch.wait('sync', 'gst1', sch.val('gst1'))

        sch.barrier()

        with contextlib.ExitStack() as pctx:
            S = lambda name, shape, dt: pctx.enter_context(nc.sbuf_tensor(name, shape, dt))
            o_sb = S("h_o", [128, NDT, 512], BF16)
            wos = [S(f"h_w{i}", [128, DM], BF16) for i in range(2)]
            x1t = [S(f"h_x{i}", [128, 512], F32) for i in range(2)]
            g1t = [S(f"h_g{i}", [128, 512], BF16) for i in range(2)]
            tmpc = [S(f"h_t{i}", [128, 512], F32) for i in range(2)]
            outb = [S(f"h_ob{i}", [128, 512], F32) for i in range(2)]
            psf = [pctx.enter_context(nc.psum_tensor(f"h_ps{i}", [128, 512], F32))
                   for i in range(2)]

            nwo = {0: 0, 1: 0}
            nxg = {0: 0, 1: 0}
            sto = {0: 0, 1: 0}
            cmb_v = {0: 0, 1: 0}
            pe_wo = {0: 0, 1: 0}
            nol = 0
            for tc in range(2):
                if tc > 0:
                    sch.wait('gpsimd', 'pe', sch.val('pe'))
                nol += 1
                lo = sch.dma('gpsimd', o_sb[:],
                             o_scr[:, tc * 512:(tc + 1) * 512]
                             .rearrange("(nd p) t -> p nd t", p=128), inc='hlo')
                sch.wait('tensor', 'hlo', 16 * nol)
                for dt_i in range(NDT):
                    u = tc * NDT + dt_i
                    b = u % 2
                    if u >= 2:
                        sch.wait('gpsimd', 'pe', pe_wo[b])
                        sch.wait('gpsimd', 've', cmb_v[b])
                    nwo[b] += 1
                    sch.dma('gpsimd', wos[b][:], wo1[dt_i], inc='hlw%d' % b)
                    nxg[b] += 1
                    lgf = sch.dma('gpsimd', x1t[b][:],
                                  x1_scr[dt_i * 128:(dt_i + 1) * 128, tc * 512:(tc + 1) * 512],
                                  inc='hlg%d' % b)
                    lgf = sch.dma('gpsimd', g1t[b][:],
                                  g1_scr[dt_i * 128:(dt_i + 1) * 128, tc * 512:(tc + 1) * 512],
                                  inc='hlg%d' % b)
                    sch.wait('tensor', 'hlw%d' % b, 16 * nwo[b])
                    pairs = [(wos[b][:, j * 128:(j + 1) * 128], o_sb[:, j, :])
                             for j in range(NDT)]
                    sch.wait('tensor', 've', cmb_v[b])
                    pv = sch.mm_group(pairs, psf[b][:], inc='pe')
                    pe_wo[b] = pv
                    sch.wait('vector', 'pe', pv)
                    sch.wait('vector', 'hlg%d' % b, lgf)
                    if sto[b]:
                        sch.wait('vector', 'hst%d' % b, sto[b])
                    v = sch.op('vector', lambda e, b=b: e.tensor_mul(
                        tmpc[b][:], psf[b][:], g1t[b][:]), inc='ve')
                    sch.wait('vector', 've', v)
                    v = sch.op('vector', lambda e, b=b: e.tensor_add(
                        outb[b][:], tmpc[b][:], x1t[b][:]), inc='ve')
                    cmb_v[b] = v
                    sch.wait('sync', 've', v)
                    sto[b] = sch.dma(
                        'sync', outF[dt_i * 128:(dt_i + 1) * 128, tc * 512:(tc + 1) * 512],
                        outb[b][:], inc='hst%d' % b)
            sch.wait('sync', 'hst0', sch.val('hst0'))
            sch.wait('sync', 'hst1', sch.val('hst1'))
        if debug:
            dv = sch.dma('sync', k1d, k1c, inc='dbgc')
            dv = sch.dma('sync', v1d, v1c, inc='dbgc')
            dv = sch.dma('sync', wud, wupc, inc='dbgc')
            dv = sch.dma('sync', wdd, wdnc, inc='dbgc')
            dv = sch.dma('sync', wugd, WUPG, inc='dbgc')
            dv = sch.dma('sync', wdgd, WDNG, inc='dbgc')
            sch.wait('sync', 'dbgc', dv)
        sch.run(block)
    return nc



_PROGS = {}


def _prog(name, builder):
    if name not in _PROGS:
        _PROGS[name] = builder()
    return _PROGS[name]


def _shard_rows(c):
    return np.r_[c * 512:(c + 1) * 512, (15 - c) * 512:(16 - c) * 512]


def prepare_in_maps(inputs):
    g = lambda k: np.asarray(inputs[k])
    x_full = np.ascontiguousarray(g('L0_attn_x').reshape(N, DM).astype(np.float32))
    shards = [_shard_rows(c) for c in range(NCORES)]
    idb = np.eye(128, dtype=BF)
    idf = np.eye(128, dtype=np.float32)

    Wqk_T = g('L0_attn_Wqk_w').T
    hh = np.arange(8)[:, None]
    ee = np.arange(64)[None, :]
    qperm = (hh * 128 + 2 * ee).reshape(-1)
    kperm = (hh * 128 + 2 * ee + 1).reshape(-1)
    big_T = np.concatenate([Wqk_T[:, qperm], Wqk_T[:, kperm], g('L0_attn_Wv_w').T], axis=1)
    b_qk = g('L0_attn_Wqk_b')
    sel8 = np.zeros((8, 128), dtype=np.float32)
    for ei in range(8):
        sel8[ei, ei * 16:(ei + 1) * 16] = 1.0

    wq1 = np.concatenate([g('L1_attn_Wq').T, g('L1_attn_Wk').T, g('L1_attn_Wv').T], axis=1)
    cod_u_pad = np.zeros((DM, 8 * HSH_PAD), dtype=np.int32)
    cod_u_pad[:, :HF] = g('L1_ffn_w_up_codes').reshape(HF, DM).T
    amx_u_pad = np.zeros((64, 8 * HSH_PAD), dtype=np.float32)
    amx_u_pad[:, :HF] = g('L1_ffn_w_up_absmax').reshape(HF, 64).T
    cod_d_pad = np.zeros((8 * HSH_PAD, DM), dtype=np.int32)
    cod_d_pad[:HF] = g('L1_ffn_w_down_codes').reshape(DM, HF).T
    amx_d_pad = np.zeros((176, DM), dtype=np.float32)
    amx_d_pad[:172] = g('L1_ffn_w_down_absmax').reshape(DM, 172).T

    common = dict(
        wqkv0=tile_w(np.ascontiguousarray(big_T)),
        bqkv=cols128(np.concatenate([b_qk[qperm], b_qk[kperm], g('L0_attn_Wv_b')])),
        wrout=np.ascontiguousarray(
            g('L0_ffn_W_router').T.reshape(32, 128, 8).transpose(1, 0, 2).reshape(128, 256)),
        wa=tile_w(np.ascontiguousarray(np.transpose(g('L0_ffn_A'), (2, 0, 1)).reshape(4096, 128))),
        wbase=tile_w(g('L0_ffn_W_base').T),
        bbase=cols128(g('L0_ffn_b_base')),
        bstk=np.ascontiguousarray(
            g('L0_ffn_B').transpose(0, 2, 1).reshape(128, 4096)
            .reshape(128, 32, 128).transpose(1, 0, 2)),
        sel8=sel8, idf=idf, idb=idb,
        ones1=np.ones((1, 128), dtype=np.float32),
        ones128=np.ones((128, 1), dtype=np.float32),
        ones128b=np.ones((128, 1), dtype=BF),
        wo0=tile_w(np.ascontiguousarray(g('L0_attn_Wo_w').T)),
        bo0=cols128(g('L0_attn_Wo_b')),
        wqkv1=tile_w(np.ascontiguousarray(wq1)),
        wnq=np.ascontiguousarray(g('L1_attn_q_norm_w')[:, None].astype(np.float32)),
        wnk=np.ascontiguousarray(g('L1_attn_k_norm_w')[:, None].astype(np.float32)),
        la_up=np.ascontiguousarray(
            g('L1_ffn_w_up_lora_a').reshape(32, 128, 16).transpose(1, 0, 2).reshape(128, 512)),
        rel5=_rel_consts(),
        wo1=tile_w(g('L1_attn_Wo').T),
        bup=np.ascontiguousarray(g('L1_ffn_b_up').reshape(NHT, 128).T.astype(np.float32)),
        bdn=cols128(g('L1_ffn_b_down')),
        lbup=np.ascontiguousarray(g('L1_ffn_w_up_lora_b').astype(np.float32)),
        ladn=np.ascontiguousarray(
            g('L1_ffn_w_down_lora_a').reshape(NHT, 128, 16).transpose(1, 0, 2)
            .reshape(128, NHT * 16).astype(np.float32)),
        lbdn=np.ascontiguousarray(g('L1_ffn_w_down_lora_b').astype(np.float32)),
    )
    in_maps = []
    for c in range(NCORES):
        xs = np.ascontiguousarray(x_full[shards[c]])
        in_maps.append(dict(
            x_tm=xs,
            xT=np.ascontiguousarray(xs.T),
            cod_u=np.ascontiguousarray(cod_u_pad[:, c * HSH_PAD:(c + 1) * HSH_PAD]),
            amx_u=np.ascontiguousarray(amx_u_pad[:, c * HSH_PAD:(c + 1) * HSH_PAD]),
            cod_d=np.ascontiguousarray(cod_d_pad[c * HSH_PAD:(c + 1) * HSH_PAD]),
            amx_d=np.ascontiguousarray(amx_d_pad[c * 22:(c + 1) * 22]),
            biast=l1b_bias_table(c),
            kidx=l1b_kidx(c),
            **common))
    return in_maps


def kernel(**inputs):
    in_maps = prepare_in_maps(inputs)
    shards = [_shard_rows(c) for c in range(NCORES)]
    nc = _prog('fused', build_fused)
    res = run_bass_kernel_spmd(nc, in_maps, core_ids=list(range(NCORES)))
    final = np.zeros((N, DM), dtype=np.float32)
    for c in range(NCORES):
        final[shards[c]] = res.results[c]["outF"].T
    return final.reshape(1, N, DM)


# version 06
# speedup vs baseline: 1.0514x; 1.0514x over previous
import math
import contextlib
import numpy as np
import ml_dtypes
import concourse.bass as bass
import concourse.mybir as mybir
from concourse.bass_utils import run_bass_kernel_spmd

F32 = mybir.dt.float32
BF16 = mybir.dt.bfloat16
I32 = mybir.dt.int32
AF = mybir.ActivationFunctionType
ALU = mybir.AluOpType
AX = mybir.AxisListType.X

NCORES = 8
N = 8192
DM = 4096
TLOC = N // NCORES
NT = TLOC // 128
NDT = DM // 128
NCH = 16

BF = ml_dtypes.bfloat16

NF4 = [-1.0, -0.6961928009986877, -0.5250730514526367, -0.39491748809814453,
       -0.28444138169288635, -0.18477343022823334, -0.09105003625154495, 0.0,
       0.07958029955625534, 0.16093020141124725, 0.24611230194568634,
       0.33791524171829224, 0.44070982933044434, 0.5626170039176941,
       0.7229568362236023, 1.0]

H1, HKV, HD = 32, 8, 128
_s0 = 2 ** (-2 ** (-(math.log2(H1) - 3)))
SLOPES = [_s0 * _s0 ** i for i in range(H1)]
WIN = [max(640, int(140.0 / s) + 1) for s in SLOPES]
WC = [int(math.ceil(w / 512.0)) + 1 for w in WIN]
B_SHIFT = 40.0

SKV_A = [min(max(WC[4 * kv + g] for g in range(4)), 8) for kv in range(HKV)]
SKV_B = [min(max(WC[4 * kv + g] for g in range(4)), 16) for kv in range(HKV)]
SMAX = 16
SH_TAB = [[min(WC[h], SKV_A[h // 4]), min(WC[h], SKV_B[h // 4])] for h in range(H1)]

H0, E0 = 8, 64
NQKV = 12
NKT = N // 128

NQKV1 = 48

HF = 11008
NHT = HF // 128
HTILES = [11] * 7 + [9]
HTOFF = [sum(HTILES[:r]) for r in range(NCORES + 1)]
HSH_PAD = 1408


def cr_of(g):
    return 2 * min(g, 15 - g) + (1 if g >= 8 else 0)


def rank_lt(ht):
    for r in range(NCORES):
        if ht < HTOFF[r + 1]:
            return r, ht - HTOFF[r]
    raise ValueError



class _Ct:
    __slots__ = ("v",)

    def __init__(self):
        self.v = 0


class Sch:
    ENGINES = ("sync", "gpsimd", "tensor", "vector", "scalar")

    def __init__(self, nc, ctx):
        self.nc = nc
        self.ctx = ctx
        self.q = {e: [] for e in self.ENGINES}
        self.sems = {}

    def sem(self, name):
        if name not in self.sems:
            h = self.ctx.enter_context(self.nc.semaphore(name))
            self.sems[name] = (h, _Ct())
        return self.sems[name]

    def val(self, name):
        return self.sem(name)[1].v

    def wait(self, eng, name, v=None):
        h, ct = self.sem(name)
        if v is None:
            v = ct.v
        if v <= 0:
            return
        self.q[eng].append(lambda e, h=h, v=v: e.wait_ge(h, v))

    def op(self, eng, build, inc=None, n=1):
        if inc is not None:
            h, ct = self.sem(inc)
            ct.v += n

            def f(e, build=build, h=h, n=n):
                build(e).then_inc(h, n)
            self.q[eng].append(f)
            return ct.v
        self.q[eng].append(build)
        return None

    def dma(self, eng, out, in_, inc=None):
        return self.op(eng, lambda e, o=out, i=in_: e.dma_start(out=o, in_=i),
                       inc=inc, n=16)

    def collective(self, in_ap, out_ap, name):
        h, ct = self.sem(name)
        ct.v = 1

        def f(e, h=h, in_ap=in_ap, out_ap=out_ap):
            e.collective_compute(
                "AllGather", mybir.AluOpType.bypass,
                replica_groups=[list(range(NCORES))],
                ins=[in_ap.opt()], outs=[out_ap.opt()]).then_inc(h)
        self.q['gpsimd'].append(f)

    def cwait(self, eng, name):
        h, _ = self.sem(name)
        self.q[eng].append(lambda e, h=h: e.wait_ge(h, 1))

    def indirect(self, out, in_, idx_ap, inc=None):
        return self.op(
            'gpsimd',
            lambda e, o=out, i=in_, x=idx_ap: e.indirect_dma_start(
                o, None, i, bass.IndirectOffsetOnAxis(ap=x, axis=0)),
            inc=inc, n=16)

    def mm_group(self, pairs, psum_ap, inc):
        nmm = len(pairs)
        v = None
        for i, (lh, rh) in enumerate(pairs):
            v = self.op('tensor',
                        lambda e, lh=lh, rh=rh, s=(i == 0), st=(i == nmm - 1):
                        e.matmul(psum_ap, lh, rh, start=s, stop=st,
                                 skip_group_check=True),
                        inc=(inc if i == nmm - 1 else None))
        return v

    def mm(self, psum_ap, lh, rh, start, stop, inc=None, always=False):
        return self.op('tensor',
                       lambda e, lh=lh, rh=rh, s=start, st=stop:
                       e.matmul(psum_ap, lh, rh, start=s, stop=st,
                                skip_group_check=True),
                       inc=(inc if (stop or always) else None))

    def barrier(self):
        snap = [(h, ct.v) for (n, (h, ct)) in self.sems.items()
                if ct.v > 0 and not n.startswith('cc')]
        for eng in self.ENGINES:
            for h, v in snap:
                self.q[eng].append(lambda e, h=h, v=v: e.wait_ge(h, v))

    def run(self, block):
        for eng in self.ENGINES:
            ops = self.q[eng]

            def body(handle, ops=ops):
                for fn in ops:
                    fn(handle)
            getattr(block, eng)(body)



def tile_w(w_t):
    K, O = w_t.shape
    nd, no = K // 128, O // 128
    return np.ascontiguousarray(
        w_t.reshape(nd, 128, no, 128).transpose(2, 1, 0, 3).reshape(no, 128, nd * 128))


def cols128(b):
    no = b.shape[0] // 128
    return np.ascontiguousarray(b.reshape(no, 128).T.astype(np.float32))


def _rel_consts():
    i = np.arange(128)[:, None]
    j = np.arange(512)[None, :]
    rel = (j - i).astype(np.float32)
    out = [rel]
    for m in range(4):
        r = rel.copy()
        r[(-m * 128 + j - i) > 0] = -1e33
        out.append(r)
    return np.ascontiguousarray(np.stack(out))


def l1b_slot_gc(core):
    out = np.full((HKV, 2, SMAX), -1, dtype=np.int64)
    for kv in range(HKV):
        for gi, ch in enumerate((core, 15 - core)):
            S_g = (SKV_A if gi == 0 else SKV_B)[kv]
            for s in range(S_g):
                gc = ch - (S_g - 1) + s
                out[kv, gi, s] = gc if gc >= 0 else -1
    return out


def l1b_bias_table(core):
    tab = np.full((H1 * 2 * SMAX * 4,), -1e30, dtype=np.float32)
    gc = l1b_slot_gc(core)
    for h in range(H1):
        kv = h // 4
        for gi, ch in enumerate((core, 15 - core)):
            S_g = (SKV_A if gi == 0 else SKV_B)[kv]
            S_h = SH_TAB[h][gi]
            for s in range(S_g - S_h, S_g):
                g = gc[kv, gi, s]
                if g < 0:
                    continue
                for j in range(4):
                    q0 = ch * 512 + j * 128
                    k0 = g * 512
                    tab[bias_col(h, gi, s, j)] = SLOPES[h] * (k0 - q0) - B_SHIFT
    return np.ascontiguousarray(
        np.broadcast_to(tab[None, :], (128, tab.shape[0])).astype(np.float32))


def bias_col(h, gi, s, j):
    return ((h * 2 + gi) * SMAX + s) * 4 + j


def l1b_kidx(core):
    gc = l1b_slot_gc(core)
    out = np.zeros((128, HKV * 2 * SMAX), dtype=np.int32)
    p = np.arange(128, dtype=np.int32)
    for kv in range(HKV):
        for gi in range(2):
            for s in range(SMAX):
                g = gc[kv, gi, s]
                base = 0 if g < 0 else cr_of(int(g)) * 1024 + kv * 128
                out[:, (kv * 2 + gi) * SMAX + s] = base + p
    return np.ascontiguousarray(out)



class Dequant:

    def __init__(self, sch, S, cod_u, amx_u, cod_d, amx_d, wupc, wdnc):
        self.sch = sch
        self.cod_i = S("dq_cod", [128, 2048], I32)
        self.codb = S("dq_codb", [128, 2048], BF16)
        self.amx = S("dq_amx", [128, 2048], F32)
        self.dacc = [S(f"dq_dacc{i}", [128, 2048], BF16) for i in range(2)]
        self.dtmp = S("dq_dtmp", [128, 2048], BF16)
        self.units = []
        for dt in range(NDT):
            self.units.append((cod_u[dt * 128:(dt + 1) * 128, :],
                               amx_u[2 * dt:2 * dt + 1, :],
                               amx_u[2 * dt + 1:2 * dt + 2, :],
                               wupc[dt * 128:(dt + 1) * 128, :], HSH_PAD))
        for ht in range(11):
            for hf in range(2):
                c0 = hf * 2048
                self.units.append((cod_d[ht * 128:(ht + 1) * 128, c0:c0 + 2048],
                                   amx_d[2 * ht:2 * ht + 1, c0:c0 + 2048],
                                   amx_d[2 * ht + 1:2 * ht + 2, c0:c0 + 2048],
                                   wdnc[ht * 128:(ht + 1) * 128, c0:c0 + 2048],
                                   2048))
        self.i = 0
        self.cv_prev = 0
        self.st = {0: 0, 1: 0}
        self.n_emitted_st = 0

    def emit(self, n):
        sch = self.sch
        for _ in range(n):
            if self.i >= len(self.units):
                return
            i = self.i
            self.i += 1
            src, am0, am1, dst, w = self.units[i]
            b = i % 2
            if self.cv_prev:
                sch.wait('gpsimd', 've', self.cv_prev)
            ld = sch.dma('gpsimd', self.cod_i[:, 0:w], src, inc='dqld')
            ld = sch.dma('gpsimd', self.amx[0:64, 0:w],
                         am0.broadcast_to([64, w]), inc='dqld')
            ld = sch.dma('gpsimd', self.amx[64:128, 0:w],
                         am1.broadcast_to([64, w]), inc='dqld')
            sch.wait('vector', 'dqld', ld)
            v = sch.op('vector', lambda e, w=w: e.tensor_copy(
                self.codb[:, 0:w], self.cod_i[:, 0:w]), inc='ve')
            if self.st[b]:
                sch.wait('vector', 'dqst%d' % b, self.st[b])
            sch.wait('vector', 've', v)
            v = sch.op('vector', lambda e, b=b, w=w: e.tensor_scalar(
                self.dacc[b][:, 0:w], self.codb[:, 0:w], 0.0, NF4[0],
                ALU.is_equal, ALU.mult), inc='ve')
            for vi in range(1, 16):
                v2 = sch.op('vector', lambda e, vi=vi, w=w: e.tensor_scalar(
                    self.dtmp[:, 0:w], self.codb[:, 0:w], float(vi), NF4[vi],
                    ALU.is_equal, ALU.mult), inc='ve')
                sch.wait('vector', 've', v2)
                v = sch.op('vector', lambda e, b=b, w=w: e.tensor_add(
                    self.dacc[b][:, 0:w], self.dacc[b][:, 0:w],
                    self.dtmp[:, 0:w]), inc='ve')
                sch.wait('vector', 've', v)
            v = sch.op('vector', lambda e, b=b, w=w: e.tensor_mul(
                self.dacc[b][:, 0:w], self.dacc[b][:, 0:w],
                self.amx[:, 0:w]), inc='ve')
            self.cv_prev = v
            sch.wait('sync', 've', v)
            self.st[b] = sch.dma('sync', dst, self.dacc[b][:, 0:w],
                                 inc='dqst%d' % b)



def emit_ln_tile(sch, x_sb, xn_sb, scr, stt, eps=1e-5):
    v = sch.op('vector', lambda e: e.reduce_sum(stt[:, 0:1], x_sb[:], axis=AX), inc='ve')
    a = sch.op('scalar', lambda e: e.activation(scr[:], x_sb[:], AF.Square,
                                                accum_out=stt[:, 1:2]), inc='ac')
    sch.wait('vector', 'ac', a)
    sch.wait('vector', 've', v)
    v = sch.op('vector', lambda e: e.tensor_scalar_mul(stt[:, 2:3], stt[:, 0:1], 1.0 / DM), inc='ve')
    sch.wait('vector', 've', v)
    v = sch.op('vector', lambda e: e.tensor_mul(stt[:, 4:5], stt[:, 2:3], stt[:, 2:3]), inc='ve')
    v = sch.op('vector', lambda e: e.tensor_scalar(stt[:, 3:4], stt[:, 1:2], 1.0 / DM,
                                                   None, ALU.mult), inc='ve')
    sch.wait('vector', 've', v)
    v = sch.op('vector', lambda e: e.tensor_sub(stt[:, 5:6], stt[:, 3:4], stt[:, 4:5]), inc='ve')
    sch.wait('vector', 've', v)
    v = sch.op('vector', lambda e: e.tensor_scalar_add(stt[:, 5:6], stt[:, 5:6], 1e-5), inc='ve')
    sch.wait('vector', 've', v)
    v = sch.op('vector', lambda e: e.reciprocal(stt[:, 6:7], stt[:, 5:6]), inc='ve')
    sch.wait('scalar', 've', v)
    a = sch.op('scalar', lambda e: e.activation(stt[:, 7:8], stt[:, 6:7], AF.Sqrt), inc='ac')
    sch.wait('vector', 'ac', a)
    v = sch.op('vector', lambda e: e.tensor_scalar(xn_sb[:], x_sb[:], stt[:, 2:3],
                                                   stt[:, 7:8], ALU.subtract, ALU.mult), inc='ve')
    return v



def build_fused(debug=False):
    nc = bass.Bass("TRN2", target_bir_lowering=False)
    T = lambda n, s, d, k: nc.dram_tensor(n, s, d, kind=k).ap()
    EI, EO = "ExternalInput", "ExternalOutput"
    x_tm = T("x_tm", [TLOC, DM], F32, EI)
    xT = T("xT", [DM, TLOC], F32, EI)
    wqkv0 = T("wqkv0", [NQKV, 128, DM], F32, EI)
    bqkv = T("bqkv", [128, NQKV], F32, EI)
    wrout = T("wrout", [128, NDT * 8], F32, EI)
    wa = T("wa", [1, 128, DM], F32, EI)
    wbase = T("wbase", [NDT, 128, DM], F32, EI)
    bbase = T("bbase", [128, NDT], F32, EI)
    bstk = T("bstk", [NDT, 128, 128], F32, EI)
    sel8 = T("sel8", [8, 128], F32, EI)
    idf = T("idf", [128, 128], F32, EI)
    idb = T("idb", [128, 128], BF16, EI)
    ones1 = T("ones1", [1, 128], F32, EI)
    ones128 = T("ones128", [128, 1], F32, EI)
    ones128b = T("ones128b", [128, 1], BF16, EI)
    wo0 = T("wo0", [NDT, 128, 512], F32, EI)
    bo0 = T("bo0", [128, NDT], F32, EI)
    wqkv1 = T("wqkv1", [NQKV1, 128, DM], F32, EI)
    wnq = T("wnq", [128, 1], F32, EI)
    wnk = T("wnk", [128, 1], F32, EI)
    la_up = T("la_up", [128, NDT * 16], F32, EI)
    rel5 = T("rel5", [5, 128, 512], F32, EI)
    biast = T("biast", [128, H1 * 2 * SMAX * 4], F32, EI)
    kidx = T("kidx", [128, HKV * 2 * SMAX], I32, EI)
    cod_u = T("cod_u", [DM, HSH_PAD], I32, EI)
    amx_u = T("amx_u", [64, HSH_PAD], F32, EI)
    cod_d = T("cod_d", [HSH_PAD, DM], I32, EI)
    amx_d = T("amx_d", [22, DM], F32, EI)
    wo1 = T("wo1", [NDT, 128, DM], F32, EI)
    bup = T("bup", [128, NHT], F32, EI)
    bdn = T("bdn", [128, NDT], F32, EI)
    lbup = T("lbup", [16, HF], F32, EI)
    ladn = T("ladn", [128, NHT * 16], F32, EI)
    lbdn = T("lbdn", [16, DM], F32, EI)
    IN = "Internal"
    DK = EO if debug else IN
    k0c = T("k0c", [1024, 512], BF16, IN)
    v0c = T("v0c", [1024, 512], BF16, IN)
    KG0 = T("KG0", [8192, 512], BF16, IN)
    VG0 = T("VG0", [8192, 512], BF16, IN)
    g0_scr = T("g0_scr", [DM, TLOC], F32, DK)
    x1_scr = T("x1_scr", [DM, TLOC], F32, DK)
    xn1_scr = T("xn1_scr", [DM, TLOC], BF16, DK)
    q1_scr = T("q1_scr", [DM, TLOC], BF16, DK)
    k1c = T("k1c", [2048, 512], BF16, IN)
    v1c = T("v1c", [2048, 512], BF16, IN)
    KG1 = T("KG1", [16384, 512], BF16, IN)
    VG1 = T("VG1", [16384, 512], BF16, IN)
    wupc = T("wupc", [DM, HSH_PAD], BF16, IN)
    wdnc = T("wdnc", [HSH_PAD, DM], BF16, IN)
    if debug:
        k1d = T("k1d", [2048, 512], BF16, EO)
        v1d = T("v1d", [2048, 512], BF16, EO)
        wud = T("wud", [DM, HSH_PAD], BF16, EO)
        wdd = T("wdd", [HSH_PAD, DM], BF16, EO)
        dbg0 = T("dbg0", [128, 8192], F32, EO)
        x2d = T("x2d", [NHT * 128, TLOC], BF16, EO)
        wugd = T("wugd", [NCORES * DM, HSH_PAD], BF16, EO)
        wdgd = T("wdgd", [NCORES * HSH_PAD, DM], BF16, EO)
        u1d = T("u1d", [16, TLOC], BF16, EO)
        u2d = T("u2d", [16, TLOC], BF16, EO)
    WUPG = T("WUPG", [NCORES * DM, HSH_PAD], BF16, IN)
    WDNG = T("WDNG", [NCORES * HSH_PAD, DM], BF16, IN)
    o_scr = T("o_scr", [DM, TLOC], BF16, DK)
    g1_scr = T("g1_scr", [DM, TLOC], BF16, DK)
    outF = T("outF", [DM, TLOC], F32, EO)

    with contextlib.ExitStack() as ctx:
        sch = Sch(nc, ctx)
        S0 = lambda name, shape, dt=F32: ctx.enter_context(nc.sbuf_tensor(name, shape, dt))
        idb_s = S0("idb_s", [128, 128], BF16)
        ones1_s = S0("ones1_s", [1, 128], F32)
        ones128_s = S0("ones128_s", [128, 1], F32)
        ones128b_s = S0("ones128b_s", [128, 1], BF16)
        q0_sb = S0("q0_sb", [128, 4, TLOC], BF16)
        u1s = S0("u1s", [16, TLOC], BF16)
        s01 = contextlib.ExitStack()
        xn_res = s01.enter_context(nc.sbuf_tensor("xn0_res", [128, NDT, TLOC], BF16))

        ld0 = sch.dma('gpsimd', idb_s[:], idb[:], inc='ld')
        ld0 = sch.dma('gpsimd', ones1_s[:], ones1[:], inc='ld')
        ld0 = sch.dma('gpsimd', ones128_s[:], ones128[:], inc='ld')
        ld0 = sch.dma('gpsimd', ones128b_s[:], ones128b[:], inc='ld')
        LD_CONST = ld0

        block = ctx.enter_context(nc.Block())

        with contextlib.ExitStack() as pctx:
            S = lambda name, shape, dt=F32: pctx.enter_context(nc.sbuf_tensor(name, shape, dt))
            xsb = [S(f"ln_x{i}", [128, DM], F32) for i in range(2)]
            xnb = [S(f"ln_xn{i}", [128, DM], BF16) for i in range(2)]
            scr = S("ln_scr", [128, DM], F32)
            stats = S("ln_st", [128, 8 * 2], F32)
            pstb = [pctx.enter_context(nc.psum_tensor(f"ln_pstb{i}", [128, 1024], BF16))
                    for i in range(2)]
            sch.wait('tensor', 'ld', LD_CONST)
            fin = [None] * NT
            for t in range(NT):
                b = t % 2
                if t >= 2:
                    vv, av = fin[t - 2]
                    sch.wait('gpsimd', 've', vv)
                    sch.wait('gpsimd', 'ac', av)
                ld = sch.dma('gpsimd', xsb[b][:], x_tm[t * 128:(t + 1) * 128, :], inc='lnl%d' % b)
                sch.wait('vector', 'lnl%d' % b, ld)
                sch.wait('scalar', 'lnl%d' % b, ld)
                lnv = emit_ln_tile(sch, xsb[b], xnb[b], scr, stats[:, 8 * b:8 * (b + 1)])
                sch.wait('tensor', 've', lnv)
                for d in range(NDT):
                    par = (d // 8) % 2
                    if d >= 16:
                        sch.wait('tensor', 've', cp8[d - 16])
                    tv = sch.op('tensor', lambda e, d=d, pb=pstb[par], b=b: e.transpose(
                        pb[:, (d % 8) * 128:(d % 8) * 128 + 128],
                        xnb[b][:, d * 128:(d + 1) * 128], idb_s[:]), inc='pe')
                    if d % 8 == 7:
                        sch.wait('vector', 'pe', tv)
                        cv = sch.op('vector', lambda e, d=d, pb=pstb[par], t=t: e.tensor_copy(
                            xn_res[:, d - 7:d + 1, t * 128:(t + 1) * 128],
                            pb[:, 0:1024].rearrange("p (a b) -> p a b", a=8)),
                            inc='ve')
                        if d == 7:
                            cp8 = [0] * NDT
                        for z in range(d - 7, d + 1):
                            cp8[z] = cv
                fin[t] = (cp8[NDT - 1], sch.val('ac'))
            LN0_VE = sch.val('ve')
        sch.barrier()

        with contextlib.ExitStack() as pctx:
            S = lambda name, shape, dt=F32: pctx.enter_context(nc.sbuf_tensor(name, shape, dt))
            wsl = [S(f"a_wsl{i}", [128, DM], BF16) for i in range(2)]
            wrs = S("a_wrs", [128, NDT * 8], BF16)
            idf_s = S("a_idf", [128, 128], F32)
            sel_s = S("a_sel", [8, 128], BF16)
            bq_s = S("a_bq", [128, NQKV], F32)
            bb_s = S("a_bb", [128, NDT], F32)
            bst_s = S("a_bst", [128, NDT, 128], BF16)
            lg = S("a_lg", [8, TLOC], F32)
            lgt = S("a_lgt", [128, NT, 8], F32)
            cstat = S("a_cst", [128, NT, 8], F32)
            cmsk = S("a_cmsk", [128, 8], F32)
            cl2 = S("a_cl2", [128, 8], F32)
            cel = S("a_cel", [128, 8], F32)
            ctm = S("a_ctm", [128, NT, 8], BF16)
            cfm = S("a_cfm", [8, TLOC], BF16)
            crep = S("a_crep", [128, TLOC], F32)
            hw = S("a_hw", [128, TLOC], BF16)
            ev = [S(f"a_ev{i}", [128, TLOC], BF16) for i in range(2)]
            gv = [S(f"a_gv{i}", [128, TLOC], F32) for i in range(2)]
            ps = [pctx.enter_context(nc.psum_tensor(f"a_ps{i}", [128, 512], F32))
                  for i in range(2)]
            psr = pctx.enter_context(nc.psum_tensor("a_psr", [8, 512], F32))
            pstf = pctx.enter_context(nc.psum_tensor("a_pstf", [128, 512], F32))
            pstb2 = pctx.enter_context(nc.psum_tensor("a_pstb", [128, 1024], BF16))

            ld = sch.dma('gpsimd', wrs[:], wrout[:], inc='ld')
            ld = sch.dma('gpsimd', idf_s[:], idf[:], inc='ld')
            ld = sch.dma('gpsimd', sel_s[:], sel8[:], inc='ld')
            ld = sch.dma('gpsimd', bq_s[:], bqkv[:], inc='ld')
            ld = sch.dma('gpsimd', bb_s[:], bbase[:], inc='ld')
            ld = sch.dma('gpsimd', bst_s[:], bstk.transpose([1, 0, 2]), inc='ld')
            LD0 = ld

            sch.wait('tensor', 'ld', LD0)
            sch.wait('tensor', 've', LN0_VE)
            for c in range(2):
                if c == 1:
                    sch.wait('tensor', 've', vv)
                pairs = [(wrs[:, d * 8:(d + 1) * 8], xn_res[:, d, c * 512:(c + 1) * 512])
                         for d in range(NDT)]
                pv = sch.mm_group(pairs, psr[:], inc='pe')
                sch.wait('vector', 'pe', pv)
                vv = sch.op('vector', lambda e, c=c, lg=lg: e.tensor_copy(
                    lg[:, c * 512:(c + 1) * 512], psr[:]), inc='ve')
            sch.wait('tensor', 've', vv)
            for t in range(NT):
                tv = sch.op('tensor', lambda e, t=t, lg=lg: e.transpose(
                    pstf[:, 0:8], lg[0:8, t * 128:(t + 1) * 128], idf_s[0:8, 0:8]), inc='pe')
                sch.wait('vector', 'pe', tv)
                vv = sch.op('vector', lambda e, t=t: e.tensor_copy(lgt[:, t, :], pstf[:, 0:8]), inc='ve')
                sch.wait('tensor', 've', vv)
            for t in range(NT):
                st = cstat[:, t, :]
                l_ = lgt[:, t, :]
                v = sch.op('vector', lambda e, l_=l_, st=st: e.reduce_max(st[:, 0:1], l_, axis=AX), inc='ve')
                sch.wait('vector', 've', v)
                v = sch.op('vector', lambda e, l_=l_, st=st: e.tensor_scalar(
                    cmsk[:], l_, st[:, 0:1], -1e30, ALU.is_equal, ALU.mult), inc='ve')
                sch.wait('vector', 've', v)
                v = sch.op('vector', lambda e, l_=l_: e.tensor_add(cl2[:], l_, cmsk[:]), inc='ve')
                sch.wait('vector', 've', v)
                v = sch.op('vector', lambda e, st=st: e.reduce_max(st[:, 1:2], cl2[:], axis=AX), inc='ve')
                sch.wait('vector', 've', v)
                v = sch.op('vector', lambda e, st=st: e.tensor_scalar_mul(st[:, 2:3], st[:, 0:1], -1.0), inc='ve')
                sch.wait('scalar', 've', v)
                a = sch.op('scalar', lambda e, l_=l_, st=st: e.activation(
                    cel[:], l_, AF.Exp, bias=st[:, 2:3]), inc='ac')
                sch.wait('vector', 'ac', a)
                v = sch.op('vector', lambda e, st=st: e.tensor_sub(st[:, 3:4], st[:, 1:2], st[:, 0:1]), inc='ve')
                sch.wait('scalar', 've', v)
                a = sch.op('scalar', lambda e, st=st: e.activation(st[:, 4:5], st[:, 3:4], AF.Exp), inc='ac')
                sch.wait('vector', 'ac', a)
                v = sch.op('vector', lambda e, st=st: e.tensor_scalar_add(st[:, 5:6], st[:, 4:5], 1.0), inc='ve')
                sch.wait('vector', 've', v)
                v = sch.op('vector', lambda e, st=st: e.reciprocal(st[:, 6:7], st[:, 5:6]), inc='ve')
                sch.wait('vector', 've', v)
                v = sch.op('vector', lambda e, st=st: e.tensor_scalar_mul(st[:, 7:8], st[:, 6:7], 2.0), inc='ve')
                v = sch.op('vector', lambda e, l_=l_, st=st: e.tensor_scalar(
                    cmsk[:], l_, st[:, 1:2], None, ALU.is_ge), inc='ve')
                sch.wait('vector', 've', v)
                v = sch.op('vector', lambda e: e.tensor_mul(cl2[:], cel[:], cmsk[:]), inc='ve')
                sch.wait('vector', 've', v)
                v = sch.op('vector', lambda e, t=t, st=st: e.tensor_scalar(
                    ctm[:, t, :], cl2[:], st[:, 7:8], None, ALU.mult), inc='ve')
                sch.wait('tensor', 've', v)
                tv = sch.op('tensor', lambda e, t=t: e.transpose(
                    pstb2[0:8, 0:128], ctm[:, t, :], idb_s[:]), inc='pe')
                sch.wait('vector', 'pe', tv)
                v = sch.op('vector', lambda e, t=t: e.tensor_copy(
                    cfm[0:8, t * 128:(t + 1) * 128], pstb2[0:8, 0:128]), inc='ve')
                sch.wait('tensor', 've', v)
            for c in range(2):
                pv = sch.mm_group([(sel_s[:], cfm[0:8, c * 512:(c + 1) * 512])], ps[c][:], inc='pe')
                sch.wait('vector', 'pe', pv)
                vv = sch.op('vector', lambda e, c=c: e.tensor_copy(
                    crep[:, c * 512:(c + 1) * 512], ps[c][:]), inc='ve')
            nld = {0: 0, 1: 0}
            nld[0] += 1
            wa_v = sch.dma('gpsimd', wsl[0][:], wa[0, :, :], inc='alw0')
            sch.wait('tensor', 'alw0', wa_v)
            sch.wait('tensor', 've', vv)
            for c in range(2):
                pairs = [(wsl[0][:, d * 128:(d + 1) * 128], xn_res[:, d, c * 512:(c + 1) * 512])
                         for d in range(NDT)]
                pv = sch.mm_group(pairs, ps[c][:], inc='pe')
                sch.wait('vector', 'pe', pv)
                vv = sch.op('vector', lambda e, c=c: e.tensor_mul(
                    hw[:, c * 512:(c + 1) * 512], ps[c][:], crep[:, c * 512:(c + 1) * 512]), inc='ve')
            HW_DONE = vv

            stk = {0: 0, 1: 0}
            for o in range(NQKV):
                b = o % 2
                if o >= 2:
                    sch.wait('gpsimd', 'aev', 2 * (o - 1))
                else:
                    sch.wait('gpsimd', 'pe', sch.val('pe'))
                nld[b] += 1
                sch.dma('gpsimd', wsl[b][:], wqkv0[o, :, :], inc='alw%d' % b)
                sch.wait('tensor', 'alw%d' % b, 16 * nld[b])
                if o == 0:
                    sch.wait('tensor', 've', HW_DONE)
                for c in range(2):
                    u = 2 * o + c
                    pairs = [(wsl[b][:, d * 128:(d + 1) * 128],
                              xn_res[:, d, c * 512:(c + 1) * 512]) for d in range(NDT)]
                    if u >= 2:
                        sch.wait('tensor', 'aev', u - 1)
                    pv = sch.mm_group(pairs, ps[c][:], inc='pe')
                    sch.wait('vector', 'pe', pv)
                    if o < 4:
                        v = sch.op('vector', lambda e, o=o, c=c: e.tensor_scalar(
                            q0_sb[:, o, c * 512:(c + 1) * 512], ps[c][:],
                            bq_s[:, o:o + 1], None, ALU.add), inc='aev')
                    else:
                        b2 = o % 2
                        if c == 0:
                            sch.wait('vector', 'astk%d' % b2, stk[b2])
                        v = sch.op('vector', lambda e, o=o, c=c, b2=b2: e.tensor_scalar(
                            ev[b2][:, c * 512:(c + 1) * 512], ps[c][:],
                            bq_s[:, o:o + 1], None, ALU.add), inc='aev')
                        if c == 1:
                            dstT = k0c if o < 8 else v0c
                            fo = (o - 4) % 4
                            sch.wait('sync', 'aev', v)
                            sv = sch.dma('sync', dstT[fo * 128:fo * 128 + 128, :],
                                         ev[b2][:, 0:512], inc='astk%d' % b2)
                            sv = sch.dma('sync', dstT[512 + fo * 128:512 + fo * 128 + 128, :],
                                         ev[b2][:, 512:1024], inc='astk%d' % b2)
                            stk[b2] = sv
                if o == 7:
                    sch.wait('gpsimd', 'astk0', sch.val('astk0'))
                    sch.wait('gpsimd', 'astk1', sch.val('astk1'))
                    sch.collective(k0c, KG0, 'ccK0')
                if o == 11:
                    sch.wait('gpsimd', 'astk0', sch.val('astk0'))
                    sch.wait('gpsimd', 'astk1', sch.val('astk1'))
                    sch.collective(v0c, VG0, 'ccV0')

            stg = {0: 0, 1: 0}
            for o in range(NDT):
                b = o % 2
                sch.wait('gpsimd', 'aev', 24 + 2 * max(o - 1, 0))
                nld[b] += 1
                sch.dma('gpsimd', wsl[b][:], wbase[o, :, :], inc='alw%d' % b)
                sch.wait('tensor', 'alw%d' % b, 16 * nld[b])
                for c in range(2):
                    u = 24 + 2 * o + c
                    pairs = [(wsl[b][:, d * 128:(d + 1) * 128],
                              xn_res[:, d, c * 512:(c + 1) * 512]) for d in range(NDT)]
                    pairs.append((bst_s[:, o, :], hw[:, c * 512:(c + 1) * 512]))
                    sch.wait('tensor', 'aev', u - 1)
                    pv = sch.mm_group(pairs, ps[c][:], inc='pe')
                    sch.wait('scalar', 'pe', pv)
                    b2 = o % 2
                    if c == 0:
                        sch.wait('scalar', 'astg%d' % b2, stg[b2])
                    v = sch.op('scalar', lambda e, o=o, c=c, b2=b2: e.activation(
                        gv[b2][:, c * 512:(c + 1) * 512], ps[c][:], AF.Silu,
                        bias=bb_s[:, o:o + 1]), inc='aev')
                    if c == 1:
                        sch.wait('sync', 'aev', v)
                        stg[b2] = sch.dma('sync', g0_scr[o * 128:(o + 1) * 128, :],
                                          gv[b2][:], inc='astg%d' % b2)
            sch.wait('sync', 'astg0', sch.val('astg0'))
            sch.wait('sync', 'astg1', sch.val('astg1'))
            if debug:
                sch.wait('gpsimd', 'aev', sch.val('aev'))
                sch.wait('gpsimd', 've', sch.val('ve'))
                dv = sch.dma('gpsimd', dbg0[0:128, 0:1024], crep[:], inc='dbg0')
                dv = sch.dma('gpsimd', dbg0[0:8, 1024:2048], lg[:], inc='dbg0')
                dv = sch.dma('gpsimd', dbg0[0:8, 2048:3072], cfm[:], inc='dbg0')
                dv = sch.dma('gpsimd',
                             dbg0[0:128, 3072:7168].rearrange("p (d t) -> p d t", d=32),
                             xn_res[:, :, 896:1024], inc='dbg0')
                dv = sch.dma('gpsimd', dbg0[0:128, 7168:8192], hw[:], inc='dbg0')
                sch.wait('gpsimd', 'dbg0', dv)
            L0A_PE = sch.val('pe')

        s01.close()
        sch.barrier()

        o_sb0 = ctx.enter_context(nc.sbuf_tensor("o_sb0", [128, 4, TLOC], BF16))
        with contextlib.ExitStack() as pctx:
            S = lambda name, shape, dt: pctx.enter_context(nc.sbuf_tensor(name, shape, dt))
            kp = S("b_kp", [128, N], BF16)
            vf = S("b_vf", [128, N], BF16)
            vp = S("b_vp", [128, NKT, 128], BF16)
            att = S("b_att", [128, N], BF16)
            attn = S("b_attn", [128, N], BF16)
            panels = S("b_pan", [128, NKT, 256], BF16)
            mcols = S("b_mc", [128, NCH + 8], F32)
            zcols = S("b_zc", [128, NCH + 8], F32)
            dq = Dequant(sch, lambda n, s, d: pctx.enter_context(nc.sbuf_tensor(n, s, d)),
                         cod_u, amx_u, cod_d, amx_d, wupc, wdnc)
            pss = [pctx.enter_context(nc.psum_tensor(f"b_pss{i}", [128, 512], F32))
                   for i in range(2)]
            pstb = [pctx.enter_context(nc.psum_tensor(f"b_pstb{i}", [128, 1024], BF16))
                    for i in range(2)]
            pso = pctx.enter_context(nc.psum_tensor("b_pso", [64, 256], F32))

            kv_users_ac = 0
            prev_exp = 0
            sch.cwait('gpsimd', 'ccK0')
            sch.cwait('gpsimd', 'ccV0')
            nsh = sch.op('vector', lambda e: e.tensor_scalar_mul(
                mcols[:, NCH + 1:NCH + 2], ones128_s[:], -50.0), inc='ve')
            sch.wait('scalar', 've', nsh)
            for p in range(4):
                if p > 0:
                    sch.wait('gpsimd', 'ac', kv_users_ac)
                    sch.wait('gpsimd', 'pe', sch.val('pe'))
                for cr in range(NCH):
                    lk = sch.dma('gpsimd', kp[:, cr * 512:(cr + 1) * 512],
                                 KG0[cr * 512 + p * 128:cr * 512 + (p + 1) * 128, :],
                                 inc='bldk')
                for cr in range(NCH):
                    lk = sch.dma('gpsimd', vf[:, cr * 512:(cr + 1) * 512],
                                 VG0[cr * 512 + p * 128:cr * 512 + (p + 1) * 128, :],
                                 inc='bldk')
                sch.wait('tensor', 'bldk', lk)
                for kt in range(NKT):
                    par = (kt // 8) % 2
                    if kt >= 16:
                        sch.wait('tensor', 've', vcp8[kt - 16])
                    elif p > 0 and kt < 16:
                        sch.wait('tensor', 've', prev_pan_last)
                    tv = sch.op('tensor', lambda e, kt=kt, pb=pstb[par]: e.transpose(
                        pb[:, (kt % 8) * 128:(kt % 8) * 128 + 128],
                        vf[:, kt * 128:(kt + 1) * 128], idb_s[:]), inc='pe')
                    if kt % 8 == 7:
                        sch.wait('vector', 'pe', tv)
                        cv = sch.op('vector', lambda e, kt=kt, pb=pstb[par]: e.tensor_copy(
                            vp[:, kt - 7:kt + 1, :],
                            pb[:, 0:1024].rearrange("p (a b) -> p a b", a=8)),
                            inc='ve')
                        if kt == 7:
                            vcp8 = [0] * NKT
                        for z in range(kt - 7, kt + 1):
                            vcp8[z] = cv
                VP_VE = vcp8[NKT - 1]
                for hh in range(2):
                    qh_ = q0_sb[hh * 64:(hh + 1) * 64, :, :]
                    kh = kp[hh * 64:(hh + 1) * 64, :]
                    for qtg in range(4):
                        pan_v = None
                        for qt in range(2):
                            tqi = qtg * 2 + qt
                            lhs = qh_[:, p, tqi * 128:(tqi + 1) * 128]
                            exp_v = [0] * NCH
                            for c in range(NCH):
                                b = c % 2
                                if c >= 2:
                                    sch.wait('tensor', 'ac', exp_v[c - 2])
                                else:
                                    sch.wait('tensor', 'ac', prev_exp)
                                pv = sch.mm_group([(lhs, kh[:, c * 512:(c + 1) * 512])],
                                                  pss[b][:], inc='pe')
                                sch.wait('scalar', 'pe', pv)
                                exp_v[c] = sch.op('scalar', lambda e, c=c, pb=pss[b], att=att,
                                                  mcols=mcols: e.activation(
                                    att[:, c * 512:(c + 1) * 512], pb[:], AF.Exp,
                                    bias=mcols[:, NCH + 1:NCH + 2],
                                    accum_out=zcols[:, c:c + 1]), inc='ac')
                            prev_exp = exp_v[NCH - 1]
                            sch.wait('vector', 'ac', exp_v[NCH - 1])
                            v = sch.op('vector', lambda e: e.reduce_sum(
                                zcols[:, NCH:NCH + 1], zcols[:, 0:NCH], axis=AX), inc='ve')
                            sch.wait('vector', 've', v)
                            v = sch.op('vector', lambda e: e.reciprocal(
                                zcols[:, NCH + 1:NCH + 2], zcols[:, NCH:NCH + 1]), inc='ve')
                            sch.wait('vector', 've', v)
                            v = sch.op('vector', lambda e: e.tensor_scalar_mul(
                                zcols[:, NCH + 2:NCH + 3], zcols[:, NCH + 1:NCH + 2],
                                1.0 / math.sqrt(512.0)), inc='ve')
                            sch.wait('vector', 've', v)
                            normv = sch.op('vector', lambda e, att=att, attn=attn: e.tensor_scalar(
                                attn[:], att[:], zcols[:, NCH + 2:NCH + 3], None, ALU.mult), inc='ve')
                            sch.wait('tensor', 've', normv)
                            if hh == 0 and qtg == 0 and qt == 0:
                                sch.wait('tensor', 've', VP_VE)
                            for kt in range(NKT):
                                half = (kt // 4) % 2
                                if kt >= 8:
                                    sch.wait('tensor', 've', pan_cp[kt - 8])
                                tv = sch.op('tensor', lambda e, kt=kt, pb=pstb[half], attn=attn: e.transpose(
                                    pb[:, (kt % 4) * 128:(kt % 4) * 128 + 128],
                                    attn[:, kt * 128:(kt + 1) * 128], idb_s[:]), inc='pe')
                                if kt % 4 == 3:
                                    sch.wait('vector', 'pe', tv)
                                    pv4 = sch.op('vector', lambda e, kt=kt, pb=pstb[half], qt=qt,
                                                 panels=panels: e.tensor_copy(
                                        panels[:, kt - 3:kt + 1, qt * 128:(qt + 1) * 128],
                                        pb[:, 0:512]
                                        .rearrange("p (a b) -> p a b", a=4)), inc='ve')
                                    if kt == 3:
                                        pan_cp = [0] * NKT
                                    for z_ in range(kt - 3, kt + 1):
                                        pan_cp[z_] = pv4
                            pan_v = pan_cp[NKT - 1]
                        prev_pan_last = pan_v
                        sch.wait('tensor', 've', pan_v)
                        pairs = [(vp[:, kt, hh * 64:(hh + 1) * 64], panels[:, kt, :])
                                 for kt in range(NKT)]
                        pvv = sch.mm_group(pairs, pso[:], inc='pe')
                        sch.wait('vector', 'pe', pvv)
                        ov = sch.op('vector', lambda e, hh=hh, p=p, qtg=qtg, pso=pso: e.tensor_copy(
                            o_sb0[hh * 64:(hh + 1) * 64, p, qtg * 256:(qtg + 1) * 256],
                            pso[:]), inc='ve')
                        sch.wait('tensor', 've', ov)
                        kv_users_ac = sch.val('ac')
                    dq.emit(4)

        sch.barrier()

        s23 = contextlib.ExitStack()
        xn1_used_as = s23.enter_context(
            nc.sbuf_tensor("xn1_res", [128, NDT, TLOC], BF16))
        with contextlib.ExitStack() as pctx:
            S = lambda name, shape, dt=F32: pctx.enter_context(nc.sbuf_tensor(name, shape, dt))
            wo_sb = S("c_wo", [128, NDT, 512], BF16)
            bo_s = S("c_bo", [128, NDT], F32)
            x1hold = S("c_x1", [128, NDT, 512], F32)
            xTt = [S(f"c_xt{i}", [128, 512], F32) for i in range(2)]
            g0t = [S(f"c_g{i}", [128, 512], F32) for i in range(2)]
            fb1 = [S(f"c_f1{i}", [128, 512], F32) for i in range(2)]
            sqb = [S(f"c_sq{i}", [128, 512], F32) for i in range(2)]
            pscb = [pctx.enter_context(nc.psum_tensor(f"c_ps{i}", [128, 512], F32))
                    for i in range(2)]
            psm = [pctx.enter_context(nc.psum_tensor(f"c_psm{i}", [1, 512], F32))
                   for i in range(2)]
            psq = [pctx.enter_context(nc.psum_tensor(f"c_psq{i}", [1, 512], F32))
                   for i in range(2)]
            psmb = pctx.enter_context(nc.psum_tensor("c_psmb", [128, 512], F32))
            psrb = pctx.enter_context(nc.psum_tensor("c_psrb", [128, 512], F32))
            rowv = S("c_rowv", [1, 512 * 3], F32)

            lw = sch.dma('gpsimd', wo_sb[:], wo0.transpose([1, 0, 2]), inc='clw')
            lw = sch.dma('gpsimd', bo_s[:], bo0[:], inc='clw')
            sch.wait('tensor', 'clw', lw)
            cmb_v = {0: 0, 1: 0}
            sq_a = {0: 0, 1: 0}
            qs_pe = {0: 0, 1: 0}
            MQ = 0
            for ch in range(2):
                if ch == 1:
                    sch.wait('vector', 'pe', MQ)
                for o in range(NDT):
                    b = o % 2
                    u = ch * NDT + o
                    if u >= 2:
                        sch.wait('gpsimd', 've', cmb_v[b])
                    lx = sch.dma('gpsimd', xTt[b][:],
                                 xT[o * 128:(o + 1) * 128, ch * 512:(ch + 1) * 512],
                                 inc='cldx%d' % b)
                    lx = sch.dma('gpsimd', g0t[b][:],
                                 g0_scr[o * 128:(o + 1) * 128, ch * 512:(ch + 1) * 512],
                                 inc='cldx%d' % b)
                    if u >= 2:
                        sch.wait('tensor', 've', cmb_v[b])
                        sch.wait('tensor', 'ac', sq_a[b])
                    pairs = [(wo_sb[:, o, j * 128:(j + 1) * 128],
                              o_sb0[:, j, ch * 512:(ch + 1) * 512]) for j in range(4)]
                    pv = sch.mm_group(pairs, pscb[b][:], inc='pe')
                    sch.wait('vector', 'pe', pv)
                    sch.wait('vector', 'cldx%d' % b, lx)
                    v = sch.op('vector', lambda e, b=b, o=o: e.tensor_scalar(
                        fb1[b][:], pscb[b][:], bo_s[:, o:o + 1], None, ALU.add), inc='ve')
                    sch.wait('vector', 've', v)
                    v = sch.op('vector', lambda e, b=b: e.tensor_mul(
                        fb1[b][:], fb1[b][:], g0t[b][:]), inc='ve')
                    sch.wait('vector', 've', v)
                    v = sch.op('vector', lambda e, b=b, o=o: e.tensor_add(
                        x1hold[:, o, :], fb1[b][:], xTt[b][:]), inc='ve')
                    cmb_v[b] = v
                    sch.wait('sync', 've', v)
                    sch.dma('sync', x1_scr[o * 128:(o + 1) * 128, ch * 512:(ch + 1) * 512],
                            x1hold[:, o, :], inc='cstx')
                    sch.wait('scalar', 've', v)
                    if qs_pe[b]:
                        sch.wait('scalar', 'pe', qs_pe[b])
                    a = sch.op('scalar', lambda e, b=b, o=o: e.activation(
                        sqb[b][:], x1hold[:, o, :], AF.Square), inc='ac')
                    sq_a[b] = a
                    sch.wait('tensor', 've', v)
                    sch.mm(psm[ch][:], ones128_s[:], x1hold[:, o, :],
                           start=(o == 0), stop=(o == NDT - 1), inc='pe')
                    sch.wait('tensor', 'ac', a)
                    qs_pe[b] = sch.mm(psq[ch][:], ones128_s[:], sqb[b][:],
                                      start=(o == 0), stop=(o == NDT - 1),
                                      inc='pe', always=True)
                MQ = sch.val('pe')
                sch.wait('vector', 'pe', MQ)
                rr = rowv[0:1, 0:512]
                rs = rowv[0:1, 512:1024]
                v = sch.op('vector', lambda e, ch=ch, rr=rr: e.tensor_scalar_mul(
                    rr, psm[ch][:], 1.0 / DM), inc='ve')
                sch.wait('vector', 've', v)
                v = sch.op('vector', lambda e, ch=ch, rr=rr, rs=rs: e.tensor_scalar(
                    rs, psq[ch][:], 1.0 / DM, None, ALU.mult), inc='ve')
                sch.wait('vector', 've', v)
                v = sch.op('vector', lambda e, rr=rr, rowv=rowv: e.tensor_mul(
                    rowv[0:1, 1024:1536], rr, rr), inc='ve')
                sch.wait('vector', 've', v)
                v = sch.op('vector', lambda e, rs=rs, rowv=rowv: e.tensor_sub(
                    rs, rs, rowv[0:1, 1024:1536]), inc='ve')
                sch.wait('vector', 've', v)
                v = sch.op('vector', lambda e, rs=rs: e.tensor_scalar_add(rs, rs, 1e-5), inc='ve')
                sch.wait('vector', 've', v)
                v = sch.op('vector', lambda e, rs=rs: e.reciprocal(rs, rs), inc='ve')
                sch.wait('scalar', 've', v)
                a = sch.op('scalar', lambda e, rs=rs: e.activation(rs, rs, AF.Sqrt), inc='ac')
                sch.wait('tensor', 'ac', a)
                if ch == 1:
                    sch.wait('tensor', 've', NRM_VE)
                sch.mm(psmb[:], ones1_s[:], rr, start=True, stop=True, inc='pe')
                bv = sch.mm(psrb[:], ones1_s[:], rs, start=True, stop=True, inc='pe')
                sch.wait('vector', 'pe', bv)
                for o in range(NDT):
                    b = o % 2
                    v = sch.op('vector', lambda e, b=b, o=o: e.tensor_sub(
                        fb1[b][:], x1hold[:, o, :], psmb[:]), inc='ve')
                    sch.wait('vector', 've', v)
                    v = sch.op('vector', lambda e, b=b, o=o, ch=ch: e.tensor_mul(
                        xn1_used_as[:, o, ch * 512:(ch + 1) * 512], fb1[b][:], psrb[:]), inc='ve')
                    sch.wait('sync', 've', v)
                    sch.dma('sync', xn1_scr[o * 128:(o + 1) * 128, ch * 512:(ch + 1) * 512],
                            xn1_used_as[:, o, ch * 512:(ch + 1) * 512], inc='cstx')
                NRM_VE = sch.val('ve')
            sch.wait('sync', 'cstx', sch.val('cstx'))

        sch.barrier()

        with contextlib.ExitStack() as pctx:
            S = lambda name, shape, dt=F32: pctx.enter_context(nc.sbuf_tensor(name, shape, dt))
            wsl = [S(f"e_wsl{i}", [128, DM], BF16) for i in range(2)]
            la_s = S("e_la", [128, NDT * 16], BF16)
            wn_s = S("e_wn", [128, 2], F32)
            sqa = [S(f"e_sq{i}", [128, 512], BF16) for i in range(2)]
            tmpA = [S(f"e_tmp{i}", [128, 512], F32) for i in range(2)]
            qev = [S(f"e_qe{i}", [128, 512], BF16) for i in range(2)]
            rwa = S("e_rw", [1, 1024], F32)
            dq2 = Dequant.__new__(Dequant)
            dq2.sch = sch
            dq2.cod_i = S("e_dqc", [128, 2048], I32)
            dq2.codb = S("e_dqcb", [128, 2048], BF16)
            dq2.amx = S("e_dqa", [128, 2048], F32)
            dq2.dacc = [S(f"e_dqd{i}", [128, 2048], BF16) for i in range(2)]
            dq2.dtmp = S("e_dqt", [128, 2048], BF16)
            dq2.units = dq.units
            dq2.i = dq.i
            dq2.cv_prev = 0
            dq2.st = {0: 0, 1: 0}
            psA = [pctx.enter_context(nc.psum_tensor(f"e_ps{i}", [128, 512], F32))
                   for i in range(2)]
            psn = [pctx.enter_context(nc.psum_tensor(f"e_psn{i}", [1, 512], F32))
                   for i in range(2)]
            psrbA = [pctx.enter_context(nc.psum_tensor(f"e_psrb{i}", [128, 512], F32))
                     for i in range(2)]
            psu = pctx.enter_context(nc.psum_tensor("e_psu", [16, 512], F32))

            ld = sch.dma('gpsimd', la_s[:], la_up[:], inc='eld')
            ld = sch.dma('gpsimd', wn_s[:, 0:1], wnq[:], inc='eld')
            ld = sch.dma('gpsimd', wn_s[:, 1:2], wnk[:], inc='eld')
            sch.wait('tensor', 'eld', ld)
            for c in range(2):
                if c == 1:
                    sch.wait('tensor', 've', uv)
                pairs = [(la_s[:, d * 16:(d + 1) * 16], xn1_used_as[:, d, c * 512:(c + 1) * 512])
                         for d in range(NDT)]
                pv = sch.mm_group(pairs, psu[:], inc='pe')
                sch.wait('vector', 'pe', pv)
                uv = sch.op('vector', lambda e, c=c: e.tensor_copy(
                    u1s[0:16, c * 512:(c + 1) * 512], psu[:]), inc='ve')

            nld1 = {0: 0, 1: 0}
            eev = {0: 0, 1: 0}
            est = {0: 0, 1: 0}
            eac = {0: 0, 1: 0}
            for o in range(NQKV1):
                b = o % 2
                if o >= 2:
                    sch.wait('gpsimd', 've', eev[0])
                    sch.wait('gpsimd', 've', eev[1])
                nld1[b] += 1
                sch.dma('gpsimd', wsl[b][:], wqkv1[o, :, :], inc='elw%d' % b)
                sch.wait('tensor', 'elw%d' % b, 16 * nld1[b])
                for c in range(2):
                    b2 = (2 * o + c) % 2
                    pairs = [(wsl[b][:, d * 128:(d + 1) * 128],
                              xn1_used_as[:, d, c * 512:(c + 1) * 512]) for d in range(NDT)]
                    if 2 * o + c >= 2:
                        sch.wait('tensor', 've', eev[b2])
                    pv = sch.mm_group(pairs, psA[b2][:], inc='pe')
                    if o < 40:
                        sch.wait('scalar', 'pe', pv)
                        if eac[b2]:
                            sch.wait('scalar', 've', eev[b2])
                        a = sch.op('scalar', lambda e, b2=b2: e.activation(
                            sqa[b2][:], psA[b2][:], AF.Square), inc='ac')
                        eac[b2] = a
                        sch.wait('tensor', 'ac', a)
                        nv = sch.mm(psn[b2][:], ones128b_s[:], sqa[b2][:],
                                    start=True, stop=True, inc='pe')
                        sch.wait('vector', 'pe', nv)
                        rr = rwa[0:1, b2 * 512:b2 * 512 + 512]
                        v = sch.op('vector', lambda e, b2=b2, rr=rr: e.tensor_scalar(
                            rr, psn[b2][:], 1.0 / 128.0, 1e-6, ALU.mult, ALU.add), inc='ve')
                        sch.wait('vector', 've', v)
                        v = sch.op('vector', lambda e, rr=rr: e.reciprocal(rr, rr), inc='ve')
                        sch.wait('scalar', 've', v)
                        a = sch.op('scalar', lambda e, rr=rr: e.activation(rr, rr, AF.Sqrt), inc='ac')
                        if o < 32:
                            sch.wait('vector', 'ac', a)
                            v = sch.op('vector', lambda e, rr=rr: e.tensor_scalar_mul(
                                rr, rr, 1.0 / math.sqrt(HD)), inc='ve')
                            sch.wait('tensor', 've', v)
                        else:
                            sch.wait('tensor', 'ac', a)
                        bv = sch.mm(psrbA[b2][:], ones1_s[:], rr, start=True, stop=True, inc='pe')
                        sch.wait('vector', 'pe', bv)
                        wcol = wn_s[:, 0:1] if o < 32 else wn_s[:, 1:2]
                        v = sch.op('vector', lambda e, b2=b2, wcol=wcol: e.tensor_scalar(
                            tmpA[b2][:], psA[b2][:], wcol, None, ALU.mult), inc='ve')
                        sch.wait('vector', 've', v)
                        if est[b2]:
                            sch.wait('vector', 'est%d' % b2, est[b2])
                        v = sch.op('vector', lambda e, b2=b2: e.tensor_mul(
                            qev[b2][:], tmpA[b2][:], psrbA[b2][:]), inc='ve')
                    else:
                        sch.wait('vector', 'pe', pv)
                        if est[b2]:
                            sch.wait('vector', 'est%d' % b2, est[b2])
                        v = sch.op('vector', lambda e, b2=b2: e.tensor_copy(
                            qev[b2][:], psA[b2][:]), inc='ve')
                    eev[b2] = v
                    sch.wait('sync', 've', v)
                    if o < 32:
                        dst = q1_scr[o * 128:(o + 1) * 128, c * 512:(c + 1) * 512]
                    elif o < 40:
                        fo = o - 32
                        dst = k1c[c * 1024 + fo * 128:c * 1024 + (fo + 1) * 128, :]
                    else:
                        fo = o - 40
                        dst = v1c[c * 1024 + fo * 128:c * 1024 + (fo + 1) * 128, :]
                    est[b2] = sch.dma('sync', dst, qev[b2][:], inc='est%d' % b2)
                if o == 1:
                    sch.wait('gpsimd', 'dqst0', dq.st[0])
                    sch.wait('gpsimd', 'dqst1', dq.st[1])
                    sch.collective(wupc, WUPG, 'ccWU')
                if o == 39:
                    sch.wait('gpsimd', 'est0', sch.val('est0'))
                    sch.wait('gpsimd', 'est1', sch.val('est1'))
                    sch.collective(k1c, KG1, 'ccK1')
                if o == 47:
                    sch.wait('gpsimd', 'est0', sch.val('est0'))
                    sch.wait('gpsimd', 'est1', sch.val('est1'))
                    sch.collective(v1c, VG1, 'ccV1')
                if o >= 4 and o % 2 == 0:
                    dq2.emit(1)
            dq2.emit(54)
            sch.wait('gpsimd', 'dqst0', dq2.st[0])
            sch.wait('gpsimd', 'dqst1', dq2.st[1])
            sch.collective(wdnc, WDNG, 'ccWD')

        s23.close()
        sch.barrier()

        with contextlib.ExitStack() as pctx:
            S = lambda name, shape, dt: pctx.enter_context(nc.sbuf_tensor(name, shape, dt))
            ksb = [S(f"f_k{i}", [128, SMAX * 512], BF16) for i in range(2)]
            vsb = [S(f"f_v{i}", [128, SMAX * 4, 128], BF16) for i in range(2)]
            vstg = [S(f"f_vs{i}", [128, 512], BF16) for i in range(2)]
            panels = S("f_pan", [128, SMAX * 4, 256], BF16)
            att = S("f_att", [128, SMAX * 512], BF16)
            qh = [S(f"f_qh{i}", [128, TLOC], BF16) for i in range(2)]
            scrb = [S(f"f_sc{i}", [128, 512], F32) for i in range(2)]
            zc = S("f_zc", [128, 24], F32)
            ob = [S(f"f_ob{i}", [128, 512], BF16) for i in range(2)]
            bia_s = S("f_bia", [128, H1 * 2 * SMAX * 4], F32)
            rel_s = S("f_rel", [128, 5, 512], F32)
            rels = S("f_rels", [128, 5, 512], F32)
            kidx_s = S("f_kidx", [128, HKV * 2 * SMAX], I32)
            pss = [pctx.enter_context(nc.psum_tensor(f"f_pss{i}", [128, 512], F32))
                   for i in range(2)]
            pstb = [pctx.enter_context(nc.psum_tensor(f"f_pstb{i}", [128, 1024], BF16))
                    for i in range(2)]
            pso = pctx.enter_context(nc.psum_tensor("f_pso", [128, 512], F32))

            ld0 = sch.dma('gpsimd', bia_s[:], biast[:], inc='fld')
            ld0 = sch.dma('gpsimd', rel_s[:], rel5.transpose([1, 0, 2]), inc='fld')
            ld0 = sch.dma('gpsimd', kidx_s[:], kidx[:], inc='fld')
            sch.wait('tensor', 'fld', ld0)
            sch.wait('scalar', 'fld', ld0)
            sch.wait('vector', 'fld', ld0)
            sch.cwait('gpsimd', 'ccK1')
            sch.cwait('gpsimd', 'ccV1')
            kv_done = 0
            kv_done_prev = 0
            qh_done = {0: 0, 1: 0}
            st_o = {0: 0, 1: 0}
            exp_prev = {0: 0, 1: 0}
            vstg_pe = {0: 0, 1: 0}
            hq_idx = 0
            for kv in range(HKV):
                for gi in range(2):
                    it = kv * 2 + gi
                    bkv = it % 2
                    S_g = (SKV_A if gi == 0 else SKV_B)[kv]
                    if it >= 2:
                        sch.wait('gpsimd', 'pe', kv_done_prev)
                    for s in range(S_g):
                        col = (kv * 2 + gi) * SMAX + s
                        sch.indirect(ksb[bkv][:, s * 512:(s + 1) * 512], KG1[0:128, :],
                                     kidx_s[:, col:col + 1], inc='fldk%d' % bkv)
                    lk = None
                    for s in range(S_g):
                        vb = s % 2
                        col = (kv * 2 + gi) * SMAX + s
                        if vstg_pe[vb]:
                            sch.wait('gpsimd', 'pe', vstg_pe[vb])
                        lv = sch.indirect(vstg[vb][:], VG1[0:128, :],
                                          kidx_s[:, col:col + 1], inc='fldv%d' % vb)
                        sch.wait('tensor', 'fldv%d' % vb, lv)
                        par = s % 2
                        if it >= 1 or s >= 2:
                            sch.wait('tensor', 've', vq_cp.get((bkv, s - 2), prev_pan8))
                        tvv = None
                        for kb in range(4):
                            tvv = sch.op('tensor', lambda e, kb=kb, par=par, vb=vb: e.transpose(
                                pstb[par][:, kb * 128:(kb + 1) * 128],
                                vstg[vb][:, kb * 128:(kb + 1) * 128], idb_s[:]), inc='pe')
                        vstg_pe[vb] = tvv
                        sch.wait('vector', 'pe', tvv)
                        cvv = sch.op('vector', lambda e, s=s, par=par, bkv=bkv: e.tensor_copy(
                            vsb[bkv][:, s * 4:(s + 1) * 4, :],
                            pstb[par][:, 0:512].rearrange("p (a b) -> p a b", a=4)), inc='ve')
                        if it == 0 and s == 0:
                            vq_cp = {}
                            prev_pan8 = 0
                        vq_cp[(bkv, s)] = cvv
                    sch.wait('tensor', 'fldk%d' % bkv, sch.val('fldk%d' % bkv))
                    sch.wait('tensor', 've', cvv)
                    kv_done_prev = kv_done
                    for hq in range(4):
                        h = kv * 4 + hq
                        bq = hq_idx % 2
                        if hq_idx >= 2:
                            sch.wait('gpsimd', 'pe', qh_done[bq])
                        lq = sch.dma('gpsimd', qh[bq][:], q1_scr[h * 128:(h + 1) * 128, :],
                                     inc='fldq%d' % bq)
                        S_h = SH_TAB[h][gi]
                        s0 = S_g - S_h
                        sch.wait('vector', 'fldq%d' % bq, lq)
                        sch.wait('vector', 'pe', kv_done)
                        rv = None
                        for vv5 in range(5):
                            rv = sch.op('vector', lambda e, vv5=vv5, h=h: e.tensor_scalar(
                                rels[:, vv5, :], rel_s[:, vv5, :], float(SLOPES[h]),
                                None, ALU.mult), inc='ve')
                        sch.wait('tensor', 've', rv)
                        hq_idx += 1
                        pan_last = None
                        pv_half = 0
                        for j in range(4):
                            jj = j % 2
                            lhs = qh[bq][:, gi * 512 + j * 128: gi * 512 + (j + 1) * 128]
                            exp_v = {}
                            for si in range(S_h):
                                s = s0 + si
                                b = si % 2
                                sch.wait('tensor', 'ac', exp_prev[b])
                                pv = sch.mm_group(
                                    [(lhs, ksb[bkv][:, s * 512:(s + 1) * 512])],
                                    pss[b][:], inc='pe')
                                variant = (1 + j) if s == S_g - 1 else 0
                                sch.wait('vector', 'pe', pv)
                                vv = sch.op('vector', lambda e, b=b, variant=variant:
                                            e.scalar_tensor_tensor(
                                                scrb[b][:], pss[b][:], 0.0,
                                                rels[:, variant, :],
                                                ALU.add, ALU.add), inc='ve')
                                sch.wait('scalar', 've', vv)
                                col = bias_col(h, gi, s, j)
                                av = sch.op('scalar', lambda e, b=b, si=si, col=col:
                                            e.activation(
                                                att[:, si * 512:(si + 1) * 512],
                                                scrb[b][:], AF.Exp,
                                                bias=bia_s[:, col:col + 1],
                                                accum_out=zc[:, si:si + 1]), inc='ac')
                                exp_v[si] = av
                                exp_prev[b] = av
                            sch.wait('vector', 'ac', exp_v[S_h - 1])
                            v = sch.op('vector', lambda e, S_h=S_h: e.reduce_sum(
                                zc[:, 16:17], zc[:, 0:S_h], axis=AX), inc='ve')
                            sch.wait('vector', 've', v)
                            v = sch.op('vector', lambda e: e.reciprocal(
                                zc[:, 17:18], zc[:, 16:17]), inc='ve')
                            sch.wait('vector', 've', v)
                            normv = sch.op('vector', lambda e, S_h=S_h: e.tensor_scalar(
                                att[:, 0:S_h * 512], att[:, 0:S_h * 512],
                                zc[:, 17:18], None, ALU.mult), inc='ve')
                            sch.wait('tensor', 've', normv)
                            pan_cp = {}
                            for kt in range(S_h * 4):
                                half = (kt // 4) % 2
                                if kt >= 8:
                                    sch.wait('tensor', 've', pan_cp[kt - 8])
                                tv = sch.op('tensor', lambda e, kt=kt, half=half: e.transpose(
                                    pstb[half][:, (kt % 4) * 128:(kt % 4) * 128 + 128],
                                    att[:, kt * 128:(kt + 1) * 128], idb_s[:]), inc='pe')
                                if kt % 4 == 3 or kt == S_h * 4 - 1:
                                    k0g = (kt // 4) * 4
                                    nk4 = kt - k0g + 1
                                    sch.wait('vector', 'pe', tv)
                                    if j >= 2:
                                        sch.wait('vector', 'pe', pv_half)
                                    pv4 = sch.op('vector', lambda e, k0g=k0g, nk4=nk4,
                                                 half=half, jj=jj, s0=s0: e.tensor_copy(
                                        panels[:, s0 * 4 + k0g:s0 * 4 + k0g + nk4,
                                               jj * 128:(jj + 1) * 128],
                                        pstb[half][:, 0:nk4 * 128]
                                        .rearrange("p (a b) -> p a b", a=nk4)), inc='ve')
                                    for z_ in range(k0g, kt + 1):
                                        pan_cp[z_] = pv4
                            pan_last = pan_cp[S_h * 4 - 1]
                            if j % 2 == 1:
                                sch.wait('tensor', 've', pan_last)
                                jh = j // 2
                                pairs = [(vsb[bkv][:, kt, :], panels[:, kt, :])
                                         for kt in range(s0 * 4, S_g * 4)]
                                pv_half = sch.mm_group(
                                    pairs, pso[:, jh * 256:(jh + 1) * 256], inc='pe')
                        prev_pan8 = pan_last
                        pvv = pv_half
                        kv_done = pvv
                        qh_done[bq] = pvv
                        bo_ = hq_idx % 2
                        sch.wait('vector', 'pe', pvv)
                        if st_o[bo_]:
                            sch.wait('vector', 'fsto%d' % bo_, st_o[bo_])
                        ov = sch.op('vector', lambda e, bo_=bo_: e.tensor_copy(
                            ob[bo_][:], pso[:]), inc='ve')
                        sch.wait('sync', 've', ov)
                        st_o[bo_] = sch.dma(
                            'sync', o_scr[h * 128:(h + 1) * 128, gi * 512:(gi + 1) * 512],
                            ob[bo_][:], inc='fsto%d' % bo_)
                        sch.wait('tensor', 've', ov)
            sch.wait('sync', 'fsto0', sch.val('fsto0'))
            sch.wait('sync', 'fsto1', sch.val('fsto1'))

        sch.barrier()

        with contextlib.ExitStack() as pctx:
            S = lambda name, shape, dt: pctx.enter_context(nc.sbuf_tensor(name, shape, dt))
            x2 = S("g_x2", [128, NHT, 512], BF16)
            u2s = S("g_u2", [16, 512], BF16)
            lbup_s = S("g_lbu", [16, HF], BF16)
            ladn_s = S("g_lad", [128, NHT * 16], BF16)
            lbdn_s = S("g_lbd", [16, DM], BF16)
            bup_s = S("g_bup", [128, NHT], F32)
            bdn_s = S("g_bdn", [128, NDT], F32)
            xnc = S("g_xnc", [128, NDT, 512], BF16)
            wu_sl = [S(f"g_wu{i}", [128, 896], BF16) for i in range(2)]
            wd_sl = [S(f"g_wd{i}", [128, 1024], BF16) for i in range(2)]
            gvb = [S(f"g_gv{i}", [128, 512], BF16) for i in range(2)]

            lc = sch.dma('gpsimd', lbup_s[:], lbup[:], inc='gld')
            lc = sch.dma('gpsimd', ladn_s[:], ladn[:], inc='gld')
            lc = sch.dma('gpsimd', lbdn_s[:], lbdn[:], inc='gld')
            lc = sch.dma('gpsimd', bup_s[:], bup[:], inc='gld')
            lc = sch.dma('gpsimd', bdn_s[:], bdn[:], inc='gld')
            sch.wait('tensor', 'gld', lc)
            sch.wait('scalar', 'gld', lc)
            sch.cwait('sync', 'ccWU')
            sch.cwait('sync', 'ccWD')
            sch.cwait('tensor', 'ccWU')
            sch.cwait('tensor', 'ccWD')

            nxl = 0
            wu_pe = {0: 0, 1: 0}
            wd_pe = {0: 0, 1: 0}
            x2_ac = 0
            dn_ac = 0
            stg1 = {0: 0, 1: 0}
            XNC_PREV = 0
            nslw = {0: 0, 1: 0}
            nsld = {0: 0, 1: 0}
            for tc in range(2):
                with contextlib.ExitStack() as tctx:
                    psH = [tctx.enter_context(nc.psum_tensor(f"g_psH{tc}{i}", [128, 512], F32))
                           for i in range(7)]
                    psu2 = tctx.enter_context(nc.psum_tensor(f"g_psu2{tc}", [16, 512], F32))
                    if tc > 0:
                        sch.wait('gpsimd', 'pe', XNC_PREV)
                    nxl += 1
                    lx = sch.dma('gpsimd', xnc[:],
                                 xn1_scr[:, tc * 512:(tc + 1) * 512]
                                 .rearrange("(d p) t -> p d t", p=128), inc='gxl')
                    sch.wait('tensor', 'gxl', 16 * nxl)
                    for r in range(NCORES):
                        ntr = HTILES[r]
                        for (g0, g1) in ((0, 7), (7, ntr)):
                            gs = g1 - g0
                            if x2_ac:
                                sch.wait('tensor', 'ac', x2_ac)
                            if dn_ac:
                                sch.wait('tensor', 'ac', dn_ac)
                            for j in range(gs):
                                ht = HTOFF[r] + g0 + j
                                sch.mm(psH[j][:], lbup_s[0:16, ht * 128:(ht + 1) * 128],
                                       u1s[0:16, tc * 512:(tc + 1) * 512],
                                       start=True, stop=False)
                            for dt in range(NDT):
                                b = dt % 2
                                if wu_pe[b]:
                                    sch.wait('sync', 'pe', wu_pe[b])
                                nslw[b] += 1
                                sch.dma('sync', wu_sl[b][:, 0:gs * 128],
                                        WUPG[r * DM + dt * 128:r * DM + (dt + 1) * 128,
                                             g0 * 128:g1 * 128], inc='gwu%d' % b)
                                sch.wait('tensor', 'gwu%d' % b, 16 * nslw[b])
                                for j in range(gs):
                                    pv = sch.mm(psH[j][:], wu_sl[b][:, j * 128:(j + 1) * 128],
                                                xnc[:, dt, :], start=False, stop=(dt == NDT - 1),
                                                inc='pe', always=(j == gs - 1))
                                wu_pe[b] = pv
                            PV_UP = sch.val('pe')
                            sch.wait('scalar', 'pe', PV_UP)
                            for j in range(gs):
                                ht = HTOFF[r] + g0 + j
                                a = sch.op('scalar', lambda e, pb=psH[j], ht=ht: e.activation(
                                    x2[:, ht, :], pb[:], AF.Relu,
                                    bias=bup_s[:, ht:ht + 1]), inc='ac')
                                x2_ac = a
                                sch.wait('tensor', 'ac', a)
                                sch.mm(psu2[:], ladn_s[:, ht * 16:(ht + 1) * 16], x2[:, ht, :],
                                       start=(ht == 0), stop=(ht == NHT - 1),
                                       inc=('pe' if ht == NHT - 1 else None))
                    U2PE = sch.val('pe')
                    sch.wait('vector', 'pe', U2PE)
                    u2v = sch.op('vector', lambda e, pb=psu2: e.tensor_copy(u2s[:], pb[:]), inc='ve')
                    if debug:
                        sch.wait('sync', 'ac', x2_ac)
                        dv = sch.dma('sync',
                                     x2d[:, tc * 512:(tc + 1) * 512]
                                     .rearrange("(ht p) t -> p ht t", p=128),
                                     x2[:], inc='dbgf')
                        sch.wait('sync', 've', u2v)
                        dv = sch.dma('sync', u2d[0:16, tc * 512:(tc + 1) * 512],
                                     u2s[:], inc='dbgf')
                        dv = sch.dma('sync', u1d, u1s[:], inc='dbgf')
                        sch.wait('sync', 'dbgf', dv)
                with contextlib.ExitStack() as tctx:
                    psD = [tctx.enter_context(nc.psum_tensor(f"g_psD{tc}{i}", [128, 512], F32))
                           for i in range(8)]
                    sch.wait('tensor', 've', u2v)
                    for dg in range(4):
                        if x2_ac:
                            sch.wait('tensor', 'ac', x2_ac)
                        if dn_ac:
                            sch.wait('tensor', 'ac', dn_ac)
                        for j in range(8):
                            dt = dg * 8 + j
                            sch.mm(psD[j][:], lbdn_s[0:16, dt * 128:(dt + 1) * 128],
                                   u2s[:], start=True, stop=False)
                        for ht in range(NHT):
                            b = ht % 2
                            r, lt = rank_lt(ht)
                            if wd_pe[b]:
                                sch.wait('sync', 'pe', wd_pe[b])
                            nsld[b] += 1
                            sch.dma('sync', wd_sl[b][:],
                                    WDNG[r * HSH_PAD + lt * 128:r * HSH_PAD + (lt + 1) * 128,
                                         dg * 1024:(dg + 1) * 1024], inc='gwd%d' % b)
                            sch.wait('tensor', 'gwd%d' % b, 16 * nsld[b])
                            for j in range(8):
                                pv = sch.mm(psD[j][:], wd_sl[b][:, j * 128:(j + 1) * 128],
                                            x2[:, ht, :], start=False, stop=(ht == NHT - 1),
                                            inc='pe', always=(j == 7))
                            wd_pe[b] = pv
                        PV_DN = sch.val('pe')
                        sch.wait('scalar', 'pe', PV_DN)
                        for j in range(8):
                            dt = dg * 8 + j
                            b2 = j % 2
                            if stg1[b2]:
                                sch.wait('scalar', 'gst%d' % b2, stg1[b2])
                            a = sch.op('scalar', lambda e, pb=psD[j], dt=dt, b2=b2: e.activation(
                                gvb[b2][:], pb[:], AF.Silu,
                                bias=bdn_s[:, dt:dt + 1]), inc='ac')
                            dn_ac = a
                            sch.wait('sync', 'ac', a)
                            stg1[b2] = sch.dma(
                                'sync', g1_scr[dt * 128:(dt + 1) * 128,
                                               tc * 512:(tc + 1) * 512],
                                gvb[b2][:], inc='gst%d' % b2)
                    XNC_PREV = sch.val('pe')
            sch.wait('sync', 'gst0', sch.val('gst0'))
            sch.wait('sync', 'gst1', sch.val('gst1'))

        sch.barrier()

        with contextlib.ExitStack() as pctx:
            S = lambda name, shape, dt: pctx.enter_context(nc.sbuf_tensor(name, shape, dt))
            o_sb = S("h_o", [128, NDT, 512], BF16)
            wos = [S(f"h_w{i}", [128, DM], BF16) for i in range(2)]
            x1t = [S(f"h_x{i}", [128, 512], F32) for i in range(2)]
            g1t = [S(f"h_g{i}", [128, 512], BF16) for i in range(2)]
            tmpc = [S(f"h_t{i}", [128, 512], F32) for i in range(2)]
            outb = [S(f"h_ob{i}", [128, 512], F32) for i in range(2)]
            psf = [pctx.enter_context(nc.psum_tensor(f"h_ps{i}", [128, 512], F32))
                   for i in range(2)]

            nwo = {0: 0, 1: 0}
            nxg = {0: 0, 1: 0}
            sto = {0: 0, 1: 0}
            cmb_v = {0: 0, 1: 0}
            pe_wo = {0: 0, 1: 0}
            nol = 0
            for tc in range(2):
                if tc > 0:
                    sch.wait('gpsimd', 'pe', sch.val('pe'))
                nol += 1
                lo = sch.dma('gpsimd', o_sb[:],
                             o_scr[:, tc * 512:(tc + 1) * 512]
                             .rearrange("(nd p) t -> p nd t", p=128), inc='hlo')
                sch.wait('tensor', 'hlo', 16 * nol)
                for dt_i in range(NDT):
                    u = tc * NDT + dt_i
                    b = u % 2
                    if u >= 2:
                        sch.wait('gpsimd', 'pe', pe_wo[b])
                        sch.wait('gpsimd', 've', cmb_v[b])
                    nwo[b] += 1
                    sch.dma('gpsimd', wos[b][:], wo1[dt_i], inc='hlw%d' % b)
                    nxg[b] += 1
                    lgf = sch.dma('gpsimd', x1t[b][:],
                                  x1_scr[dt_i * 128:(dt_i + 1) * 128, tc * 512:(tc + 1) * 512],
                                  inc='hlg%d' % b)
                    lgf = sch.dma('gpsimd', g1t[b][:],
                                  g1_scr[dt_i * 128:(dt_i + 1) * 128, tc * 512:(tc + 1) * 512],
                                  inc='hlg%d' % b)
                    sch.wait('tensor', 'hlw%d' % b, 16 * nwo[b])
                    pairs = [(wos[b][:, j * 128:(j + 1) * 128], o_sb[:, j, :])
                             for j in range(NDT)]
                    sch.wait('tensor', 've', cmb_v[b])
                    pv = sch.mm_group(pairs, psf[b][:], inc='pe')
                    pe_wo[b] = pv
                    sch.wait('vector', 'pe', pv)
                    sch.wait('vector', 'hlg%d' % b, lgf)
                    if sto[b]:
                        sch.wait('vector', 'hst%d' % b, sto[b])
                    v = sch.op('vector', lambda e, b=b: e.tensor_mul(
                        tmpc[b][:], psf[b][:], g1t[b][:]), inc='ve')
                    sch.wait('vector', 've', v)
                    v = sch.op('vector', lambda e, b=b: e.tensor_add(
                        outb[b][:], tmpc[b][:], x1t[b][:]), inc='ve')
                    cmb_v[b] = v
                    sch.wait('sync', 've', v)
                    sto[b] = sch.dma(
                        'sync', outF[dt_i * 128:(dt_i + 1) * 128, tc * 512:(tc + 1) * 512],
                        outb[b][:], inc='hst%d' % b)
            sch.wait('sync', 'hst0', sch.val('hst0'))
            sch.wait('sync', 'hst1', sch.val('hst1'))
        if debug:
            dv = sch.dma('sync', k1d, k1c, inc='dbgc')
            dv = sch.dma('sync', v1d, v1c, inc='dbgc')
            dv = sch.dma('sync', wud, wupc, inc='dbgc')
            dv = sch.dma('sync', wdd, wdnc, inc='dbgc')
            dv = sch.dma('sync', wugd, WUPG, inc='dbgc')
            dv = sch.dma('sync', wdgd, WDNG, inc='dbgc')
            sch.wait('sync', 'dbgc', dv)
        sch.run(block)
    return nc



_PROGS = {}


def _prog(name, builder):
    if name not in _PROGS:
        _PROGS[name] = builder()
    return _PROGS[name]


def _shard_rows(c):
    return np.r_[c * 512:(c + 1) * 512, (15 - c) * 512:(16 - c) * 512]


def prepare_in_maps(inputs):
    g = lambda k: np.asarray(inputs[k])
    x_full = np.ascontiguousarray(g('L0_attn_x').reshape(N, DM).astype(np.float32))
    shards = [_shard_rows(c) for c in range(NCORES)]
    idb = np.eye(128, dtype=BF)
    idf = np.eye(128, dtype=np.float32)

    Wqk_T = g('L0_attn_Wqk_w').T
    hh = np.arange(8)[:, None]
    ee = np.arange(64)[None, :]
    qperm = (hh * 128 + 2 * ee).reshape(-1)
    kperm = (hh * 128 + 2 * ee + 1).reshape(-1)
    big_T = np.concatenate([Wqk_T[:, qperm], Wqk_T[:, kperm], g('L0_attn_Wv_w').T], axis=1)
    b_qk = g('L0_attn_Wqk_b')
    sel8 = np.zeros((8, 128), dtype=np.float32)
    for ei in range(8):
        sel8[ei, ei * 16:(ei + 1) * 16] = 1.0

    wq1 = np.concatenate([g('L1_attn_Wq').T, g('L1_attn_Wk').T, g('L1_attn_Wv').T], axis=1)
    cod_u_pad = np.zeros((DM, 8 * HSH_PAD), dtype=np.int32)
    cod_u_pad[:, :HF] = g('L1_ffn_w_up_codes').reshape(HF, DM).T
    amx_u_pad = np.zeros((64, 8 * HSH_PAD), dtype=np.float32)
    amx_u_pad[:, :HF] = g('L1_ffn_w_up_absmax').reshape(HF, 64).T
    cod_d_pad = np.zeros((8 * HSH_PAD, DM), dtype=np.int32)
    cod_d_pad[:HF] = g('L1_ffn_w_down_codes').reshape(DM, HF).T
    amx_d_pad = np.zeros((176, DM), dtype=np.float32)
    amx_d_pad[:172] = g('L1_ffn_w_down_absmax').reshape(DM, 172).T

    common = dict(
        wqkv0=tile_w(np.ascontiguousarray(big_T)),
        bqkv=cols128(np.concatenate([b_qk[qperm], b_qk[kperm], g('L0_attn_Wv_b')])),
        wrout=np.ascontiguousarray(
            g('L0_ffn_W_router').T.reshape(32, 128, 8).transpose(1, 0, 2).reshape(128, 256)),
        wa=tile_w(np.ascontiguousarray(np.transpose(g('L0_ffn_A'), (2, 0, 1)).reshape(4096, 128))),
        wbase=tile_w(g('L0_ffn_W_base').T),
        bbase=cols128(g('L0_ffn_b_base')),
        bstk=np.ascontiguousarray(
            g('L0_ffn_B').transpose(0, 2, 1).reshape(128, 4096)
            .reshape(128, 32, 128).transpose(1, 0, 2)),
        sel8=sel8, idf=idf, idb=idb,
        ones1=np.ones((1, 128), dtype=np.float32),
        ones128=np.ones((128, 1), dtype=np.float32),
        ones128b=np.ones((128, 1), dtype=BF),
        wo0=tile_w(np.ascontiguousarray(g('L0_attn_Wo_w').T)),
        bo0=cols128(g('L0_attn_Wo_b')),
        wqkv1=tile_w(np.ascontiguousarray(wq1)),
        wnq=np.ascontiguousarray(g('L1_attn_q_norm_w')[:, None].astype(np.float32)),
        wnk=np.ascontiguousarray(g('L1_attn_k_norm_w')[:, None].astype(np.float32)),
        la_up=np.ascontiguousarray(
            g('L1_ffn_w_up_lora_a').reshape(32, 128, 16).transpose(1, 0, 2).reshape(128, 512)),
        rel5=_rel_consts(),
        wo1=tile_w(g('L1_attn_Wo').T),
        bup=np.ascontiguousarray(g('L1_ffn_b_up').reshape(NHT, 128).T.astype(np.float32)),
        bdn=cols128(g('L1_ffn_b_down')),
        lbup=np.ascontiguousarray(g('L1_ffn_w_up_lora_b').astype(np.float32)),
        ladn=np.ascontiguousarray(
            g('L1_ffn_w_down_lora_a').reshape(NHT, 128, 16).transpose(1, 0, 2)
            .reshape(128, NHT * 16).astype(np.float32)),
        lbdn=np.ascontiguousarray(g('L1_ffn_w_down_lora_b').astype(np.float32)),
    )
    in_maps = []
    for c in range(NCORES):
        xs = np.ascontiguousarray(x_full[shards[c]])
        in_maps.append(dict(
            x_tm=xs,
            xT=np.ascontiguousarray(xs.T),
            cod_u=np.ascontiguousarray(cod_u_pad[:, c * HSH_PAD:(c + 1) * HSH_PAD]),
            amx_u=np.ascontiguousarray(amx_u_pad[:, c * HSH_PAD:(c + 1) * HSH_PAD]),
            cod_d=np.ascontiguousarray(cod_d_pad[c * HSH_PAD:(c + 1) * HSH_PAD]),
            amx_d=np.ascontiguousarray(amx_d_pad[c * 22:(c + 1) * 22]),
            biast=l1b_bias_table(c),
            kidx=l1b_kidx(c),
            **common))
    return in_maps


def kernel(**inputs):
    in_maps = prepare_in_maps(inputs)
    shards = [_shard_rows(c) for c in range(NCORES)]
    nc = _prog('fused', build_fused)
    res = run_bass_kernel_spmd(nc, in_maps, core_ids=list(range(NCORES)))
    final = np.zeros((N, DM), dtype=np.float32)
    for c in range(NCORES):
        final[shards[c]] = res.results[c]["outF"].T
    return final.reshape(1, N, DM)


# version 08 (final)
# speedup vs baseline: 1.0652x; 1.0131x over previous
import math
import contextlib
import numpy as np
import ml_dtypes
import concourse.bass as bass
import concourse.mybir as mybir
from concourse.bass_utils import run_bass_kernel_spmd

F32 = mybir.dt.float32
BF16 = mybir.dt.bfloat16
I32 = mybir.dt.int32
AF = mybir.ActivationFunctionType
ALU = mybir.AluOpType
AX = mybir.AxisListType.X

NCORES = 8
N = 8192
DM = 4096
TLOC = N // NCORES
NT = TLOC // 128
NDT = DM // 128
NCH = 16

BF = ml_dtypes.bfloat16

NF4 = [-1.0, -0.6961928009986877, -0.5250730514526367, -0.39491748809814453,
       -0.28444138169288635, -0.18477343022823334, -0.09105003625154495, 0.0,
       0.07958029955625534, 0.16093020141124725, 0.24611230194568634,
       0.33791524171829224, 0.44070982933044434, 0.5626170039176941,
       0.7229568362236023, 1.0]

H1, HKV, HD = 32, 8, 128
_s0 = 2 ** (-2 ** (-(math.log2(H1) - 3)))
SLOPES = [_s0 * _s0 ** i for i in range(H1)]
WIN = [max(640, int(140.0 / s) + 1) for s in SLOPES]
WC = [int(math.ceil(w / 512.0)) + 1 for w in WIN]
B_SHIFT = 40.0

SKV_A = [min(max(WC[4 * kv + g] for g in range(4)), 8) for kv in range(HKV)]
SKV_B = [min(max(WC[4 * kv + g] for g in range(4)), 16) for kv in range(HKV)]
SMAX = 16
SH_TAB = [[min(WC[h], SKV_A[h // 4]), min(WC[h], SKV_B[h // 4])] for h in range(H1)]

H0, E0 = 8, 64
NQKV = 12
NKT = N // 128

NQKV1 = 48

HF = 11008
NHT = HF // 128
HTILES = [11] * 7 + [9]
HTOFF = [sum(HTILES[:r]) for r in range(NCORES + 1)]
HSH_PAD = 1408


def cr_of(g):
    return 2 * min(g, 15 - g) + (1 if g >= 8 else 0)


def rank_lt(ht):
    for r in range(NCORES):
        if ht < HTOFF[r + 1]:
            return r, ht - HTOFF[r]
    raise ValueError



class _Ct:
    __slots__ = ("v",)

    def __init__(self):
        self.v = 0


class Sch:
    ENGINES = ("sync", "gpsimd", "tensor", "vector", "scalar")

    def __init__(self, nc, ctx):
        self.nc = nc
        self.ctx = ctx
        self.q = {e: [] for e in self.ENGINES}
        self.sems = {}

    def sem(self, name):
        if name not in self.sems:
            h = self.ctx.enter_context(self.nc.semaphore(name))
            self.sems[name] = (h, _Ct())
        return self.sems[name]

    def val(self, name):
        return self.sem(name)[1].v

    def wait(self, eng, name, v=None):
        h, ct = self.sem(name)
        if v is None:
            v = ct.v
        if v <= 0:
            return
        self.q[eng].append(lambda e, h=h, v=v: e.wait_ge(h, v))

    def op(self, eng, build, inc=None, n=1):
        if inc is not None:
            h, ct = self.sem(inc)
            ct.v += n

            def f(e, build=build, h=h, n=n):
                build(e).then_inc(h, n)
            self.q[eng].append(f)
            return ct.v
        self.q[eng].append(build)
        return None

    def dma(self, eng, out, in_, inc=None):
        return self.op(eng, lambda e, o=out, i=in_: e.dma_start(out=o, in_=i),
                       inc=inc, n=16)

    def collective(self, in_ap, out_ap, name):
        h, ct = self.sem(name)
        ct.v = 1

        def f(e, h=h, in_ap=in_ap, out_ap=out_ap):
            e.collective_compute(
                "AllGather", mybir.AluOpType.bypass,
                replica_groups=[list(range(NCORES))],
                ins=[in_ap.opt()], outs=[out_ap.opt()]).then_inc(h)
        self.q['gpsimd'].append(f)

    def cwait(self, eng, name):
        h, _ = self.sem(name)
        self.q[eng].append(lambda e, h=h: e.wait_ge(h, 1))

    def indirect(self, out, in_, idx_ap, inc=None):
        return self.op(
            'gpsimd',
            lambda e, o=out, i=in_, x=idx_ap: e.indirect_dma_start(
                o, None, i, bass.IndirectOffsetOnAxis(ap=x, axis=0)),
            inc=inc, n=16)

    def mm_group(self, pairs, psum_ap, inc):
        nmm = len(pairs)
        v = None
        for i, (lh, rh) in enumerate(pairs):
            v = self.op('tensor',
                        lambda e, lh=lh, rh=rh, s=(i == 0), st=(i == nmm - 1):
                        e.matmul(psum_ap, lh, rh, start=s, stop=st,
                                 skip_group_check=True),
                        inc=(inc if i == nmm - 1 else None))
        return v

    def mm(self, psum_ap, lh, rh, start, stop, inc=None, always=False):
        return self.op('tensor',
                       lambda e, lh=lh, rh=rh, s=start, st=stop:
                       e.matmul(psum_ap, lh, rh, start=s, stop=st,
                                skip_group_check=True),
                       inc=(inc if (stop or always) else None))

    def barrier(self):
        snap = [(h, ct.v) for (n, (h, ct)) in self.sems.items()
                if ct.v > 0 and not n.startswith('cc')]
        for eng in self.ENGINES:
            for h, v in snap:
                self.q[eng].append(lambda e, h=h, v=v: e.wait_ge(h, v))

    def run(self, block):
        for eng in self.ENGINES:
            ops = self.q[eng]

            def body(handle, ops=ops):
                for fn in ops:
                    fn(handle)
            getattr(block, eng)(body)



def tile_w(w_t):
    K, O = w_t.shape
    nd, no = K // 128, O // 128
    return np.ascontiguousarray(
        w_t.reshape(nd, 128, no, 128).transpose(2, 1, 0, 3).reshape(no, 128, nd * 128))


def cols128(b):
    no = b.shape[0] // 128
    return np.ascontiguousarray(b.reshape(no, 128).T.astype(np.float32))


def _rel_consts():
    i = np.arange(128)[:, None]
    j = np.arange(512)[None, :]
    rel = (j - i).astype(np.float32)
    out = [rel]
    for m in range(4):
        r = rel.copy()
        r[(-m * 128 + j - i) > 0] = -1e33
        out.append(r)
    return np.ascontiguousarray(np.stack(out))


def l1b_slot_gc(core):
    out = np.full((HKV, 2, SMAX), -1, dtype=np.int64)
    for kv in range(HKV):
        for gi, ch in enumerate((core, 15 - core)):
            S_g = (SKV_A if gi == 0 else SKV_B)[kv]
            for s in range(S_g):
                gc = ch - (S_g - 1) + s
                out[kv, gi, s] = gc if gc >= 0 else -1
    return out


def l1b_bias_table(core):
    tab = np.full((H1 * 2 * SMAX * 4,), -1e30, dtype=np.float32)
    gc = l1b_slot_gc(core)
    for h in range(H1):
        kv = h // 4
        for gi, ch in enumerate((core, 15 - core)):
            S_g = (SKV_A if gi == 0 else SKV_B)[kv]
            S_h = SH_TAB[h][gi]
            for s in range(S_g - S_h, S_g):
                g = gc[kv, gi, s]
                if g < 0:
                    continue
                for j in range(4):
                    q0 = ch * 512 + j * 128
                    k0 = g * 512
                    tab[bias_col(h, gi, s, j)] = SLOPES[h] * (k0 - q0) - B_SHIFT
    return np.ascontiguousarray(
        np.broadcast_to(tab[None, :], (128, tab.shape[0])).astype(np.float32))


def bias_col(h, gi, s, j):
    return ((h * 2 + gi) * SMAX + s) * 4 + j


def l1b_kidx(core):
    gc = l1b_slot_gc(core)
    out = np.zeros((128, HKV * 2 * SMAX), dtype=np.int32)
    p = np.arange(128, dtype=np.int32)
    for kv in range(HKV):
        for gi in range(2):
            for s in range(SMAX):
                g = gc[kv, gi, s]
                base = 0 if g < 0 else cr_of(int(g)) * 1024 + kv * 128
                out[:, (kv * 2 + gi) * SMAX + s] = base + p
    return np.ascontiguousarray(out)



class Dequant:

    def __init__(self, sch, S, cod_u, amx_u, cod_d, amx_d, wupc, wdnc):
        self.sch = sch
        self.cod_i = S("dq_cod", [128, 2048], I32)
        self.codb = S("dq_codb", [128, 2048], BF16)
        self.amx = S("dq_amx", [128, 2048], F32)
        self.dacc = [S(f"dq_dacc{i}", [128, 2048], BF16) for i in range(2)]
        self.dtmp = S("dq_dtmp", [128, 2048], BF16)
        self.units = []
        for dt in range(NDT):
            self.units.append((cod_u[dt * 128:(dt + 1) * 128, :],
                               amx_u[2 * dt:2 * dt + 1, :],
                               amx_u[2 * dt + 1:2 * dt + 2, :],
                               wupc[dt * 128:(dt + 1) * 128, :], HSH_PAD))
        for ht in range(11):
            for hf in range(2):
                c0 = hf * 2048
                self.units.append((cod_d[ht * 128:(ht + 1) * 128, c0:c0 + 2048],
                                   amx_d[2 * ht:2 * ht + 1, c0:c0 + 2048],
                                   amx_d[2 * ht + 1:2 * ht + 2, c0:c0 + 2048],
                                   wdnc[ht * 128:(ht + 1) * 128, c0:c0 + 2048],
                                   2048))
        self.i = 0
        self.cv_prev = 0
        self.st = {0: 0, 1: 0}
        self.n_emitted_st = 0

    def emit(self, n):
        sch = self.sch
        for _ in range(n):
            if self.i >= len(self.units):
                return
            i = self.i
            self.i += 1
            src, am0, am1, dst, w = self.units[i]
            b = i % 2
            if self.cv_prev:
                sch.wait('gpsimd', 've', self.cv_prev)
            ld = sch.dma('gpsimd', self.cod_i[:, 0:w], src, inc='dqld')
            ld = sch.dma('gpsimd', self.amx[0:64, 0:w],
                         am0.broadcast_to([64, w]), inc='dqld')
            ld = sch.dma('gpsimd', self.amx[64:128, 0:w],
                         am1.broadcast_to([64, w]), inc='dqld')
            sch.wait('vector', 'dqld', ld)
            v = sch.op('vector', lambda e, w=w: e.tensor_copy(
                self.codb[:, 0:w], self.cod_i[:, 0:w]), inc='ve')
            if self.st[b]:
                sch.wait('vector', 'dqst%d' % b, self.st[b])
            sch.wait('vector', 've', v)
            v = sch.op('vector', lambda e, b=b, w=w: e.tensor_scalar(
                self.dacc[b][:, 0:w], self.codb[:, 0:w], 0.0, NF4[0],
                ALU.is_equal, ALU.mult), inc='ve')
            for vi in range(1, 16):
                v2 = sch.op('vector', lambda e, vi=vi, w=w: e.tensor_scalar(
                    self.dtmp[:, 0:w], self.codb[:, 0:w], float(vi), NF4[vi],
                    ALU.is_equal, ALU.mult), inc='ve')
                sch.wait('vector', 've', v2)
                v = sch.op('vector', lambda e, b=b, w=w: e.tensor_add(
                    self.dacc[b][:, 0:w], self.dacc[b][:, 0:w],
                    self.dtmp[:, 0:w]), inc='ve')
                sch.wait('vector', 've', v)
            v = sch.op('vector', lambda e, b=b, w=w: e.tensor_mul(
                self.dacc[b][:, 0:w], self.dacc[b][:, 0:w],
                self.amx[:, 0:w]), inc='ve')
            self.cv_prev = v
            sch.wait('sync', 've', v)
            self.st[b] = sch.dma('sync', dst, self.dacc[b][:, 0:w],
                                 inc='dqst%d' % b)



def emit_ln_tile(sch, x_sb, xn_sb, scr, stt, eps=1e-5):
    v = sch.op('vector', lambda e: e.reduce_sum(stt[:, 0:1], x_sb[:], axis=AX), inc='ve')
    a = sch.op('scalar', lambda e: e.activation(scr[:], x_sb[:], AF.Square,
                                                accum_out=stt[:, 1:2]), inc='ac')
    sch.wait('vector', 'ac', a)
    sch.wait('vector', 've', v)
    v = sch.op('vector', lambda e: e.tensor_scalar_mul(stt[:, 2:3], stt[:, 0:1], 1.0 / DM), inc='ve')
    sch.wait('vector', 've', v)
    v = sch.op('vector', lambda e: e.tensor_mul(stt[:, 4:5], stt[:, 2:3], stt[:, 2:3]), inc='ve')
    v = sch.op('vector', lambda e: e.tensor_scalar(stt[:, 3:4], stt[:, 1:2], 1.0 / DM,
                                                   None, ALU.mult), inc='ve')
    sch.wait('vector', 've', v)
    v = sch.op('vector', lambda e: e.tensor_sub(stt[:, 5:6], stt[:, 3:4], stt[:, 4:5]), inc='ve')
    sch.wait('vector', 've', v)
    v = sch.op('vector', lambda e: e.tensor_scalar_add(stt[:, 5:6], stt[:, 5:6], 1e-5), inc='ve')
    sch.wait('vector', 've', v)
    v = sch.op('vector', lambda e: e.reciprocal(stt[:, 6:7], stt[:, 5:6]), inc='ve')
    sch.wait('scalar', 've', v)
    a = sch.op('scalar', lambda e: e.activation(stt[:, 7:8], stt[:, 6:7], AF.Sqrt), inc='ac')
    sch.wait('vector', 'ac', a)
    v = sch.op('vector', lambda e: e.tensor_scalar(xn_sb[:], x_sb[:], stt[:, 2:3],
                                                   stt[:, 7:8], ALU.subtract, ALU.mult), inc='ve')
    return v



def build_fused(debug=False):
    nc = bass.Bass("TRN2", target_bir_lowering=False)
    T = lambda n, s, d, k: nc.dram_tensor(n, s, d, kind=k).ap()
    EI, EO = "ExternalInput", "ExternalOutput"
    x_tm = T("x_tm", [TLOC, DM], F32, EI)
    xT = T("xT", [DM, TLOC], F32, EI)
    wqkv0 = T("wqkv0", [NQKV, 128, DM], F32, EI)
    bqkv = T("bqkv", [128, NQKV], F32, EI)
    wrout = T("wrout", [128, NDT * 8], F32, EI)
    wa = T("wa", [1, 128, DM], F32, EI)
    wbase = T("wbase", [NDT, 128, DM], F32, EI)
    bbase = T("bbase", [128, NDT], F32, EI)
    bstk = T("bstk", [NDT, 128, 128], F32, EI)
    sel8 = T("sel8", [8, 128], F32, EI)
    idf = T("idf", [128, 128], F32, EI)
    idb = T("idb", [128, 128], BF16, EI)
    ones1 = T("ones1", [1, 128], F32, EI)
    ones128 = T("ones128", [128, 1], F32, EI)
    ones128b = T("ones128b", [128, 1], BF16, EI)
    wo0 = T("wo0", [NDT, 128, 512], F32, EI)
    bo0 = T("bo0", [128, NDT], F32, EI)
    wqkv1 = T("wqkv1", [NQKV1, 128, DM], F32, EI)
    wnq = T("wnq", [128, 1], F32, EI)
    wnk = T("wnk", [128, 1], F32, EI)
    la_up = T("la_up", [128, NDT * 16], F32, EI)
    rel5 = T("rel5", [5, 128, 512], F32, EI)
    biast = T("biast", [128, H1 * 2 * SMAX * 4], F32, EI)
    kidx = T("kidx", [128, HKV * 2 * SMAX], I32, EI)
    cod_u = T("cod_u", [DM, HSH_PAD], I32, EI)
    amx_u = T("amx_u", [64, HSH_PAD], F32, EI)
    cod_d = T("cod_d", [HSH_PAD, DM], I32, EI)
    amx_d = T("amx_d", [22, DM], F32, EI)
    wo1 = T("wo1", [NDT, 128, DM], F32, EI)
    bup = T("bup", [128, NHT], F32, EI)
    bdn = T("bdn", [128, NDT], F32, EI)
    lbup = T("lbup", [16, HF], F32, EI)
    ladn = T("ladn", [128, NHT * 16], F32, EI)
    lbdn = T("lbdn", [16, DM], F32, EI)
    IN = "Internal"
    DK = EO if debug else IN
    k0c = T("k0c", [1024, 512], BF16, IN)
    v0c = T("v0c", [1024, 512], BF16, IN)
    KG0 = T("KG0", [8192, 512], BF16, IN)
    VG0 = T("VG0", [8192, 512], BF16, IN)
    g0_scr = T("g0_scr", [DM, TLOC], F32, DK)
    x1_scr = T("x1_scr", [DM, TLOC], F32, DK)
    xn1_scr = T("xn1_scr", [DM, TLOC], BF16, DK)
    q1_scr = T("q1_scr", [DM, TLOC], BF16, DK)
    k1c = T("k1c", [2048, 512], BF16, IN)
    v1c = T("v1c", [2048, 512], BF16, IN)
    KG1 = T("KG1", [16384, 512], BF16, IN)
    VG1 = T("VG1", [16384, 512], BF16, IN)
    wupc = T("wupc", [DM, HSH_PAD], BF16, IN)
    wdnc = T("wdnc", [HSH_PAD, DM], BF16, IN)
    if debug:
        k1d = T("k1d", [2048, 512], BF16, EO)
        v1d = T("v1d", [2048, 512], BF16, EO)
        wud = T("wud", [DM, HSH_PAD], BF16, EO)
        wdd = T("wdd", [HSH_PAD, DM], BF16, EO)
        dbg0 = T("dbg0", [128, 8192], F32, EO)
        x2d = T("x2d", [NHT * 128, TLOC], BF16, EO)
        wugd = T("wugd", [NCORES * DM, HSH_PAD], BF16, EO)
        wdgd = T("wdgd", [NCORES * HSH_PAD, DM], BF16, EO)
        u1d = T("u1d", [16, TLOC], BF16, EO)
        u2d = T("u2d", [16, TLOC], BF16, EO)
    WUPG = T("WUPG", [NCORES * DM, HSH_PAD], BF16, IN)
    WDNG = T("WDNG", [NCORES * HSH_PAD, DM], BF16, IN)
    o_scr = T("o_scr", [DM, TLOC], BF16, DK)
    g1_scr = T("g1_scr", [DM, TLOC], BF16, DK)
    outF = T("outF", [DM, TLOC], F32, EO)

    with contextlib.ExitStack() as ctx:
        sch = Sch(nc, ctx)
        S0 = lambda name, shape, dt=F32: ctx.enter_context(nc.sbuf_tensor(name, shape, dt))
        idb_s = S0("idb_s", [128, 128], BF16)
        ones1_s = S0("ones1_s", [1, 128], F32)
        ones128_s = S0("ones128_s", [128, 1], F32)
        ones128b_s = S0("ones128b_s", [128, 1], BF16)
        q0_sb = S0("q0_sb", [128, 4, TLOC], BF16)
        u1s = S0("u1s", [16, TLOC], BF16)
        s01 = contextlib.ExitStack()
        xn_res = s01.enter_context(nc.sbuf_tensor("xn0_res", [128, NDT, TLOC], BF16))

        ld0 = sch.dma('gpsimd', idb_s[:], idb[:], inc='ld')
        ld0 = sch.dma('gpsimd', ones1_s[:], ones1[:], inc='ld')
        ld0 = sch.dma('gpsimd', ones128_s[:], ones128[:], inc='ld')
        ld0 = sch.dma('gpsimd', ones128b_s[:], ones128b[:], inc='ld')
        LD_CONST = ld0

        block = ctx.enter_context(nc.Block())

        with contextlib.ExitStack() as pctx:
            S = lambda name, shape, dt=F32: pctx.enter_context(nc.sbuf_tensor(name, shape, dt))
            xsb = [S(f"ln_x{i}", [128, DM], F32) for i in range(2)]
            xnb = [S(f"ln_xn{i}", [128, DM], BF16) for i in range(2)]
            scr = S("ln_scr", [128, DM], F32)
            stats = S("ln_st", [128, 8 * 2], F32)
            pstb = [pctx.enter_context(nc.psum_tensor(f"ln_pstb{i}", [128, 1024], BF16))
                    for i in range(2)]
            sch.wait('tensor', 'ld', LD_CONST)
            fin = [None] * NT
            for t in range(NT):
                b = t % 2
                if t >= 2:
                    vv, av = fin[t - 2]
                    sch.wait('gpsimd', 've', vv)
                    sch.wait('gpsimd', 'ac', av)
                ld = sch.dma('gpsimd', xsb[b][:], x_tm[t * 128:(t + 1) * 128, :], inc='lnl%d' % b)
                sch.wait('vector', 'lnl%d' % b, ld)
                sch.wait('scalar', 'lnl%d' % b, ld)
                lnv = emit_ln_tile(sch, xsb[b], xnb[b], scr, stats[:, 8 * b:8 * (b + 1)])
                sch.wait('tensor', 've', lnv)
                for d in range(NDT):
                    par = (d // 8) % 2
                    if d >= 16:
                        sch.wait('tensor', 've', cp8[d - 16])
                    tv = sch.op('tensor', lambda e, d=d, pb=pstb[par], b=b: e.transpose(
                        pb[:, (d % 8) * 128:(d % 8) * 128 + 128],
                        xnb[b][:, d * 128:(d + 1) * 128], idb_s[:]), inc='pe')
                    if d % 8 == 7:
                        sch.wait('vector', 'pe', tv)
                        cv = sch.op('vector', lambda e, d=d, pb=pstb[par], t=t: e.tensor_copy(
                            xn_res[:, d - 7:d + 1, t * 128:(t + 1) * 128],
                            pb[:, 0:1024].rearrange("p (a b) -> p a b", a=8)),
                            inc='ve')
                        if d == 7:
                            cp8 = [0] * NDT
                        for z in range(d - 7, d + 1):
                            cp8[z] = cv
                fin[t] = (cp8[NDT - 1], sch.val('ac'))
            LN0_VE = sch.val('ve')
        sch.barrier()

        with contextlib.ExitStack() as pctx:
            S = lambda name, shape, dt=F32: pctx.enter_context(nc.sbuf_tensor(name, shape, dt))
            wsl = [S(f"a_wsl{i}", [128, DM], BF16) for i in range(2)]
            wrs = S("a_wrs", [128, NDT * 8], BF16)
            idf_s = S("a_idf", [128, 128], F32)
            sel_s = S("a_sel", [8, 128], BF16)
            bq_s = S("a_bq", [128, NQKV], F32)
            bb_s = S("a_bb", [128, NDT], F32)
            bst_s = S("a_bst", [128, NDT, 128], BF16)
            lg = S("a_lg", [8, TLOC], F32)
            lgt = S("a_lgt", [128, NT, 8], F32)
            cstat = S("a_cst", [128, NT, 8], F32)
            cmsk = S("a_cmsk", [128, 8], F32)
            cl2 = S("a_cl2", [128, 8], F32)
            cel = S("a_cel", [128, 8], F32)
            ctm = S("a_ctm", [128, NT, 8], BF16)
            cfm = S("a_cfm", [8, TLOC], BF16)
            crep = S("a_crep", [128, TLOC], F32)
            hw = S("a_hw", [128, TLOC], BF16)
            ev = [S(f"a_ev{i}", [128, TLOC], BF16) for i in range(2)]
            gv = [S(f"a_gv{i}", [128, TLOC], F32) for i in range(2)]
            ps = [pctx.enter_context(nc.psum_tensor(f"a_ps{i}", [128, 512], F32))
                  for i in range(2)]
            psr = pctx.enter_context(nc.psum_tensor("a_psr", [8, 512], F32))
            pstf = pctx.enter_context(nc.psum_tensor("a_pstf", [128, 512], F32))
            pstb2 = pctx.enter_context(nc.psum_tensor("a_pstb", [128, 1024], BF16))

            ld = sch.dma('gpsimd', wrs[:], wrout[:], inc='ld')
            ld = sch.dma('gpsimd', idf_s[:], idf[:], inc='ld')
            ld = sch.dma('gpsimd', sel_s[:], sel8[:], inc='ld')
            ld = sch.dma('gpsimd', bq_s[:], bqkv[:], inc='ld')
            ld = sch.dma('gpsimd', bb_s[:], bbase[:], inc='ld')
            ld = sch.dma('gpsimd', bst_s[:], bstk.transpose([1, 0, 2]), inc='ld')
            LD0 = ld

            sch.wait('tensor', 'ld', LD0)
            sch.wait('tensor', 've', LN0_VE)
            for c in range(2):
                if c == 1:
                    sch.wait('tensor', 've', vv)
                pairs = [(wrs[:, d * 8:(d + 1) * 8], xn_res[:, d, c * 512:(c + 1) * 512])
                         for d in range(NDT)]
                pv = sch.mm_group(pairs, psr[:], inc='pe')
                sch.wait('vector', 'pe', pv)
                vv = sch.op('vector', lambda e, c=c, lg=lg: e.tensor_copy(
                    lg[:, c * 512:(c + 1) * 512], psr[:]), inc='ve')
            sch.wait('tensor', 've', vv)
            for t in range(NT):
                tv = sch.op('tensor', lambda e, t=t, lg=lg: e.transpose(
                    pstf[:, 0:8], lg[0:8, t * 128:(t + 1) * 128], idf_s[0:8, 0:8]), inc='pe')
                sch.wait('vector', 'pe', tv)
                vv = sch.op('vector', lambda e, t=t: e.tensor_copy(lgt[:, t, :], pstf[:, 0:8]), inc='ve')
                sch.wait('tensor', 've', vv)
            for t in range(NT):
                st = cstat[:, t, :]
                l_ = lgt[:, t, :]
                v = sch.op('vector', lambda e, l_=l_, st=st: e.reduce_max(st[:, 0:1], l_, axis=AX), inc='ve')
                sch.wait('vector', 've', v)
                v = sch.op('vector', lambda e, l_=l_, st=st: e.tensor_scalar(
                    cmsk[:], l_, st[:, 0:1], -1e30, ALU.is_equal, ALU.mult), inc='ve')
                sch.wait('vector', 've', v)
                v = sch.op('vector', lambda e, l_=l_: e.tensor_add(cl2[:], l_, cmsk[:]), inc='ve')
                sch.wait('vector', 've', v)
                v = sch.op('vector', lambda e, st=st: e.reduce_max(st[:, 1:2], cl2[:], axis=AX), inc='ve')
                sch.wait('vector', 've', v)
                v = sch.op('vector', lambda e, st=st: e.tensor_scalar_mul(st[:, 2:3], st[:, 0:1], -1.0), inc='ve')
                sch.wait('scalar', 've', v)
                a = sch.op('scalar', lambda e, l_=l_, st=st: e.activation(
                    cel[:], l_, AF.Exp, bias=st[:, 2:3]), inc='ac')
                sch.wait('vector', 'ac', a)
                v = sch.op('vector', lambda e, st=st: e.tensor_sub(st[:, 3:4], st[:, 1:2], st[:, 0:1]), inc='ve')
                sch.wait('scalar', 've', v)
                a = sch.op('scalar', lambda e, st=st: e.activation(st[:, 4:5], st[:, 3:4], AF.Exp), inc='ac')
                sch.wait('vector', 'ac', a)
                v = sch.op('vector', lambda e, st=st: e.tensor_scalar_add(st[:, 5:6], st[:, 4:5], 1.0), inc='ve')
                sch.wait('vector', 've', v)
                v = sch.op('vector', lambda e, st=st: e.reciprocal(st[:, 6:7], st[:, 5:6]), inc='ve')
                sch.wait('vector', 've', v)
                v = sch.op('vector', lambda e, st=st: e.tensor_scalar_mul(st[:, 7:8], st[:, 6:7], 2.0), inc='ve')
                v = sch.op('vector', lambda e, l_=l_, st=st: e.tensor_scalar(
                    cmsk[:], l_, st[:, 1:2], None, ALU.is_ge), inc='ve')
                sch.wait('vector', 've', v)
                v = sch.op('vector', lambda e: e.tensor_mul(cl2[:], cel[:], cmsk[:]), inc='ve')
                sch.wait('vector', 've', v)
                v = sch.op('vector', lambda e, t=t, st=st: e.tensor_scalar(
                    ctm[:, t, :], cl2[:], st[:, 7:8], None, ALU.mult), inc='ve')
                sch.wait('tensor', 've', v)
                tv = sch.op('tensor', lambda e, t=t: e.transpose(
                    pstb2[0:8, 0:128], ctm[:, t, :], idb_s[:]), inc='pe')
                sch.wait('vector', 'pe', tv)
                v = sch.op('vector', lambda e, t=t: e.tensor_copy(
                    cfm[0:8, t * 128:(t + 1) * 128], pstb2[0:8, 0:128]), inc='ve')
                sch.wait('tensor', 've', v)
            for c in range(2):
                pv = sch.mm_group([(sel_s[:], cfm[0:8, c * 512:(c + 1) * 512])], ps[c][:], inc='pe')
                sch.wait('vector', 'pe', pv)
                vv = sch.op('vector', lambda e, c=c: e.tensor_copy(
                    crep[:, c * 512:(c + 1) * 512], ps[c][:]), inc='ve')
            nld = {0: 0, 1: 0}
            nld[0] += 1
            wa_v = sch.dma('gpsimd', wsl[0][:], wa[0, :, :], inc='alw0')
            sch.wait('tensor', 'alw0', wa_v)
            sch.wait('tensor', 've', vv)
            for c in range(2):
                pairs = [(wsl[0][:, d * 128:(d + 1) * 128], xn_res[:, d, c * 512:(c + 1) * 512])
                         for d in range(NDT)]
                pv = sch.mm_group(pairs, ps[c][:], inc='pe')
                sch.wait('vector', 'pe', pv)
                vv = sch.op('vector', lambda e, c=c: e.tensor_mul(
                    hw[:, c * 512:(c + 1) * 512], ps[c][:], crep[:, c * 512:(c + 1) * 512]), inc='ve')
            HW_DONE = vv

            stk = {0: 0, 1: 0}
            for o in range(NQKV):
                b = o % 2
                if o >= 2:
                    sch.wait('gpsimd', 'aev', 2 * (o - 1))
                else:
                    sch.wait('gpsimd', 'pe', sch.val('pe'))
                nld[b] += 1
                sch.dma('gpsimd', wsl[b][:], wqkv0[o, :, :], inc='alw%d' % b)
                sch.wait('tensor', 'alw%d' % b, 16 * nld[b])
                if o == 0:
                    sch.wait('tensor', 've', HW_DONE)
                for c in range(2):
                    u = 2 * o + c
                    pairs = [(wsl[b][:, d * 128:(d + 1) * 128],
                              xn_res[:, d, c * 512:(c + 1) * 512]) for d in range(NDT)]
                    if u >= 2:
                        sch.wait('tensor', 'aev', u - 1)
                    pv = sch.mm_group(pairs, ps[c][:], inc='pe')
                    sch.wait('vector', 'pe', pv)
                    if o < 4:
                        v = sch.op('vector', lambda e, o=o, c=c: e.tensor_scalar(
                            q0_sb[:, o, c * 512:(c + 1) * 512], ps[c][:],
                            bq_s[:, o:o + 1], None, ALU.add), inc='aev')
                    else:
                        b2 = o % 2
                        if c == 0:
                            sch.wait('vector', 'astk%d' % b2, stk[b2])
                        v = sch.op('vector', lambda e, o=o, c=c, b2=b2: e.tensor_scalar(
                            ev[b2][:, c * 512:(c + 1) * 512], ps[c][:],
                            bq_s[:, o:o + 1], None, ALU.add), inc='aev')
                        if c == 1:
                            dstT = k0c if o < 8 else v0c
                            fo = (o - 4) % 4
                            sch.wait('sync', 'aev', v)
                            sv = sch.dma('sync', dstT[fo * 128:fo * 128 + 128, :],
                                         ev[b2][:, 0:512], inc='astk%d' % b2)
                            sv = sch.dma('sync', dstT[512 + fo * 128:512 + fo * 128 + 128, :],
                                         ev[b2][:, 512:1024], inc='astk%d' % b2)
                            stk[b2] = sv
                if o == 7:
                    sch.wait('gpsimd', 'astk0', sch.val('astk0'))
                    sch.wait('gpsimd', 'astk1', sch.val('astk1'))
                    sch.collective(k0c, KG0, 'ccK0')
                if o == 11:
                    sch.wait('gpsimd', 'astk0', sch.val('astk0'))
                    sch.wait('gpsimd', 'astk1', sch.val('astk1'))
                    sch.collective(v0c, VG0, 'ccV0')

            stg = {0: 0, 1: 0}
            for o in range(NDT):
                b = o % 2
                sch.wait('gpsimd', 'aev', 24 + 2 * max(o - 1, 0))
                nld[b] += 1
                sch.dma('gpsimd', wsl[b][:], wbase[o, :, :], inc='alw%d' % b)
                sch.wait('tensor', 'alw%d' % b, 16 * nld[b])
                for c in range(2):
                    u = 24 + 2 * o + c
                    pairs = [(wsl[b][:, d * 128:(d + 1) * 128],
                              xn_res[:, d, c * 512:(c + 1) * 512]) for d in range(NDT)]
                    pairs.append((bst_s[:, o, :], hw[:, c * 512:(c + 1) * 512]))
                    sch.wait('tensor', 'aev', u - 1)
                    pv = sch.mm_group(pairs, ps[c][:], inc='pe')
                    sch.wait('scalar', 'pe', pv)
                    b2 = o % 2
                    if c == 0:
                        sch.wait('scalar', 'astg%d' % b2, stg[b2])
                    v = sch.op('scalar', lambda e, o=o, c=c, b2=b2: e.activation(
                        gv[b2][:, c * 512:(c + 1) * 512], ps[c][:], AF.Silu,
                        bias=bb_s[:, o:o + 1]), inc='aev')
                    if c == 1:
                        sch.wait('sync', 'aev', v)
                        stg[b2] = sch.dma('sync', g0_scr[o * 128:(o + 1) * 128, :],
                                          gv[b2][:], inc='astg%d' % b2)
            sch.wait('sync', 'astg0', sch.val('astg0'))
            sch.wait('sync', 'astg1', sch.val('astg1'))
            if debug:
                sch.wait('gpsimd', 'aev', sch.val('aev'))
                sch.wait('gpsimd', 've', sch.val('ve'))
                dv = sch.dma('gpsimd', dbg0[0:128, 0:1024], crep[:], inc='dbg0')
                dv = sch.dma('gpsimd', dbg0[0:8, 1024:2048], lg[:], inc='dbg0')
                dv = sch.dma('gpsimd', dbg0[0:8, 2048:3072], cfm[:], inc='dbg0')
                dv = sch.dma('gpsimd',
                             dbg0[0:128, 3072:7168].rearrange("p (d t) -> p d t", d=32),
                             xn_res[:, :, 896:1024], inc='dbg0')
                dv = sch.dma('gpsimd', dbg0[0:128, 7168:8192], hw[:], inc='dbg0')
                sch.wait('gpsimd', 'dbg0', dv)
            L0A_PE = sch.val('pe')

        s01.close()
        sch.barrier()

        o_sb0 = ctx.enter_context(nc.sbuf_tensor("o_sb0", [128, 4, TLOC], BF16))
        with contextlib.ExitStack() as pctx:
            S = lambda name, shape, dt: pctx.enter_context(nc.sbuf_tensor(name, shape, dt))
            kp = S("b_kp", [128, N], BF16)
            vf = S("b_vf", [128, N], BF16)
            vp = S("b_vp", [128, NKT, 128], BF16)
            att = S("b_att", [128, N], BF16)
            attn = S("b_attn", [128, N], BF16)
            panels = S("b_pan", [128, NKT, 256], BF16)
            mcols = S("b_mc", [128, NCH + 8], F32)
            zcols = S("b_zc", [128, NCH + 8], F32)
            dq = Dequant(sch, lambda n, s, d: pctx.enter_context(nc.sbuf_tensor(n, s, d)),
                         cod_u, amx_u, cod_d, amx_d, wupc, wdnc)
            pss = [pctx.enter_context(nc.psum_tensor(f"b_pss{i}", [128, 1024], F32))
                   for i in range(2)]
            pstb = [pctx.enter_context(nc.psum_tensor(f"b_pstb{i}", [128, 1024], BF16))
                    for i in range(2)]
            pso = pctx.enter_context(nc.psum_tensor("b_pso", [64, 256], F32))

            kv_users_ac = 0
            prev_exp = 0
            sch.cwait('gpsimd', 'ccK0')
            sch.cwait('gpsimd', 'ccV0')
            nsh = sch.op('vector', lambda e: e.tensor_scalar_mul(
                mcols[:, NCH + 1:NCH + 2], ones128_s[:], -50.0), inc='ve')
            sch.wait('scalar', 've', nsh)
            for p in range(4):
                if p > 0:
                    sch.wait('gpsimd', 'ac', kv_users_ac)
                    sch.wait('gpsimd', 'pe', sch.val('pe'))
                for cr in range(NCH):
                    lk = sch.dma('gpsimd', kp[:, cr * 512:(cr + 1) * 512],
                                 KG0[cr * 512 + p * 128:cr * 512 + (p + 1) * 128, :],
                                 inc='bldk')
                for cr in range(NCH):
                    lk = sch.dma('gpsimd', vf[:, cr * 512:(cr + 1) * 512],
                                 VG0[cr * 512 + p * 128:cr * 512 + (p + 1) * 128, :],
                                 inc='bldk')
                sch.wait('tensor', 'bldk', lk)
                for kt in range(NKT):
                    par = (kt // 8) % 2
                    if kt >= 16:
                        sch.wait('tensor', 've', vcp8[kt - 16])
                    elif p > 0 and kt < 16:
                        sch.wait('tensor', 've', prev_pan_last)
                    tv = sch.op('tensor', lambda e, kt=kt, pb=pstb[par]: e.transpose(
                        pb[:, (kt % 8) * 128:(kt % 8) * 128 + 128],
                        vf[:, kt * 128:(kt + 1) * 128], idb_s[:]), inc='pe')
                    if kt % 8 == 7:
                        sch.wait('vector', 'pe', tv)
                        cv = sch.op('vector', lambda e, kt=kt, pb=pstb[par]: e.tensor_copy(
                            vp[:, kt - 7:kt + 1, :],
                            pb[:, 0:1024].rearrange("p (a b) -> p a b", a=8)),
                            inc='ve')
                        if kt == 7:
                            vcp8 = [0] * NKT
                        for z in range(kt - 7, kt + 1):
                            vcp8[z] = cv
                VP_VE = vcp8[NKT - 1]
                for hh in range(2):
                    qh_ = q0_sb[hh * 64:(hh + 1) * 64, :, :]
                    kh = kp[hh * 64:(hh + 1) * 64, :]
                    for qtg in range(4):
                        pan_v = None
                        for qt in range(2):
                            tqi = qtg * 2 + qt
                            lhs = qh_[:, p, tqi * 128:(tqi + 1) * 128]
                            exp_v = [0] * (NCH // 2)
                            for cp in range(NCH // 2):
                                b = cp % 2
                                if cp >= 2:
                                    sch.wait('tensor', 'ac', exp_v[cp - 2])
                                else:
                                    sch.wait('tensor', 'ac', prev_exp)
                                sch.mm(pss[b][:, 0:512], lhs,
                                       kh[:, (2 * cp) * 512:(2 * cp + 1) * 512],
                                       start=True, stop=True)
                                pv = sch.mm(pss[b][:, 512:1024], lhs,
                                            kh[:, (2 * cp + 1) * 512:(2 * cp + 2) * 512],
                                            start=True, stop=True, inc='pe')
                                sch.wait('scalar', 'pe', pv)
                                exp_v[cp] = sch.op('scalar', lambda e, cp=cp, pb=pss[b], att=att,
                                                   mcols=mcols: e.activation(
                                    att[:, cp * 1024:(cp + 1) * 1024], pb[:], AF.Exp,
                                    bias=mcols[:, NCH + 1:NCH + 2],
                                    accum_out=zcols[:, cp:cp + 1]), inc='ac')
                            prev_exp = exp_v[NCH // 2 - 1]
                            sch.wait('vector', 'ac', prev_exp)
                            v = sch.op('vector', lambda e: e.reduce_sum(
                                zcols[:, NCH:NCH + 1], zcols[:, 0:NCH // 2], axis=AX), inc='ve')
                            sch.wait('vector', 've', v)
                            v = sch.op('vector', lambda e: e.reciprocal(
                                zcols[:, NCH + 1:NCH + 2], zcols[:, NCH:NCH + 1]), inc='ve')
                            sch.wait('vector', 've', v)
                            v = sch.op('vector', lambda e: e.tensor_scalar_mul(
                                zcols[:, NCH + 2:NCH + 3], zcols[:, NCH + 1:NCH + 2],
                                1.0 / math.sqrt(512.0)), inc='ve')
                            sch.wait('vector', 've', v)
                            normv = sch.op('vector', lambda e, att=att, attn=attn: e.tensor_scalar(
                                attn[:], att[:], zcols[:, NCH + 2:NCH + 3], None, ALU.mult), inc='ve')
                            sch.wait('tensor', 've', normv)
                            if hh == 0 and qtg == 0 and qt == 0:
                                sch.wait('tensor', 've', VP_VE)
                            for kt in range(NKT):
                                half = (kt // 4) % 2
                                if kt >= 8:
                                    sch.wait('tensor', 've', pan_cp[kt - 8])
                                tv = sch.op('tensor', lambda e, kt=kt, pb=pstb[half], attn=attn: e.transpose(
                                    pb[:, (kt % 4) * 128:(kt % 4) * 128 + 128],
                                    attn[:, kt * 128:(kt + 1) * 128], idb_s[:]), inc='pe')
                                if kt % 4 == 3:
                                    sch.wait('vector', 'pe', tv)
                                    pv4 = sch.op('vector', lambda e, kt=kt, pb=pstb[half], qt=qt,
                                                 panels=panels: e.tensor_copy(
                                        panels[:, kt - 3:kt + 1, qt * 128:(qt + 1) * 128],
                                        pb[:, 0:512]
                                        .rearrange("p (a b) -> p a b", a=4)), inc='ve')
                                    if kt == 3:
                                        pan_cp = [0] * NKT
                                    for z_ in range(kt - 3, kt + 1):
                                        pan_cp[z_] = pv4
                            pan_v = pan_cp[NKT - 1]
                        prev_pan_last = pan_v
                        sch.wait('tensor', 've', pan_v)
                        pairs = [(vp[:, kt, hh * 64:(hh + 1) * 64], panels[:, kt, :])
                                 for kt in range(NKT)]
                        pvv = sch.mm_group(pairs, pso[:], inc='pe')
                        sch.wait('vector', 'pe', pvv)
                        ov = sch.op('vector', lambda e, hh=hh, p=p, qtg=qtg, pso=pso: e.tensor_copy(
                            o_sb0[hh * 64:(hh + 1) * 64, p, qtg * 256:(qtg + 1) * 256],
                            pso[:]), inc='ve')
                        sch.wait('tensor', 've', ov)
                        kv_users_ac = sch.val('ac')
                    dq.emit(4)

        sch.barrier()

        s23 = contextlib.ExitStack()
        xn1_used_as = s23.enter_context(
            nc.sbuf_tensor("xn1_res", [128, NDT, TLOC], BF16))
        with contextlib.ExitStack() as pctx:
            S = lambda name, shape, dt=F32: pctx.enter_context(nc.sbuf_tensor(name, shape, dt))
            wo_sb = S("c_wo", [128, NDT, 512], BF16)
            bo_s = S("c_bo", [128, NDT], F32)
            x1hold = S("c_x1", [128, NDT, 512], F32)
            xTt = [S(f"c_xt{i}", [128, 512], F32) for i in range(2)]
            g0t = [S(f"c_g{i}", [128, 512], F32) for i in range(2)]
            fb1 = [S(f"c_f1{i}", [128, 512], F32) for i in range(2)]
            sqb = [S(f"c_sq{i}", [128, 512], F32) for i in range(2)]
            pscb = [pctx.enter_context(nc.psum_tensor(f"c_ps{i}", [128, 512], F32))
                    for i in range(2)]
            psm = [pctx.enter_context(nc.psum_tensor(f"c_psm{i}", [1, 512], F32))
                   for i in range(2)]
            psq = [pctx.enter_context(nc.psum_tensor(f"c_psq{i}", [1, 512], F32))
                   for i in range(2)]
            psmb = pctx.enter_context(nc.psum_tensor("c_psmb", [128, 512], F32))
            psrb = pctx.enter_context(nc.psum_tensor("c_psrb", [128, 512], F32))
            rowv = S("c_rowv", [1, 512 * 3], F32)

            lw = sch.dma('gpsimd', wo_sb[:], wo0.transpose([1, 0, 2]), inc='clw')
            lw = sch.dma('gpsimd', bo_s[:], bo0[:], inc='clw')
            sch.wait('tensor', 'clw', lw)
            cmb_v = {0: 0, 1: 0}
            sq_a = {0: 0, 1: 0}
            qs_pe = {0: 0, 1: 0}
            MQ = 0
            for ch in range(2):
                if ch == 1:
                    sch.wait('vector', 'pe', MQ)
                for o in range(NDT):
                    b = o % 2
                    u = ch * NDT + o
                    if u >= 2:
                        sch.wait('gpsimd', 've', cmb_v[b])
                    lx = sch.dma('gpsimd', xTt[b][:],
                                 xT[o * 128:(o + 1) * 128, ch * 512:(ch + 1) * 512],
                                 inc='cldx%d' % b)
                    lx = sch.dma('gpsimd', g0t[b][:],
                                 g0_scr[o * 128:(o + 1) * 128, ch * 512:(ch + 1) * 512],
                                 inc='cldx%d' % b)
                    if u >= 2:
                        sch.wait('tensor', 've', cmb_v[b])
                        sch.wait('tensor', 'ac', sq_a[b])
                    pairs = [(wo_sb[:, o, j * 128:(j + 1) * 128],
                              o_sb0[:, j, ch * 512:(ch + 1) * 512]) for j in range(4)]
                    pv = sch.mm_group(pairs, pscb[b][:], inc='pe')
                    sch.wait('vector', 'pe', pv)
                    sch.wait('vector', 'cldx%d' % b, lx)
                    v = sch.op('vector', lambda e, b=b, o=o: e.tensor_scalar(
                        fb1[b][:], pscb[b][:], bo_s[:, o:o + 1], None, ALU.add), inc='ve')
                    sch.wait('vector', 've', v)
                    v = sch.op('vector', lambda e, b=b: e.tensor_mul(
                        fb1[b][:], fb1[b][:], g0t[b][:]), inc='ve')
                    sch.wait('vector', 've', v)
                    v = sch.op('vector', lambda e, b=b, o=o: e.tensor_add(
                        x1hold[:, o, :], fb1[b][:], xTt[b][:]), inc='ve')
                    cmb_v[b] = v
                    sch.wait('sync', 've', v)
                    sch.dma('sync', x1_scr[o * 128:(o + 1) * 128, ch * 512:(ch + 1) * 512],
                            x1hold[:, o, :], inc='cstx')
                    sch.wait('scalar', 've', v)
                    if qs_pe[b]:
                        sch.wait('scalar', 'pe', qs_pe[b])
                    a = sch.op('scalar', lambda e, b=b, o=o: e.activation(
                        sqb[b][:], x1hold[:, o, :], AF.Square), inc='ac')
                    sq_a[b] = a
                    sch.wait('tensor', 've', v)
                    sch.mm(psm[ch][:], ones128_s[:], x1hold[:, o, :],
                           start=(o == 0), stop=(o == NDT - 1), inc='pe')
                    sch.wait('tensor', 'ac', a)
                    qs_pe[b] = sch.mm(psq[ch][:], ones128_s[:], sqb[b][:],
                                      start=(o == 0), stop=(o == NDT - 1),
                                      inc='pe', always=True)
                MQ = sch.val('pe')
                sch.wait('vector', 'pe', MQ)
                rr = rowv[0:1, 0:512]
                rs = rowv[0:1, 512:1024]
                v = sch.op('vector', lambda e, ch=ch, rr=rr: e.tensor_scalar_mul(
                    rr, psm[ch][:], 1.0 / DM), inc='ve')
                sch.wait('vector', 've', v)
                v = sch.op('vector', lambda e, ch=ch, rr=rr, rs=rs: e.tensor_scalar(
                    rs, psq[ch][:], 1.0 / DM, None, ALU.mult), inc='ve')
                sch.wait('vector', 've', v)
                v = sch.op('vector', lambda e, rr=rr, rowv=rowv: e.tensor_mul(
                    rowv[0:1, 1024:1536], rr, rr), inc='ve')
                sch.wait('vector', 've', v)
                v = sch.op('vector', lambda e, rs=rs, rowv=rowv: e.tensor_sub(
                    rs, rs, rowv[0:1, 1024:1536]), inc='ve')
                sch.wait('vector', 've', v)
                v = sch.op('vector', lambda e, rs=rs: e.tensor_scalar_add(rs, rs, 1e-5), inc='ve')
                sch.wait('vector', 've', v)
                v = sch.op('vector', lambda e, rs=rs: e.reciprocal(rs, rs), inc='ve')
                sch.wait('scalar', 've', v)
                a = sch.op('scalar', lambda e, rs=rs: e.activation(rs, rs, AF.Sqrt), inc='ac')
                sch.wait('tensor', 'ac', a)
                if ch == 1:
                    sch.wait('tensor', 've', NRM_VE)
                sch.mm(psmb[:], ones1_s[:], rr, start=True, stop=True, inc='pe')
                bv = sch.mm(psrb[:], ones1_s[:], rs, start=True, stop=True, inc='pe')
                sch.wait('vector', 'pe', bv)
                for o in range(NDT):
                    b = o % 2
                    v = sch.op('vector', lambda e, b=b, o=o: e.tensor_sub(
                        fb1[b][:], x1hold[:, o, :], psmb[:]), inc='ve')
                    sch.wait('vector', 've', v)
                    v = sch.op('vector', lambda e, b=b, o=o, ch=ch: e.tensor_mul(
                        xn1_used_as[:, o, ch * 512:(ch + 1) * 512], fb1[b][:], psrb[:]), inc='ve')
                    sch.wait('sync', 've', v)
                    sch.dma('sync', xn1_scr[o * 128:(o + 1) * 128, ch * 512:(ch + 1) * 512],
                            xn1_used_as[:, o, ch * 512:(ch + 1) * 512], inc='cstx')
                NRM_VE = sch.val('ve')
            sch.wait('sync', 'cstx', sch.val('cstx'))

        sch.barrier()

        with contextlib.ExitStack() as pctx:
            S = lambda name, shape, dt=F32: pctx.enter_context(nc.sbuf_tensor(name, shape, dt))
            wsl = [S(f"e_wsl{i}", [128, DM], BF16) for i in range(2)]
            la_s = S("e_la", [128, NDT * 16], BF16)
            wn_s = S("e_wn", [128, 2], F32)
            sqa = [S(f"e_sq{i}", [128, 512], BF16) for i in range(2)]
            tmpA = [S(f"e_tmp{i}", [128, 512], F32) for i in range(2)]
            qev = [S(f"e_qe{i}", [128, 512], BF16) for i in range(2)]
            rwa = S("e_rw", [1, 1024], F32)
            dq2 = Dequant.__new__(Dequant)
            dq2.sch = sch
            dq2.cod_i = S("e_dqc", [128, 2048], I32)
            dq2.codb = S("e_dqcb", [128, 2048], BF16)
            dq2.amx = S("e_dqa", [128, 2048], F32)
            dq2.dacc = [S(f"e_dqd{i}", [128, 2048], BF16) for i in range(2)]
            dq2.dtmp = S("e_dqt", [128, 2048], BF16)
            dq2.units = dq.units
            dq2.i = dq.i
            dq2.cv_prev = 0
            dq2.st = {0: 0, 1: 0}
            psA = [pctx.enter_context(nc.psum_tensor(f"e_ps{i}", [128, 512], F32))
                   for i in range(2)]
            psn = [pctx.enter_context(nc.psum_tensor(f"e_psn{i}", [1, 512], F32))
                   for i in range(2)]
            psrbA = [pctx.enter_context(nc.psum_tensor(f"e_psrb{i}", [128, 512], F32))
                     for i in range(2)]
            psu = pctx.enter_context(nc.psum_tensor("e_psu", [16, 512], F32))

            ld = sch.dma('gpsimd', la_s[:], la_up[:], inc='eld')
            ld = sch.dma('gpsimd', wn_s[:, 0:1], wnq[:], inc='eld')
            ld = sch.dma('gpsimd', wn_s[:, 1:2], wnk[:], inc='eld')
            sch.wait('tensor', 'eld', ld)
            for c in range(2):
                if c == 1:
                    sch.wait('tensor', 've', uv)
                pairs = [(la_s[:, d * 16:(d + 1) * 16], xn1_used_as[:, d, c * 512:(c + 1) * 512])
                         for d in range(NDT)]
                pv = sch.mm_group(pairs, psu[:], inc='pe')
                sch.wait('vector', 'pe', pv)
                uv = sch.op('vector', lambda e, c=c: e.tensor_copy(
                    u1s[0:16, c * 512:(c + 1) * 512], psu[:]), inc='ve')

            nld1 = {0: 0, 1: 0}
            eev = {0: 0, 1: 0}
            est = {0: 0, 1: 0}
            eac = {0: 0, 1: 0}
            for o in range(NQKV1):
                b = o % 2
                if o >= 2:
                    sch.wait('gpsimd', 've', eev[0])
                    sch.wait('gpsimd', 've', eev[1])
                nld1[b] += 1
                sch.dma('gpsimd', wsl[b][:], wqkv1[o, :, :], inc='elw%d' % b)
                sch.wait('tensor', 'elw%d' % b, 16 * nld1[b])
                for c in range(2):
                    b2 = (2 * o + c) % 2
                    pairs = [(wsl[b][:, d * 128:(d + 1) * 128],
                              xn1_used_as[:, d, c * 512:(c + 1) * 512]) for d in range(NDT)]
                    if 2 * o + c >= 2:
                        sch.wait('tensor', 've', eev[b2])
                    pv = sch.mm_group(pairs, psA[b2][:], inc='pe')
                    if o < 40:
                        sch.wait('scalar', 'pe', pv)
                        if eac[b2]:
                            sch.wait('scalar', 've', eev[b2])
                        a = sch.op('scalar', lambda e, b2=b2: e.activation(
                            sqa[b2][:], psA[b2][:], AF.Square), inc='ac')
                        eac[b2] = a
                        sch.wait('tensor', 'ac', a)
                        nv = sch.mm(psn[b2][:], ones128b_s[:], sqa[b2][:],
                                    start=True, stop=True, inc='pe')
                        sch.wait('vector', 'pe', nv)
                        rr = rwa[0:1, b2 * 512:b2 * 512 + 512]
                        v = sch.op('vector', lambda e, b2=b2, rr=rr: e.tensor_scalar(
                            rr, psn[b2][:], 1.0 / 128.0, 1e-6, ALU.mult, ALU.add), inc='ve')
                        sch.wait('vector', 've', v)
                        v = sch.op('vector', lambda e, rr=rr: e.reciprocal(rr, rr), inc='ve')
                        sch.wait('scalar', 've', v)
                        a = sch.op('scalar', lambda e, rr=rr: e.activation(rr, rr, AF.Sqrt), inc='ac')
                        if o < 32:
                            sch.wait('vector', 'ac', a)
                            v = sch.op('vector', lambda e, rr=rr: e.tensor_scalar_mul(
                                rr, rr, 1.0 / math.sqrt(HD)), inc='ve')
                            sch.wait('tensor', 've', v)
                        else:
                            sch.wait('tensor', 'ac', a)
                        bv = sch.mm(psrbA[b2][:], ones1_s[:], rr, start=True, stop=True, inc='pe')
                        sch.wait('vector', 'pe', bv)
                        wcol = wn_s[:, 0:1] if o < 32 else wn_s[:, 1:2]
                        v = sch.op('vector', lambda e, b2=b2, wcol=wcol: e.tensor_scalar(
                            tmpA[b2][:], psA[b2][:], wcol, None, ALU.mult), inc='ve')
                        sch.wait('vector', 've', v)
                        if est[b2]:
                            sch.wait('vector', 'est%d' % b2, est[b2])
                        v = sch.op('vector', lambda e, b2=b2: e.tensor_mul(
                            qev[b2][:], tmpA[b2][:], psrbA[b2][:]), inc='ve')
                    else:
                        sch.wait('vector', 'pe', pv)
                        if est[b2]:
                            sch.wait('vector', 'est%d' % b2, est[b2])
                        v = sch.op('vector', lambda e, b2=b2: e.tensor_copy(
                            qev[b2][:], psA[b2][:]), inc='ve')
                    eev[b2] = v
                    sch.wait('sync', 've', v)
                    if o < 32:
                        dst = q1_scr[o * 128:(o + 1) * 128, c * 512:(c + 1) * 512]
                    elif o < 40:
                        fo = o - 32
                        dst = k1c[c * 1024 + fo * 128:c * 1024 + (fo + 1) * 128, :]
                    else:
                        fo = o - 40
                        dst = v1c[c * 1024 + fo * 128:c * 1024 + (fo + 1) * 128, :]
                    est[b2] = sch.dma('sync', dst, qev[b2][:], inc='est%d' % b2)
                if o == 1:
                    sch.wait('gpsimd', 'dqst0', dq.st[0])
                    sch.wait('gpsimd', 'dqst1', dq.st[1])
                    sch.collective(wupc, WUPG, 'ccWU')
                if o == 39:
                    sch.wait('gpsimd', 'est0', sch.val('est0'))
                    sch.wait('gpsimd', 'est1', sch.val('est1'))
                    sch.collective(k1c, KG1, 'ccK1')
                if o == 47:
                    sch.wait('gpsimd', 'est0', sch.val('est0'))
                    sch.wait('gpsimd', 'est1', sch.val('est1'))
                    sch.collective(v1c, VG1, 'ccV1')
                if o >= 4 and o % 2 == 0:
                    dq2.emit(1)
            dq2.emit(54)
            sch.wait('gpsimd', 'dqst0', dq2.st[0])
            sch.wait('gpsimd', 'dqst1', dq2.st[1])
            sch.collective(wdnc, WDNG, 'ccWD')

        s23.close()
        sch.barrier()

        with contextlib.ExitStack() as pctx:
            S = lambda name, shape, dt: pctx.enter_context(nc.sbuf_tensor(name, shape, dt))
            ksb = [S(f"f_k{i}", [128, SMAX * 512], BF16) for i in range(2)]
            vsb = [S(f"f_v{i}", [128, SMAX * 4, 128], BF16) for i in range(2)]
            vstg = [S(f"f_vs{i}", [128, 512], BF16) for i in range(2)]
            panels = S("f_pan", [128, SMAX * 4, 256], BF16)
            att = S("f_att", [128, SMAX * 512], BF16)
            qh = [S(f"f_qh{i}", [128, TLOC], BF16) for i in range(2)]
            scrb = [S(f"f_sc{i}", [128, 512], F32) for i in range(2)]
            zc = S("f_zc", [128, 24], F32)
            ob = [S(f"f_ob{i}", [128, 512], BF16) for i in range(2)]
            bia_s = S("f_bia", [128, H1 * 2 * SMAX * 4], F32)
            rel_s = S("f_rel", [128, 5, 512], F32)
            rels = S("f_rels", [128, 5, 512], F32)
            kidx_s = S("f_kidx", [128, HKV * 2 * SMAX], I32)
            pss = [pctx.enter_context(nc.psum_tensor(f"f_pss{i}", [128, 512], F32))
                   for i in range(2)]
            pstb = [pctx.enter_context(nc.psum_tensor(f"f_pstb{i}", [128, 1024], BF16))
                    for i in range(2)]
            pso = pctx.enter_context(nc.psum_tensor("f_pso", [128, 512], F32))

            ld0 = sch.dma('gpsimd', bia_s[:], biast[:], inc='fld')
            ld0 = sch.dma('gpsimd', rel_s[:], rel5.transpose([1, 0, 2]), inc='fld')
            ld0 = sch.dma('gpsimd', kidx_s[:], kidx[:], inc='fld')
            sch.wait('tensor', 'fld', ld0)
            sch.wait('scalar', 'fld', ld0)
            sch.wait('vector', 'fld', ld0)
            sch.cwait('gpsimd', 'ccK1')
            sch.cwait('gpsimd', 'ccV1')
            kv_done = 0
            kv_done_prev = 0
            qh_done = {0: 0, 1: 0}
            st_o = {0: 0, 1: 0}
            exp_prev = {0: 0, 1: 0}
            vstg_pe = {0: 0, 1: 0}
            hq_idx = 0
            for kv in range(HKV):
                for gi in range(2):
                    it = kv * 2 + gi
                    bkv = it % 2
                    S_g = (SKV_A if gi == 0 else SKV_B)[kv]
                    if it >= 2:
                        sch.wait('gpsimd', 'pe', kv_done_prev)
                    for s in range(S_g):
                        col = (kv * 2 + gi) * SMAX + s
                        sch.indirect(ksb[bkv][:, s * 512:(s + 1) * 512], KG1[0:128, :],
                                     kidx_s[:, col:col + 1], inc='fldk%d' % bkv)
                    lk = None
                    for s in range(S_g):
                        vb = s % 2
                        col = (kv * 2 + gi) * SMAX + s
                        if vstg_pe[vb]:
                            sch.wait('gpsimd', 'pe', vstg_pe[vb])
                        lv = sch.indirect(vstg[vb][:], VG1[0:128, :],
                                          kidx_s[:, col:col + 1], inc='fldv%d' % vb)
                        sch.wait('tensor', 'fldv%d' % vb, lv)
                        par = s % 2
                        if it >= 1 or s >= 2:
                            sch.wait('tensor', 've', vq_cp.get((bkv, s - 2), prev_pan8))
                        tvv = None
                        for kb in range(4):
                            tvv = sch.op('tensor', lambda e, kb=kb, par=par, vb=vb: e.transpose(
                                pstb[par][:, kb * 128:(kb + 1) * 128],
                                vstg[vb][:, kb * 128:(kb + 1) * 128], idb_s[:]), inc='pe')
                        vstg_pe[vb] = tvv
                        sch.wait('vector', 'pe', tvv)
                        cvv = sch.op('vector', lambda e, s=s, par=par, bkv=bkv: e.tensor_copy(
                            vsb[bkv][:, s * 4:(s + 1) * 4, :],
                            pstb[par][:, 0:512].rearrange("p (a b) -> p a b", a=4)), inc='ve')
                        if it == 0 and s == 0:
                            vq_cp = {}
                            prev_pan8 = 0
                        vq_cp[(bkv, s)] = cvv
                    sch.wait('tensor', 'fldk%d' % bkv, sch.val('fldk%d' % bkv))
                    sch.wait('tensor', 've', cvv)
                    kv_done_prev = kv_done
                    for hq in range(4):
                        h = kv * 4 + hq
                        bq = hq_idx % 2
                        if hq_idx >= 2:
                            sch.wait('gpsimd', 'pe', qh_done[bq])
                        lq = sch.dma('gpsimd', qh[bq][:], q1_scr[h * 128:(h + 1) * 128, :],
                                     inc='fldq%d' % bq)
                        S_h = SH_TAB[h][gi]
                        s0 = S_g - S_h
                        sch.wait('vector', 'fldq%d' % bq, lq)
                        sch.wait('vector', 'pe', kv_done)
                        rv = None
                        for vv5 in range(5):
                            rv = sch.op('vector', lambda e, vv5=vv5, h=h: e.tensor_scalar(
                                rels[:, vv5, :], rel_s[:, vv5, :], float(SLOPES[h]),
                                None, ALU.mult), inc='ve')
                        sch.wait('tensor', 've', rv)
                        hq_idx += 1
                        pan_last = None
                        pv_half = 0
                        for j in range(4):
                            jj = j % 2
                            lhs = qh[bq][:, gi * 512 + j * 128: gi * 512 + (j + 1) * 128]
                            exp_v = {}
                            for si in range(S_h):
                                s = s0 + si
                                b = si % 2
                                sch.wait('tensor', 'ac', exp_prev[b])
                                pv = sch.mm_group(
                                    [(lhs, ksb[bkv][:, s * 512:(s + 1) * 512])],
                                    pss[b][:], inc='pe')
                                variant = (1 + j) if s == S_g - 1 else 0
                                sch.wait('vector', 'pe', pv)
                                vv = sch.op('vector', lambda e, b=b, variant=variant:
                                            e.scalar_tensor_tensor(
                                                scrb[b][:], pss[b][:], 0.0,
                                                rels[:, variant, :],
                                                ALU.add, ALU.add), inc='ve')
                                sch.wait('scalar', 've', vv)
                                col = bias_col(h, gi, s, j)
                                av = sch.op('scalar', lambda e, b=b, si=si, col=col:
                                            e.activation(
                                                att[:, si * 512:(si + 1) * 512],
                                                scrb[b][:], AF.Exp,
                                                bias=bia_s[:, col:col + 1],
                                                accum_out=zc[:, si:si + 1]), inc='ac')
                                exp_v[si] = av
                                exp_prev[b] = av
                            sch.wait('vector', 'ac', exp_v[S_h - 1])
                            v = sch.op('vector', lambda e, S_h=S_h: e.reduce_sum(
                                zc[:, 16:17], zc[:, 0:S_h], axis=AX), inc='ve')
                            sch.wait('vector', 've', v)
                            v = sch.op('vector', lambda e: e.reciprocal(
                                zc[:, 17:18], zc[:, 16:17]), inc='ve')
                            sch.wait('vector', 've', v)
                            normv = sch.op('vector', lambda e, S_h=S_h: e.tensor_scalar(
                                att[:, 0:S_h * 512], att[:, 0:S_h * 512],
                                zc[:, 17:18], None, ALU.mult), inc='ve')
                            sch.wait('tensor', 've', normv)
                            pan_cp = {}
                            for kt in range(S_h * 4):
                                half = (kt // 4) % 2
                                if kt >= 8:
                                    sch.wait('tensor', 've', pan_cp[kt - 8])
                                tv = sch.op('tensor', lambda e, kt=kt, half=half: e.transpose(
                                    pstb[half][:, (kt % 4) * 128:(kt % 4) * 128 + 128],
                                    att[:, kt * 128:(kt + 1) * 128], idb_s[:]), inc='pe')
                                if kt % 4 == 3 or kt == S_h * 4 - 1:
                                    k0g = (kt // 4) * 4
                                    nk4 = kt - k0g + 1
                                    sch.wait('vector', 'pe', tv)
                                    if j >= 2:
                                        sch.wait('vector', 'pe', pv_half)
                                    pv4 = sch.op('vector', lambda e, k0g=k0g, nk4=nk4,
                                                 half=half, jj=jj, s0=s0: e.tensor_copy(
                                        panels[:, s0 * 4 + k0g:s0 * 4 + k0g + nk4,
                                               jj * 128:(jj + 1) * 128],
                                        pstb[half][:, 0:nk4 * 128]
                                        .rearrange("p (a b) -> p a b", a=nk4)), inc='ve')
                                    for z_ in range(k0g, kt + 1):
                                        pan_cp[z_] = pv4
                            pan_last = pan_cp[S_h * 4 - 1]
                            if j % 2 == 1:
                                sch.wait('tensor', 've', pan_last)
                                jh = j // 2
                                pairs = [(vsb[bkv][:, kt, :], panels[:, kt, :])
                                         for kt in range(s0 * 4, S_g * 4)]
                                pv_half = sch.mm_group(
                                    pairs, pso[:, jh * 256:(jh + 1) * 256], inc='pe')
                        prev_pan8 = pan_last
                        pvv = pv_half
                        kv_done = pvv
                        qh_done[bq] = pvv
                        bo_ = hq_idx % 2
                        sch.wait('vector', 'pe', pvv)
                        if st_o[bo_]:
                            sch.wait('vector', 'fsto%d' % bo_, st_o[bo_])
                        ov = sch.op('vector', lambda e, bo_=bo_: e.tensor_copy(
                            ob[bo_][:], pso[:]), inc='ve')
                        sch.wait('sync', 've', ov)
                        st_o[bo_] = sch.dma(
                            'sync', o_scr[h * 128:(h + 1) * 128, gi * 512:(gi + 1) * 512],
                            ob[bo_][:], inc='fsto%d' % bo_)
                        sch.wait('tensor', 've', ov)
            sch.wait('sync', 'fsto0', sch.val('fsto0'))
            sch.wait('sync', 'fsto1', sch.val('fsto1'))

        sch.barrier()

        with contextlib.ExitStack() as pctx:
            S = lambda name, shape, dt: pctx.enter_context(nc.sbuf_tensor(name, shape, dt))
            x2 = S("g_x2", [128, NHT, 512], BF16)
            u2s = S("g_u2", [16, 512], BF16)
            lbup_s = S("g_lbu", [16, HF], BF16)
            ladn_s = S("g_lad", [128, NHT * 16], BF16)
            lbdn_s = S("g_lbd", [16, DM], BF16)
            bup_s = S("g_bup", [128, NHT], F32)
            bdn_s = S("g_bdn", [128, NDT], F32)
            xnc = S("g_xnc", [128, NDT, 512], BF16)
            wu_sl = [S(f"g_wu{i}", [128, 896], BF16) for i in range(2)]
            wd_sl = [S(f"g_wd{i}", [128, 1024], BF16) for i in range(2)]
            gvb = [S(f"g_gv{i}", [128, 512], BF16) for i in range(2)]

            lc = sch.dma('gpsimd', lbup_s[:], lbup[:], inc='gld')
            lc = sch.dma('gpsimd', ladn_s[:], ladn[:], inc='gld')
            lc = sch.dma('gpsimd', lbdn_s[:], lbdn[:], inc='gld')
            lc = sch.dma('gpsimd', bup_s[:], bup[:], inc='gld')
            lc = sch.dma('gpsimd', bdn_s[:], bdn[:], inc='gld')
            sch.wait('tensor', 'gld', lc)
            sch.wait('scalar', 'gld', lc)
            sch.cwait('sync', 'ccWU')
            sch.cwait('sync', 'ccWD')
            sch.cwait('tensor', 'ccWU')
            sch.cwait('tensor', 'ccWD')

            nxl = 0
            wu_pe = {0: 0, 1: 0}
            wd_pe = {0: 0, 1: 0}
            x2_ac = 0
            dn_ac = 0
            stg1 = {0: 0, 1: 0}
            XNC_PREV = 0
            nslw = {0: 0, 1: 0}
            nsld = {0: 0, 1: 0}
            for tc in range(2):
                with contextlib.ExitStack() as tctx:
                    psH = [tctx.enter_context(nc.psum_tensor(f"g_psH{tc}{i}", [128, 512], F32))
                           for i in range(7)]
                    psu2 = tctx.enter_context(nc.psum_tensor(f"g_psu2{tc}", [16, 512], F32))
                    if tc > 0:
                        sch.wait('gpsimd', 'pe', XNC_PREV)
                    nxl += 1
                    lx = sch.dma('gpsimd', xnc[:],
                                 xn1_scr[:, tc * 512:(tc + 1) * 512]
                                 .rearrange("(d p) t -> p d t", p=128), inc='gxl')
                    sch.wait('tensor', 'gxl', 16 * nxl)
                    for r in range(NCORES):
                        ntr = HTILES[r]
                        for (g0, g1) in ((0, 7), (7, ntr)):
                            gs = g1 - g0
                            if x2_ac:
                                sch.wait('tensor', 'ac', x2_ac)
                            if dn_ac:
                                sch.wait('tensor', 'ac', dn_ac)
                            for j in range(gs):
                                ht = HTOFF[r] + g0 + j
                                sch.mm(psH[j][:], lbup_s[0:16, ht * 128:(ht + 1) * 128],
                                       u1s[0:16, tc * 512:(tc + 1) * 512],
                                       start=True, stop=False)
                            for dt in range(NDT):
                                b = dt % 2
                                if wu_pe[b]:
                                    sch.wait('sync', 'pe', wu_pe[b])
                                nslw[b] += 1
                                sch.dma('sync', wu_sl[b][:, 0:gs * 128],
                                        WUPG[r * DM + dt * 128:r * DM + (dt + 1) * 128,
                                             g0 * 128:g1 * 128], inc='gwu%d' % b)
                                sch.wait('tensor', 'gwu%d' % b, 16 * nslw[b])
                                for j in range(gs):
                                    pv = sch.mm(psH[j][:], wu_sl[b][:, j * 128:(j + 1) * 128],
                                                xnc[:, dt, :], start=False, stop=(dt == NDT - 1),
                                                inc='pe', always=(j == gs - 1))
                                wu_pe[b] = pv
                            PV_UP = sch.val('pe')
                            sch.wait('scalar', 'pe', PV_UP)
                            for j in range(gs):
                                ht = HTOFF[r] + g0 + j
                                a = sch.op('scalar', lambda e, pb=psH[j], ht=ht: e.activation(
                                    x2[:, ht, :], pb[:], AF.Relu,
                                    bias=bup_s[:, ht:ht + 1]), inc='ac')
                                x2_ac = a
                                sch.wait('tensor', 'ac', a)
                                sch.mm(psu2[:], ladn_s[:, ht * 16:(ht + 1) * 16], x2[:, ht, :],
                                       start=(ht == 0), stop=(ht == NHT - 1),
                                       inc=('pe' if ht == NHT - 1 else None))
                    U2PE = sch.val('pe')
                    sch.wait('vector', 'pe', U2PE)
                    u2v = sch.op('vector', lambda e, pb=psu2: e.tensor_copy(u2s[:], pb[:]), inc='ve')
                    if debug:
                        sch.wait('sync', 'ac', x2_ac)
                        dv = sch.dma('sync',
                                     x2d[:, tc * 512:(tc + 1) * 512]
                                     .rearrange("(ht p) t -> p ht t", p=128),
                                     x2[:], inc='dbgf')
                        sch.wait('sync', 've', u2v)
                        dv = sch.dma('sync', u2d[0:16, tc * 512:(tc + 1) * 512],
                                     u2s[:], inc='dbgf')
                        dv = sch.dma('sync', u1d, u1s[:], inc='dbgf')
                        sch.wait('sync', 'dbgf', dv)
                with contextlib.ExitStack() as tctx:
                    psD = [tctx.enter_context(nc.psum_tensor(f"g_psD{tc}{i}", [128, 512], F32))
                           for i in range(8)]
                    sch.wait('tensor', 've', u2v)
                    for dg in range(4):
                        if x2_ac:
                            sch.wait('tensor', 'ac', x2_ac)
                        if dn_ac:
                            sch.wait('tensor', 'ac', dn_ac)
                        for j in range(8):
                            dt = dg * 8 + j
                            sch.mm(psD[j][:], lbdn_s[0:16, dt * 128:(dt + 1) * 128],
                                   u2s[:], start=True, stop=False)
                        for ht in range(NHT):
                            b = ht % 2
                            r, lt = rank_lt(ht)
                            if wd_pe[b]:
                                sch.wait('sync', 'pe', wd_pe[b])
                            nsld[b] += 1
                            sch.dma('sync', wd_sl[b][:],
                                    WDNG[r * HSH_PAD + lt * 128:r * HSH_PAD + (lt + 1) * 128,
                                         dg * 1024:(dg + 1) * 1024], inc='gwd%d' % b)
                            sch.wait('tensor', 'gwd%d' % b, 16 * nsld[b])
                            for j in range(8):
                                pv = sch.mm(psD[j][:], wd_sl[b][:, j * 128:(j + 1) * 128],
                                            x2[:, ht, :], start=False, stop=(ht == NHT - 1),
                                            inc='pe', always=(j == 7))
                            wd_pe[b] = pv
                        PV_DN = sch.val('pe')
                        sch.wait('scalar', 'pe', PV_DN)
                        for j in range(8):
                            dt = dg * 8 + j
                            b2 = j % 2
                            if stg1[b2]:
                                sch.wait('scalar', 'gst%d' % b2, stg1[b2])
                            a = sch.op('scalar', lambda e, pb=psD[j], dt=dt, b2=b2: e.activation(
                                gvb[b2][:], pb[:], AF.Silu,
                                bias=bdn_s[:, dt:dt + 1]), inc='ac')
                            dn_ac = a
                            sch.wait('sync', 'ac', a)
                            stg1[b2] = sch.dma(
                                'sync', g1_scr[dt * 128:(dt + 1) * 128,
                                               tc * 512:(tc + 1) * 512],
                                gvb[b2][:], inc='gst%d' % b2)
                    XNC_PREV = sch.val('pe')
            sch.wait('sync', 'gst0', sch.val('gst0'))
            sch.wait('sync', 'gst1', sch.val('gst1'))

        sch.barrier()

        with contextlib.ExitStack() as pctx:
            S = lambda name, shape, dt: pctx.enter_context(nc.sbuf_tensor(name, shape, dt))
            o_sb = S("h_o", [128, NDT, 512], BF16)
            wos = [S(f"h_w{i}", [128, DM], BF16) for i in range(2)]
            x1t = [S(f"h_x{i}", [128, 512], F32) for i in range(2)]
            g1t = [S(f"h_g{i}", [128, 512], BF16) for i in range(2)]
            tmpc = [S(f"h_t{i}", [128, 512], F32) for i in range(2)]
            outb = [S(f"h_ob{i}", [128, 512], F32) for i in range(2)]
            psf = [pctx.enter_context(nc.psum_tensor(f"h_ps{i}", [128, 512], F32))
                   for i in range(2)]

            nwo = {0: 0, 1: 0}
            nxg = {0: 0, 1: 0}
            sto = {0: 0, 1: 0}
            cmb_v = {0: 0, 1: 0}
            pe_wo = {0: 0, 1: 0}
            nol = 0
            for tc in range(2):
                if tc > 0:
                    sch.wait('gpsimd', 'pe', sch.val('pe'))
                nol += 1
                lo = sch.dma('gpsimd', o_sb[:],
                             o_scr[:, tc * 512:(tc + 1) * 512]
                             .rearrange("(nd p) t -> p nd t", p=128), inc='hlo')
                sch.wait('tensor', 'hlo', 16 * nol)
                for dt_i in range(NDT):
                    u = tc * NDT + dt_i
                    b = u % 2
                    if u >= 2:
                        sch.wait('gpsimd', 'pe', pe_wo[b])
                        sch.wait('gpsimd', 've', cmb_v[b])
                    nwo[b] += 1
                    sch.dma('gpsimd', wos[b][:], wo1[dt_i], inc='hlw%d' % b)
                    nxg[b] += 1
                    lgf = sch.dma('gpsimd', x1t[b][:],
                                  x1_scr[dt_i * 128:(dt_i + 1) * 128, tc * 512:(tc + 1) * 512],
                                  inc='hlg%d' % b)
                    lgf = sch.dma('gpsimd', g1t[b][:],
                                  g1_scr[dt_i * 128:(dt_i + 1) * 128, tc * 512:(tc + 1) * 512],
                                  inc='hlg%d' % b)
                    sch.wait('tensor', 'hlw%d' % b, 16 * nwo[b])
                    pairs = [(wos[b][:, j * 128:(j + 1) * 128], o_sb[:, j, :])
                             for j in range(NDT)]
                    sch.wait('tensor', 've', cmb_v[b])
                    pv = sch.mm_group(pairs, psf[b][:], inc='pe')
                    pe_wo[b] = pv
                    sch.wait('vector', 'pe', pv)
                    sch.wait('vector', 'hlg%d' % b, lgf)
                    if sto[b]:
                        sch.wait('vector', 'hst%d' % b, sto[b])
                    v = sch.op('vector', lambda e, b=b: e.tensor_mul(
                        tmpc[b][:], psf[b][:], g1t[b][:]), inc='ve')
                    sch.wait('vector', 've', v)
                    v = sch.op('vector', lambda e, b=b: e.tensor_add(
                        outb[b][:], tmpc[b][:], x1t[b][:]), inc='ve')
                    cmb_v[b] = v
                    sch.wait('sync', 've', v)
                    sto[b] = sch.dma(
                        'sync', outF[dt_i * 128:(dt_i + 1) * 128, tc * 512:(tc + 1) * 512],
                        outb[b][:], inc='hst%d' % b)
            sch.wait('sync', 'hst0', sch.val('hst0'))
            sch.wait('sync', 'hst1', sch.val('hst1'))
        if debug:
            dv = sch.dma('sync', k1d, k1c, inc='dbgc')
            dv = sch.dma('sync', v1d, v1c, inc='dbgc')
            dv = sch.dma('sync', wud, wupc, inc='dbgc')
            dv = sch.dma('sync', wdd, wdnc, inc='dbgc')
            dv = sch.dma('sync', wugd, WUPG, inc='dbgc')
            dv = sch.dma('sync', wdgd, WDNG, inc='dbgc')
            sch.wait('sync', 'dbgc', dv)
        sch.run(block)
    return nc



_PROGS = {}


def _prog(name, builder):
    if name not in _PROGS:
        _PROGS[name] = builder()
    return _PROGS[name]


def _shard_rows(c):
    return np.r_[c * 512:(c + 1) * 512, (15 - c) * 512:(16 - c) * 512]


def prepare_in_maps(inputs):
    g = lambda k: np.asarray(inputs[k])
    x_full = np.ascontiguousarray(g('L0_attn_x').reshape(N, DM).astype(np.float32))
    shards = [_shard_rows(c) for c in range(NCORES)]
    idb = np.eye(128, dtype=BF)
    idf = np.eye(128, dtype=np.float32)

    Wqk_T = g('L0_attn_Wqk_w').T
    hh = np.arange(8)[:, None]
    ee = np.arange(64)[None, :]
    qperm = (hh * 128 + 2 * ee).reshape(-1)
    kperm = (hh * 128 + 2 * ee + 1).reshape(-1)
    big_T = np.concatenate([Wqk_T[:, qperm], Wqk_T[:, kperm], g('L0_attn_Wv_w').T], axis=1)
    b_qk = g('L0_attn_Wqk_b')
    sel8 = np.zeros((8, 128), dtype=np.float32)
    for ei in range(8):
        sel8[ei, ei * 16:(ei + 1) * 16] = 1.0

    wq1 = np.concatenate([g('L1_attn_Wq').T, g('L1_attn_Wk').T, g('L1_attn_Wv').T], axis=1)
    cod_u_pad = np.zeros((DM, 8 * HSH_PAD), dtype=np.int32)
    cod_u_pad[:, :HF] = g('L1_ffn_w_up_codes').reshape(HF, DM).T
    amx_u_pad = np.zeros((64, 8 * HSH_PAD), dtype=np.float32)
    amx_u_pad[:, :HF] = g('L1_ffn_w_up_absmax').reshape(HF, 64).T
    cod_d_pad = np.zeros((8 * HSH_PAD, DM), dtype=np.int32)
    cod_d_pad[:HF] = g('L1_ffn_w_down_codes').reshape(DM, HF).T
    amx_d_pad = np.zeros((176, DM), dtype=np.float32)
    amx_d_pad[:172] = g('L1_ffn_w_down_absmax').reshape(DM, 172).T

    common = dict(
        wqkv0=tile_w(np.ascontiguousarray(big_T)),
        bqkv=cols128(np.concatenate([b_qk[qperm], b_qk[kperm], g('L0_attn_Wv_b')])),
        wrout=np.ascontiguousarray(
            g('L0_ffn_W_router').T.reshape(32, 128, 8).transpose(1, 0, 2).reshape(128, 256)),
        wa=tile_w(np.ascontiguousarray(np.transpose(g('L0_ffn_A'), (2, 0, 1)).reshape(4096, 128))),
        wbase=tile_w(g('L0_ffn_W_base').T),
        bbase=cols128(g('L0_ffn_b_base')),
        bstk=np.ascontiguousarray(
            g('L0_ffn_B').transpose(0, 2, 1).reshape(128, 4096)
            .reshape(128, 32, 128).transpose(1, 0, 2)),
        sel8=sel8, idf=idf, idb=idb,
        ones1=np.ones((1, 128), dtype=np.float32),
        ones128=np.ones((128, 1), dtype=np.float32),
        ones128b=np.ones((128, 1), dtype=BF),
        wo0=tile_w(np.ascontiguousarray(g('L0_attn_Wo_w').T)),
        bo0=cols128(g('L0_attn_Wo_b')),
        wqkv1=tile_w(np.ascontiguousarray(wq1)),
        wnq=np.ascontiguousarray(g('L1_attn_q_norm_w')[:, None].astype(np.float32)),
        wnk=np.ascontiguousarray(g('L1_attn_k_norm_w')[:, None].astype(np.float32)),
        la_up=np.ascontiguousarray(
            g('L1_ffn_w_up_lora_a').reshape(32, 128, 16).transpose(1, 0, 2).reshape(128, 512)),
        rel5=_rel_consts(),
        wo1=tile_w(g('L1_attn_Wo').T),
        bup=np.ascontiguousarray(g('L1_ffn_b_up').reshape(NHT, 128).T.astype(np.float32)),
        bdn=cols128(g('L1_ffn_b_down')),
        lbup=np.ascontiguousarray(g('L1_ffn_w_up_lora_b').astype(np.float32)),
        ladn=np.ascontiguousarray(
            g('L1_ffn_w_down_lora_a').reshape(NHT, 128, 16).transpose(1, 0, 2)
            .reshape(128, NHT * 16).astype(np.float32)),
        lbdn=np.ascontiguousarray(g('L1_ffn_w_down_lora_b').astype(np.float32)),
    )
    in_maps = []
    for c in range(NCORES):
        xs = np.ascontiguousarray(x_full[shards[c]])
        in_maps.append(dict(
            x_tm=xs,
            xT=np.ascontiguousarray(xs.T),
            cod_u=np.ascontiguousarray(cod_u_pad[:, c * HSH_PAD:(c + 1) * HSH_PAD]),
            amx_u=np.ascontiguousarray(amx_u_pad[:, c * HSH_PAD:(c + 1) * HSH_PAD]),
            cod_d=np.ascontiguousarray(cod_d_pad[c * HSH_PAD:(c + 1) * HSH_PAD]),
            amx_d=np.ascontiguousarray(amx_d_pad[c * 22:(c + 1) * 22]),
            biast=l1b_bias_table(c),
            kidx=l1b_kidx(c),
            **common))
    return in_maps


def kernel(**inputs):
    in_maps = prepare_in_maps(inputs)
    shards = [_shard_rows(c) for c in range(NCORES)]
    nc = _prog('fused', build_fused)
    res = run_bass_kernel_spmd(nc, in_maps, core_ids=list(range(NCORES)))
    final = np.zeros((N, DM), dtype=np.float32)
    for c in range(NCORES):
        final[shards[c]] = res.results[c]["outF"].T
    return final.reshape(1, N, DM)
